# version 06
# baseline (speedup 1.0000x reference)
import numpy as np
import ml_dtypes
import concourse.bass as bass
import concourse.mybir as mybir
from concourse.bass_utils import run_bass_kernel_spmd
from contextlib import ExitStack

F32 = mybir.dt.float32
BF16 = mybir.dt.bfloat16
AX = mybir.AxisListType.X
OP = mybir.AluOpType
AF = mybir.ActivationFunctionType

D = 1024
T = 2048
KC = 8
TT = 16
NP = 8
E = 8
F = 4096
FC = 32
NTG = 2
IC = 32
NTG4 = 4
NCORE = 8
SCALE = 0.125
_DBG = {}
NEG_BIG = -1.0e9



class Chan:
    __slots__ = ("name", "n", "h")

    def __init__(self, name):
        self.name, self.n, self.h = name, 0, None


class Buf:
    __slots__ = ("name", "w", "r", "birth")

    def __init__(self, name, parts=1, birth=None):
        self.name = name
        self.w = [None] * parts
        self.r = [dict() for _ in range(parts)]
        self.birth = birth or []


class Arena:

    def __init__(self):
        self.live = []

    def claim(self, lo, hi, name, parts=1):
        toks = []
        for a, b, ob in self.live:
            if a < hi and lo < b:
                for p in range(len(ob.w)):
                    if ob.w[p] is not None:
                        toks.append(ob.w[p])
                    toks.extend(ob.r[p].values())
        nb = Buf(name, parts, birth=toks)
        self.live.append((lo, hi, nb))
        return nb


class Em:
    def __init__(self):
        self.ch = {e: Chan(e) for e in "tvsqy"}
        self.dch = {}
        self.ops = {e: [] for e in "tvsqy"}
        self.waited = {e: {} for e in "tvsqy"}
        self.pend = {}
        self.n_inst = 0

    def dma_chan(self, name):
        if name not in self.dch:
            self.dch[name] = Chan(name)
        return self.dch[name]

    def _wait(self, e, tok):
        ch, val = tok
        if self.waited[e].get(ch.name, 0) >= val:
            return
        assert val <= ch.n, f"wait on future token {ch.name}:{val} > {ch.n}"
        self.waited[e][ch.name] = val
        self.ops[e].append((0, ch, val))

    def _deps(self, rd, wr):
        deps = []
        for b, p in rd:
            t = b.w[p]
            if t is not None:
                deps.append(t)
        for b, p in wr:
            deps.extend(b.birth)
            if b.w[p] is not None:
                deps.append(b.w[p])
            deps.extend(b.r[p].values())
        return deps

    def _upd(self, rd, wr, tok):
        for b, p in rd:
            b.r[p][tok[0].name] = tok
        for b, p in wr:
            b.w[p] = tok
            b.r[p] = {}

    def emit(self, e, fn, rd=(), wr=(), dma=None, inc=True):
        for t in self._deps(rd, wr):
            self._wait(e, t)
        if not inc:
            self.ops[e].append((1, fn, None, 0))
            self.n_inst += 1
            self.pend.setdefault(e, []).append((list(rd), list(wr)))
            return None
        if dma is not None:
            ch = self.dma_chan(dma) if isinstance(dma, str) else dma
            amt = 16
        else:
            ch = self.ch[e]
            amt = 1
        ch.n += amt
        self.ops[e].append((1, fn, ch, amt))
        self.n_inst += 1
        tok = (ch, ch.n)
        for prd, pwr in self.pend.pop(e, []):
            self._upd(prd, pwr, tok)
        self._upd(rd, wr, tok)
        return tok

    def bulk_fixup(self, bufs, tok):
        for b in bufs:
            for p in range(len(b.w)):
                if b.w[p] is not None:
                    b.w[p] = tok

    def emit_group(self, e, members, wr=()):
        all_rd = []
        for fn, rd in members:
            for t in self._deps(rd, wr):
                self._wait(e, t)
            all_rd.extend(rd)
        ch = self.ch[e]
        for i, (fn, rd) in enumerate(members):
            last = i == len(members) - 1
            if last:
                ch.n += 1
            self.ops[e].append((1, fn, ch if last else None, 1))
            self.n_inst += 1
        tok = (ch, ch.n)
        self._upd(all_rd, wr, tok)
        return tok

    def final_wait(self, e):
        for ch in list(self.ch.values()) + list(self.dch.values()):
            if ch.n > 0:
                self.ops[e].append((0, ch, ch.n))


def _replay(em, e, eng):
    for entry in em.ops[e]:
        if entry[0] == 0:
            _, ch, val = entry
            eng.wait_ge(ch.h, val)
        else:
            _, fn, ch, amt = entry
            ins = fn(eng)
            if ch is not None:
                ins.then_inc(ch.h, amt)



def _pair_blocks(w):
    return np.ascontiguousarray(w.reshape(8, 128, 8, 128).transpose(2, 0, 1, 3))


def _cols(b, n):
    return np.ascontiguousarray(b.reshape(n, 128).T)


def _host_prep(inputs):
    f32 = lambda a: np.ascontiguousarray(np.asarray(a, dtype=np.float32))
    d = {}
    d["l0q"] = _pair_blocks(f32(inputs["L0_attn_Wq"]))
    d["l0k"] = _pair_blocks(f32(inputs["L0_attn_Wk"]))
    d["l0v"] = f32(inputs["L0_attn_Wv"])
    d["l0o"] = f32(inputs["L0_attn_Wo"])
    d["l0bo"] = _cols(f32(inputs["L0_attn_bo"]), 8)
    d["wg"] = np.ascontiguousarray(
        f32(inputs["L0_ffn_Wg"]).reshape(8, 128, 8).transpose(1, 0, 2).reshape(128, 64))
    d["bg"] = f32(inputs["L0_ffn_bg"]).reshape(1, 8)
    d["w1r"] = np.ascontiguousarray(
        f32(inputs["L0_ffn_W1"]).reshape(8, 8, 128, 32, 128).transpose(0, 3, 1, 2, 4))
    d["w3r"] = np.ascontiguousarray(
        f32(inputs["L0_ffn_W3"]).reshape(8, 8, 128, 32, 128).transpose(0, 3, 1, 2, 4))
    d["w2r"] = np.ascontiguousarray(
        f32(inputs["L0_ffn_W2"]).reshape(8, 32, 128, 8, 128).transpose(0, 3, 1, 2, 4))
    d["l1q"] = _pair_blocks(f32(inputs["L1_attn_Wq"]).T)
    d["l1k"] = _pair_blocks(f32(inputs["L1_attn_Wk"]).T)
    d["l1v"] = np.ascontiguousarray(f32(inputs["L1_attn_Wv"]).T)
    d["l1o"] = np.ascontiguousarray(f32(inputs["L1_attn_Wo"]).T)
    d["l1bq"] = _cols(f32(inputs["L1_attn_bq"]), 8)
    d["l1bk"] = _cols(f32(inputs["L1_attn_bk"]), 8)
    d["l1bv"] = f32(inputs["L1_attn_bv"]).reshape(1, 1024)
    d["l1bo"] = _cols(f32(inputs["L1_attn_bo"]), 8)
    d["w1n"] = f32(inputs["L1_ffn_w1"])
    d["w1tr"] = np.ascontiguousarray(
        f32(inputs["L1_ffn_w1"]).T.reshape(8, 128, 32, 128).transpose(2, 0, 1, 3))
    d["w2n"] = f32(inputs["L1_ffn_w2"])
    d["w2tr"] = np.ascontiguousarray(
        f32(inputs["L1_ffn_w2"]).T.reshape(32, 128, 8, 128).transpose(2, 0, 1, 3))
    d["b1c"] = _cols(f32(inputs["L1_ffn_b1"]), 32)
    d["b2c"] = _cols(f32(inputs["L1_ffn_b2"]), 8)
    return d


_DRAM_SPECS = [
    ("l0q", [8, 8, 128, 128]), ("l0k", [8, 8, 128, 128]),
    ("l0v", [1024, 1024]), ("l0o", [1024, 1024]), ("l0bo", [128, 8]),
    ("wg", [128, 64]), ("bg", [1, 8]),
    ("w1r", [8, 32, 8, 128, 128]), ("w3r", [8, 32, 8, 128, 128]),
    ("w2r", [8, 8, 32, 128, 128]),
    ("l1q", [8, 8, 128, 128]), ("l1k", [8, 8, 128, 128]),
    ("l1v", [1024, 1024]), ("l1o", [1024, 1024]),
    ("l1bq", [128, 8]), ("l1bk", [128, 8]), ("l1bv", [1, 1024]), ("l1bo", [128, 8]),
    ("w1n", [4096, 1024]), ("w1tr", [32, 8, 128, 128]),
    ("w2n", [1024, 4096]), ("w2tr", [8, 32, 128, 128]),
    ("b1c", [128, 32]), ("b2c", [128, 8]),
]

A_V = 0
A_QT = 16384
A_KT = 18432
A_OT = 20480
A_P = 22528
A_P2 = 24576
A_P3 = 33792
A_PT = 26624
A_QKW = 27648
A_WOW = 31744
A_W13 = 32768
A_W2R = 36864
A_CWT = 40960

Z_R1 = 0
Z_R2 = 2048
Z_TMP = 4096
Z_CWB = 0
Z_BVR = 1024
Z_ALA = 2048
Z_ALB = 2048


def _build(upto="full"):
    nc = bass.Bass("TRN2", target_bir_lowering=False)
    em = Em()
    ar_16 = Arena()
    ar_32 = Arena()

    xT = nc.dram_tensor("xT", [D, T], F32, kind="ExternalInput")
    outT = nc.dram_tensor("outT", [D, T], F32, kind="ExternalOutput")
    dbgT = nc.dram_tensor("dbgT", [128, 64], F32, kind="ExternalOutput") if _DBG else None
    dr = {n: nc.dram_tensor(n, s, F32, kind="ExternalInput") for n, s in _DRAM_SPECS}

    ident_bf = nc.inline_tensor(np.eye(128, dtype=ml_dtypes.bfloat16), name="ident_bf")
    maskc = np.zeros((128, 128), np.float32)
    maskc[np.triu_indices(128, 1)] = NEG_BIG
    mask_f = nc.inline_tensor(maskc, name="mask_f")
    onerow_f = nc.inline_tensor(np.ones((1, 128), np.float32), name="onerow_f")
    onecol_f = nc.inline_tensor(np.ones((128, 1), np.float32), name="onecol_f")
    ohm = np.zeros((8, 8 * 128), ml_dtypes.bfloat16)
    for e_ in range(8):
        ohm[e_, e_ * 128:(e_ + 1) * 128] = 1
    oh_bf = nc.inline_tensor(ohm, name="oh_bf")

    res_t = [nc.alloc_sbuf_tensor(f"res{c}", [128, T], F32) for c in range(KC)]
    xn_t = [nc.alloc_sbuf_tensor(f"xn{c}", [128, T], BF16) for c in range(KC)]
    AR16 = nc.alloc_sbuf_tensor("AR16", [128, 43008], BF16)
    AR32 = nc.alloc_sbuf_tensor("AR32", [128, 5120], F32)
    cwf_t = nc.alloc_sbuf_tensor("cwf", [128, 128], F32)
    cwbf_t = nc.alloc_sbuf_tensor("cwbf", [128, 128], BF16)
    idb_t = nc.alloc_sbuf_tensor("idb", [128, 128], BF16)
    mask_t = nc.alloc_sbuf_tensor("maskt", [128, 128], F32)
    onerow_t = nc.alloc_sbuf_tensor("onerow", [1, 128], F32)
    onecol_t = nc.alloc_sbuf_tensor("onecol", [128, 1], F32)
    oh_t = nc.alloc_sbuf_tensor("oht", [8, 8 * 128], BF16)
    wg_t = nc.alloc_sbuf_tensor("wgt", [128, 64], BF16)
    bgb_t = nc.alloc_sbuf_tensor("bgb", [128, 8], F32)
    bgrow_t = nc.alloc_sbuf_tensor("bgrow", [1, 8], F32)
    sc_t = nc.alloc_sbuf_tensor("sc", [128, 48], F32)
    bias_t = {n: nc.alloc_sbuf_tensor("b_" + n, [128, 8], F32)
              for n in ("l0bo", "l1bq", "l1bk", "l1bo", "b2c")}
    b1c_t = nc.alloc_sbuf_tensor("b_b1c", [128, 32], F32)
    al1_t = nc.alloc_sbuf_tensor("al1", [128, 32], F32)
    al2_t = nc.alloc_sbuf_tensor("al2", [128, 8], F32)

    PS_A = nc.alloc_psum_tensor("PS_A", [128, 2048], F32)
    PS_B = nc.alloc_psum_tensor("PS_B", [128, 512], F32)
    PS_C = nc.alloc_psum_tensor("PS_C", [128, 512], F32)
    PT_T = [nc.alloc_psum_tensor("PT_A", [128, 512], BF16),
            nc.alloc_psum_tensor("PT_B", [128, 512], BF16)]

    res = [Buf(f"res{c}", 4) for c in range(KC)]
    xn = [Buf(f"xn{c}", 4) for c in range(KC)]
    psA = Buf("psA", 4)
    psB = Buf("psB", 1)
    psC = Buf("psC", 1)
    ptb = [Buf("ptA", 4), Buf("ptB", 4)]
    consts = Buf("consts", 6)
    C_ID, C_MASK, C_ROW, C_COL, C_OH, C_WG = range(6)
    biasb = Buf("biasb", 6)
    B_IDX = {"l0bo": 0, "l1bq": 1, "l1bk": 2, "l1bo": 3, "b2c": 4, "b1c": 5}
    scb = Buf("scb", 16)
    tmpf = Buf("tmpf", 2)
    cwfb = Buf("cwf", 16)
    cwbfb = Buf("cwbf", 1)
    al1b = Buf("al1", 1)
    al2b = Buf("al2", 1)
    bgrb = Buf("bgrow", 1)
    bgbb = Buf("bgb", 1)
    wgb = Buf("wgb", 1)

    S_GL, S_E8, S_MS, S_CW = 0, 8, 16, 24
    S_MC = 32
    S_MX, S_NM, S_L, S_RL = 36, 37, 38, 39
    S_G1, S_G2, S_G3, S_G4 = 40, 41, 42, 43
    S_A2 = 44

    def a16(off, ln, p0=0, p1=128):
        return AR16[p0:p1, off:off + ln]

    def a32(off, ln, p0=0, p1=128):
        return AR32[p0:p1, off:off + ln]

    E_ = em.emit
    G_ = em.emit_group
    PSBC = [(PS_B, psB), (PS_C, psC)]

    E_("y", lambda y: y.dma_start(idb_t[:, :], ident_bf[:, :]), wr=[(consts, C_ID)], dma="cst")
    E_("y", lambda y: y.dma_start(mask_t[:, :], mask_f[:, :]), wr=[(consts, C_MASK)], dma="cst")
    E_("y", lambda y: y.dma_start(onerow_t[:, :], onerow_f[:, :]), wr=[(consts, C_ROW)], dma="cst")
    E_("y", lambda y: y.dma_start(onecol_t[:, :], onecol_f[:, :]), wr=[(consts, C_COL)], dma="cst")
    E_("y", lambda y: y.dma_start(oh_t[:, :], oh_bf[:, :]), wr=[(consts, C_OH)], dma="cst")
    E_("q", lambda q: q.dma_start(wg_t[:, :], dr["wg"][:, :]), wr=[(wgb, 0)], dma="cwg")
    for n in ("l0bo", "l1bq", "l1bk", "l1bo", "b2c"):
        E_("y", lambda y, n=n: y.dma_start(bias_t[n][:, :], dr[n][:, :]),
           wr=[(biasb, B_IDX[n])], dma="cst")
    E_("y", lambda y: y.dma_start(b1c_t[:, :], dr["b1c"][:, :]), wr=[(biasb, B_IDX["b1c"])], dma="cst")
    E_("y", lambda y: y.dma_start(bgrow_t[:, :], dr["bg"][:, :]), wr=[(bgrb, 0)], dma="cst")
    em.bulk_fixup([consts, biasb, bgrb], (em.dma_chan("cst"), em.dma_chan("cst").n))
    for c in range(KC):
        E_("y", lambda y, c=c: y.dma_start(res_t[c][:, :], xT[c * 128:(c + 1) * 128, :]),
           wr=[(res[c], p) for p in range(4)], dma="xin")
    em.bulk_fixup(res, (em.dma_chan("xin"), em.dma_chan("xin").n))
    E_("t", lambda t: t.matmul(PS_B[:, 0:8], onerow_t[:, :], bgrow_t[:, :],
                               start=True, stop=True),
       rd=[(consts, C_ROW), (bgrb, 0)], wr=[(psB, 0)])
    E_("v", lambda v: v.tensor_copy(bgb_t[:, :], PS_B[:, 0:8]),
       rd=[(psB, 0)], wr=[(bgbb, 0)])

    def phase_ln(L):
        r1 = ar_32.claim(Z_R1, Z_R1 + 2048, f"r1L{L}")
        r2 = ar_32.claim(Z_R2, Z_R2 + 2048, f"r2L{L}")
        for g in range(NTG4):
            for c in range(KC):
                E_("t", lambda t, c=c, g=g: t.matmul(
                    PS_A[0:1, g * 512:(g + 1) * 512], onecol_t[:, :],
                    res_t[c][:, g * 512:(g + 1) * 512],
                    start=(c == 0), stop=(c == KC - 1)),
                   rd=[(consts, C_COL), (res[c], g)],
                   wr=[(psA, g)] if c in (0, KC - 1) else [])
        for g in range(NTG4):
            E_("v", lambda v, g=g: v.tensor_scalar(
                a32(Z_R1 + g * 512, 512, 0, 1), PS_A[0:1, g * 512:(g + 1) * 512],
                1.0 / D, None, OP.mult), rd=[(psA, g)], wr=[(r1, 0)])
        for g in range(NTG4):
            for c in range(KC):
                sl = c % 2
                E_("s", lambda s, c=c, g=g, sl=sl: s.square(
                    a32(Z_TMP + sl * 512, 512), res_t[c][:, g * 512:(g + 1) * 512]),
                    rd=[(res[c], g)], wr=[(tmpf, sl)])
                E_("t", lambda t, c=c, g=g, sl=sl: t.matmul(
                    PS_A[32:33, g * 512:(g + 1) * 512], onecol_t[:, :],
                    a32(Z_TMP + sl * 512, 512),
                    start=(c == 0), stop=(c == KC - 1)),
                   rd=[(consts, C_COL), (tmpf, sl)],
                   wr=[(psA, g)] if c in (0, KC - 1) else [])
        for g in range(NTG4):
            E_("v", lambda v, g=g: v.tensor_scalar(
                a32(Z_R2 + g * 512, 512, 0, 1), PS_A[32:33, g * 512:(g + 1) * 512],
                1.0 / D, 1e-5, OP.mult, OP.add), rd=[(psA, g)], wr=[(r2, 0)])
        for hh in range(2):
            E_("v", lambda v, hh=hh: v.tensor_mul(
                a32(Z_TMP, 1024, 0, 1), a32(Z_R1 + hh * 1024, 1024, 0, 1),
                a32(Z_R1 + hh * 1024, 1024, 0, 1)),
               rd=[(r1, 0)], wr=[(tmpf, 0), (tmpf, 1)])
            E_("v", lambda v, hh=hh: v.tensor_sub(
                a32(Z_R2 + hh * 1024, 1024, 0, 1), a32(Z_R2 + hh * 1024, 1024, 0, 1),
                a32(Z_TMP, 1024, 0, 1)),
               rd=[(r2, 0), (tmpf, 0), (tmpf, 1)], wr=[(r2, 0)])
        E_("s", lambda s: s.activation(a32(Z_R2, T, 0, 1), a32(Z_R2, T, 0, 1),
                                       AF.Sqrt), rd=[(r2, 0)], wr=[(r2, 0)])
        E_("v", lambda v: v.reciprocal(a32(Z_R2, T, 0, 1), a32(Z_R2, T, 0, 1)),
           rd=[(r2, 0)], wr=[(r2, 0)])
        E_("v", lambda v: v.scalar_tensor_tensor(
            a32(Z_R1, T, 0, 1), a32(Z_R1, T, 0, 1), -1.0, a32(Z_R2, T, 0, 1),
            OP.mult, OP.mult), rd=[(r1, 0), (r2, 0)], wr=[(r1, 0)])
        for g in range(NTG4):
            nbt, nb = PSBC[g % 2]
            E_("t", lambda t, g=g: t.matmul(
                PS_A[:, g * 512:(g + 1) * 512], onerow_t[:, :],
                a32(Z_R2 + g * 512, 512, 0, 1), start=True, stop=True),
               rd=[(consts, C_ROW), (r2, 0)], wr=[(psA, g)])
            E_("t", lambda t, g=g, nbt=nbt: t.matmul(
                nbt[:, :], onerow_t[:, :], a32(Z_R1 + g * 512, 512, 0, 1),
                start=True, stop=True),
               rd=[(consts, C_ROW), (r1, 0)], wr=[(nb, 0)])
            for c in range(KC):
                sl = c % 2
                E_("v", lambda v, c=c, g=g, sl=sl: v.tensor_mul(
                    a32(Z_TMP + sl * 512, 512), res_t[c][:, g * 512:(g + 1) * 512],
                    PS_A[:, g * 512:(g + 1) * 512]),
                    rd=[(res[c], g), (psA, g)], wr=[(tmpf, sl)])
                E_("v", lambda v, c=c, g=g, sl=sl, nbt=nbt: v.tensor_add(
                    xn_t[c][:, g * 512:(g + 1) * 512], a32(Z_TMP + sl * 512, 512),
                    nbt[:, :]), rd=[(tmpf, sl), (nb, 0)], wr=[(xn[c], g)])

    def phase_gate():
        cwtb = ar_16.claim(A_CWT, A_CWT + 2048, "cwT", 16)
        for tt in range(TT):
            pst, ppp = PSBC[tt % 2]
            mem = []
            for c in range(KC):
                def f(t, c=c, tt=tt, pst=pst):
                    return t.matmul(pst[:, 0:8], xn_t[c][:, tt * 128:(tt + 1) * 128],
                                    wg_t[:, c * 8:(c + 1) * 8],
                                    start=(c == 0), stop=(c == KC - 1))
                mem.append((f, [(xn[c], tt // 4), (wgb, 0)]))
            G_("t", mem, wr=[(ppp, 0)])
            E_("v", lambda v, pst=pst: v.tensor_add(
                sc_t[:, S_GL:S_GL + 8], bgb_t[:, :], pst[:, 0:8]),
                rd=[(ppp, 0), (bgbb, 0)], wr=[(scb, 0)])
            E_("v", lambda v: v.reduce_max(sc_t[:, S_G1:S_G1 + 1],
                                           sc_t[:, S_GL:S_GL + 8], axis=AX, negate=True),
               rd=[(scb, 0)], wr=[(scb, 4)])
            E_("s", lambda s: s.activation(sc_t[:, S_E8:S_E8 + 8],
                                           sc_t[:, S_GL:S_GL + 8], AF.Exp,
                                           bias=sc_t[:, S_G1:S_G1 + 1]),
               rd=[(scb, 0), (scb, 4)], wr=[(scb, 1)])
            E_("v", lambda v: v.reduce_max(sc_t[:, S_G2:S_G2 + 1],
                                           sc_t[:, S_E8:S_E8 + 8], axis=AX),
               rd=[(scb, 1)], wr=[(scb, 5)])
            E_("v", lambda v: v.tensor_scalar(
                sc_t[:, S_MS:S_MS + 8], sc_t[:, S_E8:S_E8 + 8],
                sc_t[:, S_G2:S_G2 + 1], NEG_BIG, OP.is_ge, OP.mult),
               rd=[(scb, 1), (scb, 5)], wr=[(scb, 2)])
            E_("v", lambda v: v.tensor_add(sc_t[:, S_MS:S_MS + 8],
                                           sc_t[:, S_MS:S_MS + 8],
                                           sc_t[:, S_E8:S_E8 + 8]),
               rd=[(scb, 2), (scb, 1)], wr=[(scb, 2)])
            E_("v", lambda v: v.reduce_max(sc_t[:, S_G3:S_G3 + 1],
                                           sc_t[:, S_MS:S_MS + 8], axis=AX),
               rd=[(scb, 2)], wr=[(scb, 6)])
            E_("v", lambda v: v.tensor_scalar(
                sc_t[:, S_CW:S_CW + 8], sc_t[:, S_E8:S_E8 + 8],
                sc_t[:, S_G3:S_G3 + 1], None, OP.is_ge),
               rd=[(scb, 1), (scb, 6)], wr=[(scb, 3)])
            E_("v", lambda v: v.tensor_mul(sc_t[:, S_CW:S_CW + 8],
                                           sc_t[:, S_CW:S_CW + 8],
                                           sc_t[:, S_E8:S_E8 + 8]),
               rd=[(scb, 3), (scb, 1)], wr=[(scb, 3)])
            E_("v", lambda v: v.reduce_sum(sc_t[:, S_G4:S_G4 + 1],
                                           sc_t[:, S_CW:S_CW + 8], axis=AX),
               rd=[(scb, 3)], wr=[(scb, 7)])
            E_("v", lambda v: v.reciprocal(sc_t[:, S_G4:S_G4 + 1],
                                           sc_t[:, S_G4:S_G4 + 1]),
               rd=[(scb, 7)], wr=[(scb, 7)])
            E_("v", lambda v, tt=tt: v.tensor_scalar(
                cwf_t[:, tt * 8:(tt + 1) * 8], sc_t[:, S_CW:S_CW + 8],
                sc_t[:, S_G4:S_G4 + 1], None, OP.mult),
               rd=[(scb, 3), (scb, 7)], wr=[(cwfb, tt)])
        E_("v", lambda v: v.tensor_copy(cwbf_t[:, :], cwf_t[:, :]),
           rd=[(cwfb, i) for i in range(TT)], wr=[(cwbfb, 0)])
        for tt in range(TT):
            k = tt % 2
            E_("t", lambda t, tt=tt, k=k: t.transpose(
                PT_T[k][0:8, 0:128], cwbf_t[:, tt * 8:(tt + 1) * 8], idb_t[:, :]),
               rd=[(cwbfb, 0), (consts, C_ID)], wr=[(ptb[k], 0)])
            E_("v", lambda v, tt=tt, k=k: v.tensor_copy(
                a16(A_CWT + tt * 128, 128, 0, 8), PT_T[k][0:8, 0:128]),
               rd=[(ptb[k], 0)], wr=[(cwtb, tt)])
        return cwtb

    def phase_attn(L):
        slab_off = [A_QKW, A_QKW + 1024, A_QKW + 2048, A_QKW + 3072,
                    A_WOW, A_WOW + 1024, A_P, A_P + 1024]
        wvs = [ar_16.claim(slab_off[c], slab_off[c] + 1024, f"wv{L}_{c}")
               for c in range(KC)]
        vbufs = [ar_16.claim(A_V + tt * 1024, A_V + (tt + 1) * 1024, f"V{L}_{tt}")
                 for tt in range(TT)]
        wv_d = dr["l0v"] if L == 0 else dr["l1v"]
        if L == 1:
            bvrb = ar_32.claim(Z_BVR, Z_BVR + 1024, f"bvr{L}")
            bvbb = ar_32.claim(Z_CWB, Z_CWB + 1024, f"bvb{L}")
            E_("y", lambda y: y.dma_start(a32(Z_BVR, 1024, 0, 1), dr["l1bv"][:, :]),
               wr=[(bvrb, 0)], dma="bvr")
            for half in range(2):
                pst, ppp = PSBC[half]
                E_("t", lambda t, half=half, pst=pst: t.matmul(
                    pst[:, :], onerow_t[:, :], a32(Z_BVR + half * 512, 512, 0, 1),
                    start=True, stop=True),
                   rd=[(consts, C_ROW), (bvrb, 0)], wr=[(ppp, 0)])
                E_("v", lambda v, half=half, pst=pst: v.tensor_copy(
                    a32(Z_CWB + half * 512, 512), pst[:, :]),
                   rd=[(ppp, 0)], wr=[(bvbb, 0)])
        for c in range(KC):
            E_("q", lambda q, c=c: q.dma_start(
                a16(slab_off[c], 1024), wv_d[c * 128:(c + 1) * 128, :]),
                wr=[(wvs[c], 0)], dma="wv")
        em.bulk_fixup(wvs, (em.dma_chan("wv"), em.dma_chan("wv").n))
        for tt in range(TT):
            half = tt % 2
            mem = []
            for c in range(KC):
                for dv in range(2):
                    def f(t, tt=tt, c=c, half=half, dv=dv):
                        return t.matmul(
                            PS_A[:, half * 1024 + dv * 512:half * 1024 + dv * 512 + 512],
                            xn_t[c][:, tt * 128:(tt + 1) * 128],
                            a16(slab_off[c] + dv * 512, 512),
                            start=(c == 0), stop=(c == KC - 1))
                    mem.append((f, [(xn[c], tt // 4), (wvs[c], 0)]))
            G_("t", mem, wr=[(psA, 2 * half), (psA, 2 * half + 1)])
            if L == 0:
                E_("v", lambda v, tt=tt, half=half: v.tensor_copy(
                    a16(A_V + tt * 1024, 1024), PS_A[:, half * 1024:half * 1024 + 1024]),
                   rd=[(psA, 2 * half), (psA, 2 * half + 1)], wr=[(vbufs[tt], 0)])
            else:
                E_("v", lambda v, tt=tt, half=half: v.tensor_add(
                    a16(A_V + tt * 1024, 1024), a32(Z_CWB, 1024),
                    PS_A[:, half * 1024:half * 1024 + 1024]),
                   rd=[(psA, 2 * half), (psA, 2 * half + 1), (bvbb, 0)],
                   wr=[(vbufs[tt], 0)])

        qtp = ar_16.claim(A_QT, A_QT + 2048, f"qt{L}", 4)
        ktp = ar_16.claim(A_KT, A_KT + 2048, f"kt{L}", 4)
        otp = ar_16.claim(A_OT, A_OT + 2048, f"ot{L}", 16)
        pb = [ar_16.claim(A_P, A_P + 2048, f"P{L}a", 16),
              ar_16.claim(A_P2, A_P2 + 2048, f"P{L}b", 16),
              ar_16.claim(A_P3, A_P3 + 2048, f"P{L}c", 16)]
        PO = [A_P, A_P2, A_P3]
        pts = [ar_16.claim(A_PT + i * 512, A_PT + (i + 1) * 512, f"pts{L}{i}")
               for i in range(2)]
        wqs = [ar_16.claim(A_QKW + i * 2048, A_QKW + i * 2048 + 1024, f"wq{L}{i}")
               for i in range(2)]
        wks = [ar_16.claim(A_QKW + i * 2048 + 1024, A_QKW + (i + 1) * 2048, f"wk{L}{i}")
               for i in range(2)]
        wos = [ar_16.claim(A_WOW + i * 1024, A_WOW + (i + 1) * 1024, f"wo{L}{i}")
               for i in range(2)]
        mcolb = Buf("mcol", 1)
        mxb = Buf("mx", 1)
        nmb = Buf("nm", 1)
        lb = Buf("l", 1)
        rlb = Buf("rl", 1)

        wq_d = dr["l0q"] if L == 0 else dr["l1q"]
        wk_d = dr["l0k"] if L == 0 else dr["l1k"]
        wo_d = dr["l0o"] if L == 0 else dr["l1o"]
        bo_t = bias_t["l0bo"] if L == 0 else bias_t["l1bo"]

        def ld_qk(pair, sl):
            ch = "wq" + str(sl)
            E_("q", lambda q, pair=pair, sl=sl: q.dma_start(
                a16(A_QKW + sl * 2048, 1024).rearrange("p (k n) -> p k n", n=128),
                bass.AP(wq_d, pair * (8 * 128 * 128),
                        [[128, 128], [128 * 128, 8], [1, 128]])),
                wr=[(wqs[sl], 0)], dma=ch)
            tok = E_("q", lambda q, pair=pair, sl=sl: q.dma_start(
                a16(A_QKW + sl * 2048 + 1024, 1024).rearrange("p (k n) -> p k n", n=128),
                bass.AP(wk_d, pair * (8 * 128 * 128),
                        [[128, 128], [128 * 128, 8], [1, 128]])),
                wr=[(wks[sl], 0)], dma=ch)
            em.bulk_fixup([wqs[sl], wks[sl]], tok)

        def ld_wo(pair, sl):
            E_("q", lambda q, pair=pair, sl=sl: q.dma_start(
                a16(A_WOW + sl * 1024, 1024), wo_d[pair * 128:(pair + 1) * 128, :]),
                wr=[(wos[sl], 0)], dma="wo" + str(sl))

        ld_qk(0, 0)
        ld_wo(0, 0)

        for pair in range(NP):
            sl = pair % 2
            if pair + 1 < NP:
                ld_qk(pair + 1, 1 - sl)
                ld_wo(pair + 1, 1 - sl)
            for which, wbuf, dstoff, dstb, bcol in (
                    (0, wqs[sl], A_QT, qtp, "l1bq"), (1, wks[sl], A_KT, ktp, "l1bk")):
                for tg in range(NTG4):
                    pst, ppp = PSBC[tg % 2]
                    mem = []
                    for c in range(KC):
                        def f(t, which=which, sl=sl, c=c, tg=tg, pst=pst):
                            return t.matmul(
                                pst[:, :],
                                a16(A_QKW + sl * 2048 + which * 1024 + c * 128, 128),
                                xn_t[c][:, tg * 512:(tg + 1) * 512],
                                start=(c == 0), stop=(c == KC - 1))
                        mem.append((f, [(wbuf, 0), (xn[c], tg)]))
                    G_("t", mem, wr=[(ppp, 0)])
                    if L == 0:
                        E_("v", lambda v, dstoff=dstoff, tg=tg, pst=pst: v.tensor_copy(
                            a16(dstoff + tg * 512, 512), pst[:, :]),
                           rd=[(ppp, 0)], wr=[(dstb, tg)])
                    else:
                        E_("v", lambda v, dstoff=dstoff, tg=tg, pst=pst, bcol=bcol,
                           pair=pair: v.tensor_scalar(
                               a16(dstoff + tg * 512, 512), pst[:, :],
                               bias_t[bcol][:, pair:pair + 1], None, OP.add),
                           rd=[(ppp, 0), (biasb, B_IDX[bcol])], wr=[(dstb, tg)])

            def emit_S(qi, h, soff, ps):
                hp = h & 1
                nk = (qi + 1) * 128
                nkb = (nk + 511) // 512
                for kb in range(nkb):
                    n = min(512, nk - kb * 512)
                    E_("t", lambda t, qi=qi, hp=hp, kb=kb, n=n, soff=soff: t.matmul(
                        PS_A[:, soff + kb * 512:soff + kb * 512 + n],
                        a16(A_QT + qi * 128, 128, 64 * hp, 64 * hp + 64),
                        a16(A_KT + kb * 512, n, 64 * hp, 64 * hp + 64),
                        start=True, stop=True),
                       rd=[(qtp, qi // 4), (ktp, kb)], wr=[(psA, soff // 512 + kb)])
                dkb = (soff + qi * 128) // 512
                E_("v", lambda v, qi=qi, soff=soff: v.tensor_add(
                    PS_A[:, soff + qi * 128:soff + (qi + 1) * 128],
                    PS_A[:, soff + qi * 128:soff + (qi + 1) * 128],
                    mask_t[:, :]), rd=[(consts, C_MASK)], wr=[(psA, dkb)])
                E_("s", lambda s, nk=nk, ps=ps, soff=soff: s.activation(
                    a16(PO[ps], nk), PS_A[:, soff:soff + nk], AF.Exp,
                    scale=SCALE, accum_out=sc_t[:, S_L:S_L + 1]),
                   rd=[(psA, soff // 512 + kb) for kb in range(nkb)],
                   wr=[(pb[ps], k) for k in range(qi + 1)] + [(lb, 0)])
                E_("v", lambda v: v.reciprocal(sc_t[:, S_RL:S_RL + 1],
                                               sc_t[:, S_L:S_L + 1]),
                   rd=[(lb, 0)], wr=[(rlb, 0)])
                E_("v", lambda v, nk=nk, ps=ps: v.tensor_scalar(
                    a16(PO[ps], nk), a16(PO[ps], nk), sc_t[:, S_RL:S_RL + 1], None,
                    OP.mult),
                   rd=[(rlb, 0)] + [(pb[ps], k) for k in range(qi + 1)],
                   wr=[(pb[ps], k) for k in range(qi + 1)])
                if _DBG.get("dumpP") == (pair, qi, h):
                    E_("q", lambda q, nk=nk, hp=hp: q.dma_start(
                        outT[0:128, 0:nk], a16(PO[hp], nk)),
                       rd=[(pb[hp], k) for k in range(qi + 1)], dma="dbg")
                    E_("q", lambda q: q.dma_start(
                        outT[128:256, 0:1024], a16(A_V + 8 * 1024, 1024)),
                       rd=[(vbufs[8], 0)], dma="dbg")
                    E_("q", lambda q: q.dma_start(
                        outT[256:384, 0:1024], a16(A_KT, 1024, 0, 128)),
                       rd=[(ktp, 0), (ktp, 1)], dma="dbg")
                    E_("q", lambda q: q.dma_start(
                        outT[384:512, 0:1024], a16(A_KT + 1024, 1024, 0, 128)),
                       rd=[(ktp, 2), (ktp, 3)], dma="dbg")
                    E_("q", lambda q, qi=qi: q.dma_start(
                        outT[512:640, 0:128], a16(A_QT + qi * 128, 128)),
                       rd=[(qtp, qi // 4)], dma="dbg")

            def emit_TPV(qi, h, ps):
                hp = h & 1
                otps, otpb2 = PSBC[qi % 2]
                nkc = qi + 1
                batches = [(b4 * 4, min(nkc, b4 * 4 + 4)) for b4 in range((nkc + 3) // 4)]

                def tp_batch(j):
                    lo, hi = batches[j]
                    k = j % 2
                    for kc in range(lo, hi):
                        E_("t", lambda t, kc=kc, k=k, ps=ps, lo=lo: t.transpose(
                            PT_T[k][:, (kc - lo) * 128:(kc - lo) * 128 + 128],
                            a16(PO[ps] + kc * 128, 128), idb_t[:, :]),
                           rd=[(pb[ps], kc), (consts, C_ID)],
                           wr=[(ptb[k], kc - lo)], inc=(kc == hi - 1))
                    n = (hi - lo) * 128
                    if k == 0:
                        E_("v", lambda v, k=k, n=n: v.tensor_copy(
                            a16(A_PT + k * 512, n), PT_T[k][:, 0:n]),
                           rd=[(ptb[k], i) for i in range(hi - lo)], wr=[(pts[k], 0)])
                    else:
                        E_("s", lambda s, k=k, n=n: s.copy(
                            a16(A_PT + k * 512, n), PT_T[k][:, 0:n]),
                           rd=[(ptb[k], i) for i in range(hi - lo)], wr=[(pts[k], 0)])

                def pv_batch(j):
                    lo, hi = batches[j]
                    k = j % 2
                    for kc in range(lo, hi):
                        E_("t", lambda t, kc=kc, k=k, hp=hp, h=h, qi=qi, otps=otps,
                           lo=lo: t.matmul(
                               otps[64 * hp:64 * hp + 64, 0:128],
                               a16(A_V + kc * 1024 + h * 64, 64),
                               a16(A_PT + k * 512 + (kc - lo) * 128, 128),
                               start=(kc == 0), stop=(kc == qi),
                               tile_position=(0, 64 * hp)),
                           rd=[(pts[k], 0), (vbufs[kc], 0)],
                           wr=[(otpb2, 0)] if kc == qi else [], inc=(kc == qi))

                for j in range(len(batches)):
                    tp_batch(j)
                    if j > 0:
                        pv_batch(j - 1)
                pv_batch(len(batches) - 1)
                if hp == 1:
                    E_("v", lambda v, qi=qi, otps=otps: v.tensor_copy(
                        a16(A_OT + qi * 128, 128), otps[:, 0:128]),
                       rd=[(otpb2, 0)], wr=[(otp, qi)])

            pend = []
            idx = 0
            for qi in range(_DBG.get('maxqi', TT)):
                for h in (2 * pair, 2 * pair + 1):
                    soff = (idx % 2) * 1024 if qi < 8 else 0
                    ps = idx % 3
                    emit_S(qi, h, soff, ps)
                    pend.append((qi, h, ps))
                    idx += 1
                    want = 2 if qi < 8 else 1
                    while len(pend) > want:
                        emit_TPV(*pend.pop(0))
            while pend:
                emit_TPV(*pend.pop(0))

            for dc in range(KC):
                for tg in range(NTG4):
                    pst, ppp = PSBC[(dc * NTG4 + tg) % 2]
                    E_("t", lambda t, sl=sl, dc=dc, tg=tg, pst=pst: t.matmul(
                        pst[:, :], a16(A_WOW + sl * 1024 + dc * 128, 128),
                        a16(A_OT + tg * 512, 512), start=True, stop=True),
                       rd=[(wos[sl], 0)] + [(otp, q) for q in range(tg * 4, tg * 4 + 4)],
                       wr=[(ppp, 0)])
                    if pair == 0:
                        E_("v", lambda v, dc=dc, tg=tg, pst=pst, bo_t=bo_t:
                           v.scalar_tensor_tensor(
                               res_t[dc][:, tg * 512:(tg + 1) * 512], pst[:, :],
                               bo_t[:, dc:dc + 1],
                               res_t[dc][:, tg * 512:(tg + 1) * 512], OP.add, OP.add),
                           rd=[(ppp, 0), (biasb, B_IDX['l0bo'] if L == 0 else B_IDX['l1bo']), (res[dc], tg)], wr=[(res[dc], tg)])
                    else:
                        E_("v", lambda v, dc=dc, tg=tg, pst=pst: v.tensor_add(
                            res_t[dc][:, tg * 512:(tg + 1) * 512],
                            res_t[dc][:, tg * 512:(tg + 1) * 512], pst[:, :]),
                           rd=[(ppp, 0), (res[dc], tg)], wr=[(res[dc], tg)])

    def phase_moe(cwtb):
        hp = [ar_16.claim(A_V + f_ * 1024, A_V + (f_ + 1) * 1024, f"hp{f_}", 2)
              for f_ in range(FC)]
        w13 = [ar_16.claim(A_W13 + i * 2048, A_W13 + (i + 1) * 2048, f"w13_{i}")
               for i in range(2)]
        w2h = [ar_16.claim(A_W2R + i * 2048, A_W2R + (i + 1) * 2048, f"w2h{i}")
               for i in range(2)]
        cwbb = ar_32.claim(Z_CWB, Z_CWB + 1024, "cwb", 2)
        W13O = [A_W13, A_W13 + 2048]
        W2O = [A_W2R, A_W2R + 2048]

        def ld_w13(e, fc, sl):
            ch = "w13" + str(sl)
            E_("q", lambda q, e=e, fc=fc, sl=sl: q.dma_start(
                a16(W13O[sl], 1024).rearrange("p (k n) -> p k n", n=128),
                bass.AP(dr["w1r"], (e * 32 + fc) * 8 * 128 * 128,
                        [[128, 128], [128 * 128, 8], [1, 128]])),
                wr=[(w13[sl], 0)], dma=ch)
            tok = E_("q", lambda q, e=e, fc=fc, sl=sl: q.dma_start(
                a16(W13O[sl] + 1024, 1024).rearrange("p (k n) -> p k n", n=128),
                bass.AP(dr["w3r"], (e * 32 + fc) * 8 * 128 * 128,
                        [[128, 128], [128 * 128, 8], [1, 128]])),
                wr=[(w13[sl], 0)], dma=ch)
            em.bulk_fixup([w13[sl]], tok)

        def ld_w2h(e, dc, hh, sl):
            E_("q", lambda q, e=e, dc=dc, hh=hh, sl=sl: q.dma_start(
                a16(W2O[sl], 2048).rearrange("p (k n) -> p k n", n=128),
                bass.AP(dr["w2r"], ((e * 8 + dc) * 32 + hh * 16) * 128 * 128,
                        [[128, 128], [128 * 128, 16], [1, 128]])),
                wr=[(w2h[sl], 0)], dma="w2" + str(sl))

        ld_w13(0, 0, 0)
        step = 0
        for e in range(E):
            for tgi in range(NTG):
                t0 = tgi * 1024
                for half in range(2):
                    pst, ppp = PSBC[half]
                    E_("t", lambda t, e=e, t0=t0, half=half, pst=pst: t.matmul(
                        pst[:, :], oh_t[:, e * 128:(e + 1) * 128],
                        a16(A_CWT + t0 + half * 512, 512, 0, 8),
                        start=True, stop=True),
                       rd=[(consts, C_OH)] + [(cwtb, (t0 + half * 512) // 128 + i)
                                           for i in range(4)],
                       wr=[(ppp, 0)])
                    E_("v", lambda v, half=half, pst=pst: v.tensor_copy(
                        a32(Z_CWB + half * 512, 512), pst[:, :]),
                       rd=[(ppp, 0)], wr=[(cwbb, half)])
                ROT = [(PS_B[:, :], psB, 0), (PS_C[:, :], psC, 0),
                       (PS_A[:, 0:512], psA, 0), (PS_A[:, 512:1024], psA, 1)]
                rot = 0
                for fc in range(FC):
                    sl = step % 2
                    step += 1
                    nxt = step
                    if nxt < E * NTG * FC:
                        ne, rem = divmod(nxt, NTG * FC)
                        _ntg, nfc = divmod(rem, FC)
                        ld_w13(ne, nfc, 1 - sl)
                    if fc == FC - 14:
                        ld_w2h(e, 0, 0, 0)
                    elif fc == FC - 10:
                        ld_w2h(e, 0, 1, 1)
                    for stg in range(2):
                        pstg, pppg, ppg = ROT[rot % 4]
                        pstu, pppu, ppu = ROT[(rot + 1) % 4]
                        rot += 2
                        memg, memu = [], []
                        for c in range(KC):
                            def fg(t, sl=sl, c=c, t0=t0, stg=stg, pstg=pstg):
                                return t.matmul(
                                    pstg, a16(W13O[sl] + c * 128, 128),
                                    xn_t[c][:, t0 + stg * 512:t0 + stg * 512 + 512],
                                    start=(c == 0), stop=(c == KC - 1))
                            memg.append((fg, [(w13[sl], 0),
                                              (xn[c], (t0 + stg * 512) // 512)]))
                            def fu(t, sl=sl, c=c, t0=t0, stg=stg, pstu=pstu):
                                return t.matmul(
                                    pstu, a16(W13O[sl] + 1024 + c * 128, 128),
                                    xn_t[c][:, t0 + stg * 512:t0 + stg * 512 + 512],
                                    start=(c == 0), stop=(c == KC - 1))
                            memu.append((fu, [(w13[sl], 0),
                                              (xn[c], (t0 + stg * 512) // 512)]))
                        G_("t", memg, wr=[(pppg, ppg)])
                        E_("s", lambda s, fc=fc, stg=stg, pstg=pstg: s.activation(
                            a16(A_V + fc * 1024 + stg * 512, 512), pstg, AF.Silu),
                           rd=[(pppg, ppg)], wr=[(hp[fc], stg)])
                        G_("t", memu, wr=[(pppu, ppu)])
                        E_("v", lambda v, fc=fc, stg=stg, pstu=pstu: v.tensor_mul(
                            a16(A_V + fc * 1024 + stg * 512, 512),
                            a16(A_V + fc * 1024 + stg * 512, 512), pstu),
                           rd=[(pppu, ppu), (hp[fc], stg)], wr=[(hp[fc], stg)])
                for dc in range(KC):
                    for stg in range(2):
                        pidx = (dc * 2 + stg) % 4
                        for hh in range(2):
                            sl2 = hh
                            mem = []
                            for j in range(16):
                                fc = hh * 16 + j
                                def fd(t, sl2=sl2, j=j, fc=fc, stg=stg, pidx=pidx,
                                       first=(hh == 0 and j == 0),
                                       last=(hh == 1 and j == 15)):
                                    return t.matmul(
                                        PS_A[:, pidx * 512:pidx * 512 + 512],
                                        a16(W2O[sl2] + j * 128, 128),
                                        a16(A_V + fc * 1024 + stg * 512, 512),
                                        start=first, stop=last)
                                mem.append((fd, [(w2h[sl2], 0), (hp[fc], stg)]))
                            G_("t", mem, wr=[(psA, pidx)])
                            if stg == 1:
                                if hh == 0 and dc + 1 < KC:
                                    ld_w2h(e, dc + 1, 0, 0)
                                elif hh == 1 and dc + 1 < KC:
                                    ld_w2h(e, dc + 1, 1, 1)
                        g4 = (t0 + stg * 512) // 512
                        tsl = (dc * 2 + stg) % 2
                        E_("v", lambda v, pidx=pidx, stg=stg, tsl=tsl: v.tensor_mul(
                            a32(Z_TMP + tsl * 512, 512),
                            PS_A[:, pidx * 512:pidx * 512 + 512],
                            a32(Z_CWB + stg * 512, 512)),
                           rd=[(psA, pidx), (cwbb, stg)], wr=[(tmpf, tsl)])
                        E_("v", lambda v, dc=dc, g4=g4, tsl=tsl: v.tensor_add(
                            res_t[dc][:, g4 * 512:(g4 + 1) * 512],
                            res_t[dc][:, g4 * 512:(g4 + 1) * 512],
                            a32(Z_TMP + tsl * 512, 512)),
                           rd=[(tmpf, tsl), (res[dc], g4)], wr=[(res[dc], g4)])

    def phase_alpha():
        slA = ar_32.claim(Z_ALA, Z_ALA + 1024, "slA")
        slB = ar_32.claim(Z_ALB, Z_ALB + 2048, "slB")
        tcol = Buf("tcol", 1)
        for i in range(IC):
            E_("y", lambda y, i=i: y.dma_start(
                a32(Z_ALA, 1024), dr["w1n"][i * 128:(i + 1) * 128, :]),
                wr=[(slA, 0)], dma="alA")
            E_("v", lambda v, i=i: v.reduce_sum(
                al1_t[:, i:i + 1], a32(Z_ALA, 1024), axis=AX,
                apply_absolute_value=True), rd=[(slA, 0)], wr=[(al1b, 0)])
        E_("v", lambda v: v.tensor_scalar(al1_t[:, :], al1_t[:, :], 1.0 / D, None,
                                          OP.mult), rd=[(al1b, 0)], wr=[(al1b, 0)])
        for i in range(KC):
            for hh in range(2):
                E_("y", lambda y, i=i, hh=hh: y.dma_start(
                    a32(Z_ALB, 2048), dr["w2n"][i * 128:(i + 1) * 128,
                                                hh * 2048:(hh + 1) * 2048]),
                    wr=[(slB, 0)], dma="alB")
                E_("v", lambda v, hh=hh: v.reduce_sum(
                    sc_t[:, S_A2 + hh:S_A2 + hh + 1], a32(Z_ALB, 2048), axis=AX,
                    apply_absolute_value=True), rd=[(slB, 0)], wr=[(tcol, 0)])
            E_("v", lambda v, i=i: v.reduce_sum(
                al2_t[:, i:i + 1], sc_t[:, S_A2:S_A2 + 2], axis=AX),
               rd=[(tcol, 0)], wr=[(al2b, 0)])
        E_("v", lambda v: v.tensor_scalar(al2_t[:, :], al2_t[:, :], 1.0 / F, None,
                                          OP.mult), rd=[(al2b, 0)], wr=[(al2b, 0)])
        if _DBG.get("dump_al"):
            E_("q", lambda q: q.dma_start(dbgT[:, 0:32], al1_t[:, :]),
               rd=[(al1b, 0)], dma="dbg")
            E_("q", lambda q: q.dma_start(dbgT[:, 32:40], al2_t[:, :]),
               rd=[(al2b, 0)], dma="dbg")

    def phase_ffn1():
        hT = [ar_16.claim(A_V + i * 1024, A_V + (i + 1) * 1024, f"hT{i}", 2)
              for i in range(IC)]
        w1t = [ar_16.claim(A_W13 + i * 1024, A_W13 + (i + 1) * 1024, f"w1t{i}")
               for i in range(2)]
        s1t = [ar_16.claim(A_W13 + 2048 + i * 1024, A_W13 + 2048 + (i + 1) * 1024,
                           f"s1t{i}") for i in range(2)]
        w2t = [ar_16.claim(A_W2R + i * 2048, A_W2R + (i + 1) * 2048, f"w2t{i}")
               for i in range(2)]
        W1O = [A_W13, A_W13 + 1024]
        S1O = [A_W13 + 2048, A_W13 + 3072]
        W2TO = [A_W2R, A_W2R + 2048]

        def ld_s1(i, sl):
            E_("q", lambda q, i=i, sl=sl: q.dma_start(
                a16(W1O[sl], 1024).rearrange("p (k n) -> p k n", n=128),
                bass.AP(dr["w1tr"], i * 8 * 128 * 128,
                        [[128, 128], [128 * 128, 8], [1, 128]])),
                wr=[(w1t[sl], 0)], dma="s1" + str(sl))
            E_("s", lambda s, sl=sl: s.activation(
                a16(S1O[sl], 1024), a16(W1O[sl], 1024), AF.Sign),
               rd=[(w1t[sl], 0)], wr=[(s1t[sl], 0)])

        def ld_s2h(dc, hh, sl):
            E_("q", lambda q, dc=dc, hh=hh, sl=sl: q.dma_start(
                a16(W2TO[sl], 2048).rearrange("p (k n) -> p k n", n=128),
                bass.AP(dr["w2tr"], (dc * 32 + hh * 16) * 128 * 128,
                        [[128, 128], [128 * 128, 16], [1, 128]])),
                wr=[(w2t[sl], 0)], dma="s2" + str(sl))
            E_("s", lambda s, sl=sl: s.activation(
                a16(W2TO[sl], 2048), a16(W2TO[sl], 2048), AF.Sign),
               rd=[(w2t[sl], 0)], wr=[(w2t[sl], 0)])

        for tgi in range(NTG):
            t0 = tgi * 1024
            if tgi == 0:
                ld_s1(0, 0)
            for i in range(IC):
                sl = i % 2
                if i + 1 < IC:
                    ld_s1(i + 1, 1 - sl)
                if i == IC - 14:
                    ld_s2h(0, 0, 0)
                elif i == IC - 10:
                    ld_s2h(0, 1, 1)
                for stg in range(2):
                    pst, ppp = PSBC[stg]
                    mem = []
                    for c in range(KC):
                        def fu(t, sl=sl, c=c, t0=t0, stg=stg, pst=pst):
                            return t.matmul(
                                pst[:, :], a16(S1O[sl] + c * 128, 128),
                                xn_t[c][:, t0 + stg * 512:t0 + stg * 512 + 512],
                                start=(c == 0), stop=(c == KC - 1))
                        mem.append((fu, [(s1t[sl], 0),
                                         (xn[c], (t0 + stg * 512) // 512)]))
                    G_("t", mem, wr=[(ppp, 0)])
                    E_("s", lambda s, i=i, stg=stg, pst=pst: s.activation(
                        a16(A_V + i * 1024 + stg * 512, 512), pst[:, :], AF.Gelu,
                        bias=b1c_t[:, i:i + 1], scale=al1_t[:, i:i + 1]),
                       rd=[(ppp, 0), (biasb, B_IDX['b1c']), (al1b, 0)], wr=[(hT[i], stg)])
            for dc in range(KC):
                for stg in range(2):
                    pidx = (dc * 2 + stg) % 4
                    for hh in range(2):
                        sl2 = hh
                        mem = []
                        for j in range(16):
                            i = hh * 16 + j
                            def fd(t, sl2=sl2, j=j, i=i, stg=stg, pidx=pidx,
                                   first=(hh == 0 and j == 0),
                                   last=(hh == 1 and j == 15)):
                                return t.matmul(
                                    PS_A[:, pidx * 512:pidx * 512 + 512],
                                    a16(W2TO[sl2] + j * 128, 128),
                                    a16(A_V + i * 1024 + stg * 512, 512),
                                    start=first, stop=last)
                            mem.append((fd, [(w2t[sl2], 0), (hT[i], stg)]))
                        G_("t", mem, wr=[(psA, pidx)])
                        if stg == 1:
                            if hh == 0 and dc + 1 < KC:
                                ld_s2h(dc + 1, 0, 0)
                            elif hh == 1 and dc + 1 < KC:
                                ld_s2h(dc + 1, 1, 1)
                            elif hh == 1 and dc == KC - 1 and tgi == 0:
                                ld_s1(0, 0)
                    g4 = (t0 + stg * 512) // 512
                    tsl = (dc * 2 + stg) % 2
                    E_("s", lambda s, pidx=pidx, dc=dc, tsl=tsl: s.mul(
                        a32(Z_TMP + tsl * 512, 512),
                        PS_A[:, pidx * 512:pidx * 512 + 512], al2_t[:, dc:dc + 1]),
                       rd=[(psA, pidx), (al2b, 0)], wr=[(tmpf, tsl)])
                    E_("v", lambda v, dc=dc, g4=g4, tsl=tsl: v.scalar_tensor_tensor(
                        res_t[dc][:, g4 * 512:(g4 + 1) * 512],
                        a32(Z_TMP + tsl * 512, 512), bias_t["b2c"][:, dc:dc + 1],
                        res_t[dc][:, g4 * 512:(g4 + 1) * 512], OP.add, OP.add),
                       rd=[(tmpf, tsl), (biasb, B_IDX['b2c']), (res[dc], g4)],
                       wr=[(res[dc], g4)])

    phase_ln(0)
    cwtb = None
    if upto == "a1solo":
        phase_attn(1)
    elif upto != "ln0":
        phase_attn(0)
        cwtb = phase_gate()
    if upto in ("moe0", "ln1", "attn1", "full", "xnl1"):
        phase_moe(cwtb)
    if upto in ("ln1", "attn1", "full", "xnl1"):
        phase_ln(1)
    if upto == "full":
        phase_alpha()
    if upto in ("attn1", "full"):
        phase_attn(1)
    if upto == "full":
        phase_ffn1()

    if upto in ("ln0", "xnl1"):
        for c in range(KC):
            E_("q", lambda q, c=c: q.dma_start(
                outT[c * 128:(c + 1) * 128, :], xn_t[c][:, :]),
               rd=[(xn[c], p) for p in range(4)], dma="dbg")
        em.final_wait("q")
    else:
        for c in range(KC):
            E_("y", lambda y, c=c: y.dma_start(
                outT[c * 128:(c + 1) * 128, :], res_t[c][:, :]),
               rd=[(res[c], p) for p in range(4)], dma="out")
    em.final_wait("y")

    with ExitStack() as st:
        for ch in list(em.ch.values()) + list(em.dch.values()):
            ch.h = st.enter_context(nc.semaphore("ch_" + ch.name))
        with nc.Block() as block:
            @block.tensor
            def _(eng):
                _replay(em, "t", eng)

            @block.vector
            def _(eng):
                _replay(em, "v", eng)

            @block.scalar
            def _(eng):
                _replay(em, "s", eng)

            @block.gpsimd
            def _(eng):
                _replay(em, "q", eng)

            @block.sync
            def _(eng):
                _replay(em, "y", eng)
    return nc, em


_NC_CACHE = {}


def _get_nc(upto="full"):
    if upto not in _NC_CACHE:
        _NC_CACHE[upto] = _build(upto)
    return _NC_CACHE[upto]


def kernel(**inputs):
    x = np.asarray(inputs["L0_attn_x"], dtype=np.float32)
    host = _host_prep(inputs)
    nc, _em = _get_nc("full")
    in_maps = []
    for b in range(NCORE):
        m = {"xT": np.ascontiguousarray(x[b].T)}
        m.update(host)
        in_maps.append(m)
    res = run_bass_kernel_spmd(nc, in_maps, core_ids=list(range(NCORE)))
    out = np.stack([np.ascontiguousarray(r["outT"].T) for r in res.results], axis=0)
    return out


# version 07
# speedup vs baseline: 1.0270x; 1.0270x over previous
import numpy as np
import ml_dtypes
import concourse.bass as bass
import concourse.mybir as mybir
from concourse.bass_utils import run_bass_kernel_spmd
from contextlib import ExitStack

F32 = mybir.dt.float32
BF16 = mybir.dt.bfloat16
AX = mybir.AxisListType.X
OP = mybir.AluOpType
AF = mybir.ActivationFunctionType

D = 1024
T = 2048
KC = 8
TT = 16
NP = 8
E = 8
F = 4096
FC = 32
NTG = 2
IC = 32
NTG4 = 4
NCORE = 8
SCALE = 0.125
_DBG = {}
NEG_BIG = -1.0e9



class Chan:
    __slots__ = ("name", "n", "h")

    def __init__(self, name):
        self.name, self.n, self.h = name, 0, None


class Buf:
    __slots__ = ("name", "w", "r", "birth")

    def __init__(self, name, parts=1, birth=None):
        self.name = name
        self.w = [None] * parts
        self.r = [dict() for _ in range(parts)]
        self.birth = birth or []


class Arena:

    def __init__(self):
        self.live = []

    def claim(self, lo, hi, name, parts=1):
        toks = []
        for a, b, ob in self.live:
            if a < hi and lo < b:
                for p in range(len(ob.w)):
                    if ob.w[p] is not None:
                        toks.append(ob.w[p])
                    toks.extend(ob.r[p].values())
        nb = Buf(name, parts, birth=toks)
        self.live.append((lo, hi, nb))
        return nb


class Em:
    def __init__(self):
        self.ch = {e: Chan(e) for e in "tvsqy"}
        self.dch = {}
        self.ops = {e: [] for e in "tvsqy"}
        self.waited = {e: {} for e in "tvsqy"}
        self.pend = {}
        self.n_inst = 0

    def dma_chan(self, name):
        if name not in self.dch:
            self.dch[name] = Chan(name)
        return self.dch[name]

    def _wait(self, e, tok):
        ch, val = tok
        if self.waited[e].get(ch.name, 0) >= val:
            return
        assert val <= ch.n, f"wait on future token {ch.name}:{val} > {ch.n}"
        self.waited[e][ch.name] = val
        self.ops[e].append((0, ch, val))

    def _deps(self, rd, wr):
        deps = []
        for b, p in rd:
            t = b.w[p]
            if t is not None:
                deps.append(t)
        for b, p in wr:
            deps.extend(b.birth)
            if b.w[p] is not None:
                deps.append(b.w[p])
            deps.extend(b.r[p].values())
        return deps

    def _upd(self, rd, wr, tok):
        for b, p in rd:
            b.r[p][tok[0].name] = tok
        for b, p in wr:
            b.w[p] = tok
            b.r[p] = {}

    def emit(self, e, fn, rd=(), wr=(), dma=None, inc=True):
        for t in self._deps(rd, wr):
            self._wait(e, t)
        if not inc:
            self.ops[e].append((1, fn, None, 0))
            self.n_inst += 1
            self.pend.setdefault(e, []).append((list(rd), list(wr)))
            return None
        if dma is not None:
            ch = self.dma_chan(dma) if isinstance(dma, str) else dma
            amt = 16
        else:
            ch = self.ch[e]
            amt = 1
        ch.n += amt
        self.ops[e].append((1, fn, ch, amt))
        self.n_inst += 1
        tok = (ch, ch.n)
        for prd, pwr in self.pend.pop(e, []):
            self._upd(prd, pwr, tok)
        self._upd(rd, wr, tok)
        return tok

    def bulk_fixup(self, bufs, tok):
        for b in bufs:
            for p in range(len(b.w)):
                if b.w[p] is not None:
                    b.w[p] = tok

    def emit_group(self, e, members, wr=()):
        all_rd = []
        for fn, rd in members:
            for t in self._deps(rd, wr):
                self._wait(e, t)
            all_rd.extend(rd)
        ch = self.ch[e]
        for i, (fn, rd) in enumerate(members):
            last = i == len(members) - 1
            if last:
                ch.n += 1
            self.ops[e].append((1, fn, ch if last else None, 1))
            self.n_inst += 1
        tok = (ch, ch.n)
        self._upd(all_rd, wr, tok)
        return tok

    def final_wait(self, e):
        for ch in list(self.ch.values()) + list(self.dch.values()):
            if ch.n > 0:
                self.ops[e].append((0, ch, ch.n))


def _replay(em, e, eng):
    for entry in em.ops[e]:
        if entry[0] == 0:
            _, ch, val = entry
            eng.wait_ge(ch.h, val)
        else:
            _, fn, ch, amt = entry
            ins = fn(eng)
            if ch is not None:
                ins.then_inc(ch.h, amt)



def _pair_blocks(w):
    return np.ascontiguousarray(w.reshape(8, 128, 8, 128).transpose(2, 0, 1, 3))


def _cols(b, n):
    return np.ascontiguousarray(b.reshape(n, 128).T)


def _host_prep(inputs):
    f32 = lambda a: np.ascontiguousarray(np.asarray(a, dtype=np.float32))
    d = {}
    d["l0q"] = _pair_blocks(f32(inputs["L0_attn_Wq"]))
    d["l0k"] = _pair_blocks(f32(inputs["L0_attn_Wk"]))
    d["l0v"] = f32(inputs["L0_attn_Wv"])
    d["l0o"] = f32(inputs["L0_attn_Wo"])
    d["l0bo"] = _cols(f32(inputs["L0_attn_bo"]), 8)
    d["wg"] = np.ascontiguousarray(
        f32(inputs["L0_ffn_Wg"]).reshape(8, 128, 8).transpose(1, 0, 2).reshape(128, 64))
    d["bg"] = f32(inputs["L0_ffn_bg"]).reshape(1, 8)
    d["w1r"] = np.ascontiguousarray(
        f32(inputs["L0_ffn_W1"]).reshape(8, 8, 128, 32, 128).transpose(0, 3, 1, 2, 4))
    d["w3r"] = np.ascontiguousarray(
        f32(inputs["L0_ffn_W3"]).reshape(8, 8, 128, 32, 128).transpose(0, 3, 1, 2, 4))
    d["w2r"] = np.ascontiguousarray(
        f32(inputs["L0_ffn_W2"]).reshape(8, 32, 128, 8, 128).transpose(0, 3, 1, 2, 4))
    d["l1q"] = _pair_blocks(f32(inputs["L1_attn_Wq"]).T)
    d["l1k"] = _pair_blocks(f32(inputs["L1_attn_Wk"]).T)
    d["l1v"] = np.ascontiguousarray(f32(inputs["L1_attn_Wv"]).T)
    d["l1o"] = np.ascontiguousarray(f32(inputs["L1_attn_Wo"]).T)
    d["l1bq"] = _cols(f32(inputs["L1_attn_bq"]), 8)
    d["l1bk"] = _cols(f32(inputs["L1_attn_bk"]), 8)
    d["l1bv"] = f32(inputs["L1_attn_bv"]).reshape(1, 1024)
    d["l1bo"] = _cols(f32(inputs["L1_attn_bo"]), 8)
    d["w1n"] = f32(inputs["L1_ffn_w1"])
    d["w1tr"] = np.ascontiguousarray(
        f32(inputs["L1_ffn_w1"]).T.reshape(8, 128, 32, 128).transpose(2, 0, 1, 3))
    d["w2n"] = f32(inputs["L1_ffn_w2"])
    d["w2tr"] = np.ascontiguousarray(
        f32(inputs["L1_ffn_w2"]).T.reshape(32, 128, 8, 128).transpose(2, 0, 1, 3))
    d["b1c"] = _cols(f32(inputs["L1_ffn_b1"]), 32)
    d["b2c"] = _cols(f32(inputs["L1_ffn_b2"]), 8)
    return d


_DRAM_SPECS = [
    ("l0q", [8, 8, 128, 128]), ("l0k", [8, 8, 128, 128]),
    ("l0v", [1024, 1024]), ("l0o", [1024, 1024]), ("l0bo", [128, 8]),
    ("wg", [128, 64]), ("bg", [1, 8]),
    ("w1r", [8, 32, 8, 128, 128]), ("w3r", [8, 32, 8, 128, 128]),
    ("w2r", [8, 8, 32, 128, 128]),
    ("l1q", [8, 8, 128, 128]), ("l1k", [8, 8, 128, 128]),
    ("l1v", [1024, 1024]), ("l1o", [1024, 1024]),
    ("l1bq", [128, 8]), ("l1bk", [128, 8]), ("l1bv", [1, 1024]), ("l1bo", [128, 8]),
    ("w1n", [4096, 1024]), ("w1tr", [32, 8, 128, 128]),
    ("w2n", [1024, 4096]), ("w2tr", [8, 32, 128, 128]),
    ("b1c", [128, 32]), ("b2c", [128, 8]),
]

A_V = 0
A_QT = 16384
A_KT = 18432
A_OT = 20480
A_P = 22528
A_P2 = 24576
A_P3 = 33792
A_PT = 26624
A_QKW = 27648
A_WOW = 31744
A_W13 = 32768
A_W2R = 36864
A_CWT = 40960

Z_R1 = 0
Z_R2 = 2048
Z_TMP = 4096
Z_CWB = 0
Z_BVR = 1024
Z_ALA = 2048
Z_ALB = 2048


def _build(upto="full"):
    nc = bass.Bass("TRN2", target_bir_lowering=False)
    em = Em()
    ar_16 = Arena()
    ar_32 = Arena()

    xT = nc.dram_tensor("xT", [D, T], F32, kind="ExternalInput")
    outT = nc.dram_tensor("outT", [D, T], F32, kind="ExternalOutput")
    dbgT = nc.dram_tensor("dbgT", [128, 64], F32, kind="ExternalOutput") if _DBG else None
    dr = {n: nc.dram_tensor(n, s, F32, kind="ExternalInput") for n, s in _DRAM_SPECS}

    ident_bf = nc.inline_tensor(np.eye(128, dtype=ml_dtypes.bfloat16), name="ident_bf")
    maskc = np.zeros((128, 128), np.float32)
    maskc[np.triu_indices(128, 1)] = NEG_BIG
    mask_f = nc.inline_tensor(maskc, name="mask_f")
    onerow_f = nc.inline_tensor(np.ones((1, 128), np.float32), name="onerow_f")
    onecol_f = nc.inline_tensor(np.ones((128, 1), np.float32), name="onecol_f")
    ohm = np.zeros((8, 8 * 128), ml_dtypes.bfloat16)
    for e_ in range(8):
        ohm[e_, e_ * 128:(e_ + 1) * 128] = 1
    oh_bf = nc.inline_tensor(ohm, name="oh_bf")

    res_t = [nc.alloc_sbuf_tensor(f"res{c}", [128, T], F32) for c in range(KC)]
    xn_t = [nc.alloc_sbuf_tensor(f"xn{c}", [128, T], BF16) for c in range(KC)]
    AR16 = nc.alloc_sbuf_tensor("AR16", [128, 43008], BF16)
    AR32 = nc.alloc_sbuf_tensor("AR32", [128, 5120], F32)
    cwf_t = nc.alloc_sbuf_tensor("cwf", [128, 128], F32)
    cwbf_t = nc.alloc_sbuf_tensor("cwbf", [128, 128], BF16)
    idb_t = nc.alloc_sbuf_tensor("idb", [128, 128], BF16)
    mask_t = nc.alloc_sbuf_tensor("maskt", [128, 128], F32)
    onerow_t = nc.alloc_sbuf_tensor("onerow", [1, 128], F32)
    onecol_t = nc.alloc_sbuf_tensor("onecol", [128, 1], F32)
    oh_t = nc.alloc_sbuf_tensor("oht", [8, 8 * 128], BF16)
    wg_t = nc.alloc_sbuf_tensor("wgt", [128, 64], BF16)
    bgb_t = nc.alloc_sbuf_tensor("bgb", [128, 8], F32)
    bgrow_t = nc.alloc_sbuf_tensor("bgrow", [1, 8], F32)
    sc_t = nc.alloc_sbuf_tensor("sc", [128, 48], F32)
    bias_t = {n: nc.alloc_sbuf_tensor("b_" + n, [128, 8], F32)
              for n in ("l0bo", "l1bq", "l1bk", "l1bo", "b2c")}
    b1c_t = nc.alloc_sbuf_tensor("b_b1c", [128, 32], F32)
    al1_t = nc.alloc_sbuf_tensor("al1", [128, 32], F32)
    al2_t = nc.alloc_sbuf_tensor("al2", [128, 8], F32)

    PS_A = nc.alloc_psum_tensor("PS_A", [128, 2048], F32)
    PS_B = nc.alloc_psum_tensor("PS_B", [128, 512], F32)
    PS_C = nc.alloc_psum_tensor("PS_C", [128, 512], F32)
    PT_T = [nc.alloc_psum_tensor("PT_A", [128, 512], BF16),
            nc.alloc_psum_tensor("PT_B", [128, 512], BF16)]

    res = [Buf(f"res{c}", 4) for c in range(KC)]
    xn = [Buf(f"xn{c}", 4) for c in range(KC)]
    psA = Buf("psA", 4)
    psB = Buf("psB", 1)
    psC = Buf("psC", 1)
    ptb = [Buf("ptA", 4), Buf("ptB", 4)]
    consts = Buf("consts", 6)
    C_ID, C_MASK, C_ROW, C_COL, C_OH, C_WG = range(6)
    biasb = Buf("biasb", 6)
    B_IDX = {"l0bo": 0, "l1bq": 1, "l1bk": 2, "l1bo": 3, "b2c": 4, "b1c": 5}
    scb = Buf("scb", 16)
    tmpf = Buf("tmpf", 2)
    cwfb = Buf("cwf", 16)
    cwbfb = Buf("cwbf", 1)
    al1b = Buf("al1", 1)
    al2b = Buf("al2", 1)
    bgrb = Buf("bgrow", 1)
    bgbb = Buf("bgb", 1)
    wgb = Buf("wgb", 1)

    S_GL, S_E8, S_MS, S_CW = 0, 8, 16, 24
    S_MC = 32
    S_MX, S_NM, S_L, S_RL = 36, 37, 38, 39
    S_G1, S_G2, S_G3, S_G4 = 40, 41, 42, 43
    S_A2 = 44

    def a16(off, ln, p0=0, p1=128):
        return AR16[p0:p1, off:off + ln]

    def a32(off, ln, p0=0, p1=128):
        return AR32[p0:p1, off:off + ln]

    E_ = em.emit
    G_ = em.emit_group
    PSBC = [(PS_B, psB), (PS_C, psC)]

    E_("y", lambda y: y.dma_start(idb_t[:, :], ident_bf[:, :]), wr=[(consts, C_ID)], dma="cst")
    E_("y", lambda y: y.dma_start(mask_t[:, :], mask_f[:, :]), wr=[(consts, C_MASK)], dma="cst")
    E_("y", lambda y: y.dma_start(onerow_t[:, :], onerow_f[:, :]), wr=[(consts, C_ROW)], dma="cst")
    E_("y", lambda y: y.dma_start(onecol_t[:, :], onecol_f[:, :]), wr=[(consts, C_COL)], dma="cst")
    E_("y", lambda y: y.dma_start(oh_t[:, :], oh_bf[:, :]), wr=[(consts, C_OH)], dma="cst")
    E_("q", lambda q: q.dma_start(wg_t[:, :], dr["wg"][:, :]), wr=[(wgb, 0)], dma="cwg")
    for n in ("l0bo", "l1bq", "l1bk", "l1bo", "b2c"):
        E_("y", lambda y, n=n: y.dma_start(bias_t[n][:, :], dr[n][:, :]),
           wr=[(biasb, B_IDX[n])], dma="cst")
    E_("y", lambda y: y.dma_start(b1c_t[:, :], dr["b1c"][:, :]), wr=[(biasb, B_IDX["b1c"])], dma="cst")
    E_("y", lambda y: y.dma_start(bgrow_t[:, :], dr["bg"][:, :]), wr=[(bgrb, 0)], dma="cst")
    em.bulk_fixup([consts, biasb, bgrb], (em.dma_chan("cst"), em.dma_chan("cst").n))
    for c in range(KC):
        E_("y", lambda y, c=c: y.dma_start(res_t[c][:, :], xT[c * 128:(c + 1) * 128, :]),
           wr=[(res[c], p) for p in range(4)], dma="xin")
    em.bulk_fixup(res, (em.dma_chan("xin"), em.dma_chan("xin").n))
    E_("t", lambda t: t.matmul(PS_B[:, 0:8], onerow_t[:, :], bgrow_t[:, :],
                               start=True, stop=True),
       rd=[(consts, C_ROW), (bgrb, 0)], wr=[(psB, 0)])
    E_("v", lambda v: v.tensor_copy(bgb_t[:, :], PS_B[:, 0:8]),
       rd=[(psB, 0)], wr=[(bgbb, 0)])

    def phase_ln(L):
        r1 = ar_32.claim(Z_R1, Z_R1 + 2048, f"r1L{L}")
        r2 = ar_32.claim(Z_R2, Z_R2 + 2048, f"r2L{L}")
        for g in range(NTG4):
            for c in range(KC):
                E_("t", lambda t, c=c, g=g: t.matmul(
                    PS_A[0:1, g * 512:(g + 1) * 512], onecol_t[:, :],
                    res_t[c][:, g * 512:(g + 1) * 512],
                    start=(c == 0), stop=(c == KC - 1)),
                   rd=[(consts, C_COL), (res[c], g)],
                   wr=[(psA, g)] if c in (0, KC - 1) else [])
        for g in range(NTG4):
            E_("v", lambda v, g=g: v.tensor_scalar(
                a32(Z_R1 + g * 512, 512, 0, 1), PS_A[0:1, g * 512:(g + 1) * 512],
                1.0 / D, None, OP.mult), rd=[(psA, g)], wr=[(r1, 0)])
        for g in range(NTG4):
            for c in range(KC):
                sl = c % 2
                E_("s", lambda s, c=c, g=g, sl=sl: s.square(
                    a32(Z_TMP + sl * 512, 512), res_t[c][:, g * 512:(g + 1) * 512]),
                    rd=[(res[c], g)], wr=[(tmpf, sl)])
                E_("t", lambda t, c=c, g=g, sl=sl: t.matmul(
                    PS_A[32:33, g * 512:(g + 1) * 512], onecol_t[:, :],
                    a32(Z_TMP + sl * 512, 512),
                    start=(c == 0), stop=(c == KC - 1)),
                   rd=[(consts, C_COL), (tmpf, sl)],
                   wr=[(psA, g)] if c in (0, KC - 1) else [])
        for g in range(NTG4):
            E_("v", lambda v, g=g: v.tensor_scalar(
                a32(Z_R2 + g * 512, 512, 0, 1), PS_A[32:33, g * 512:(g + 1) * 512],
                1.0 / D, 1e-5, OP.mult, OP.add), rd=[(psA, g)], wr=[(r2, 0)])
        for hh in range(2):
            E_("v", lambda v, hh=hh: v.tensor_mul(
                a32(Z_TMP, 1024, 0, 1), a32(Z_R1 + hh * 1024, 1024, 0, 1),
                a32(Z_R1 + hh * 1024, 1024, 0, 1)),
               rd=[(r1, 0)], wr=[(tmpf, 0), (tmpf, 1)])
            E_("v", lambda v, hh=hh: v.tensor_sub(
                a32(Z_R2 + hh * 1024, 1024, 0, 1), a32(Z_R2 + hh * 1024, 1024, 0, 1),
                a32(Z_TMP, 1024, 0, 1)),
               rd=[(r2, 0), (tmpf, 0), (tmpf, 1)], wr=[(r2, 0)])
        E_("s", lambda s: s.activation(a32(Z_R2, T, 0, 1), a32(Z_R2, T, 0, 1),
                                       AF.Sqrt), rd=[(r2, 0)], wr=[(r2, 0)])
        E_("v", lambda v: v.reciprocal(a32(Z_R2, T, 0, 1), a32(Z_R2, T, 0, 1)),
           rd=[(r2, 0)], wr=[(r2, 0)])
        E_("v", lambda v: v.scalar_tensor_tensor(
            a32(Z_R1, T, 0, 1), a32(Z_R1, T, 0, 1), -1.0, a32(Z_R2, T, 0, 1),
            OP.mult, OP.mult), rd=[(r1, 0), (r2, 0)], wr=[(r1, 0)])
        for g in range(NTG4):
            nbt, nb = PSBC[g % 2]
            E_("t", lambda t, g=g: t.matmul(
                PS_A[:, g * 512:(g + 1) * 512], onerow_t[:, :],
                a32(Z_R2 + g * 512, 512, 0, 1), start=True, stop=True),
               rd=[(consts, C_ROW), (r2, 0)], wr=[(psA, g)])
            E_("t", lambda t, g=g, nbt=nbt: t.matmul(
                nbt[:, :], onerow_t[:, :], a32(Z_R1 + g * 512, 512, 0, 1),
                start=True, stop=True),
               rd=[(consts, C_ROW), (r1, 0)], wr=[(nb, 0)])
            for c in range(KC):
                sl = c % 2
                E_("v", lambda v, c=c, g=g, sl=sl: v.tensor_mul(
                    a32(Z_TMP + sl * 512, 512), res_t[c][:, g * 512:(g + 1) * 512],
                    PS_A[:, g * 512:(g + 1) * 512]),
                    rd=[(res[c], g), (psA, g)], wr=[(tmpf, sl)])
                E_("v", lambda v, c=c, g=g, sl=sl, nbt=nbt: v.tensor_add(
                    xn_t[c][:, g * 512:(g + 1) * 512], a32(Z_TMP + sl * 512, 512),
                    nbt[:, :]), rd=[(tmpf, sl), (nb, 0)], wr=[(xn[c], g)])

    def phase_gate():
        cwtb = ar_16.claim(A_CWT, A_CWT + 2048, "cwT", 16)
        for tt in range(TT):
            pst, ppp = PSBC[tt % 2]
            mem = []
            for c in range(KC):
                def f(t, c=c, tt=tt, pst=pst):
                    return t.matmul(pst[:, 0:8], xn_t[c][:, tt * 128:(tt + 1) * 128],
                                    wg_t[:, c * 8:(c + 1) * 8],
                                    start=(c == 0), stop=(c == KC - 1))
                mem.append((f, [(xn[c], tt // 4), (wgb, 0)]))
            G_("t", mem, wr=[(ppp, 0)])
            E_("v", lambda v, pst=pst: v.tensor_add(
                sc_t[:, S_GL:S_GL + 8], bgb_t[:, :], pst[:, 0:8]),
                rd=[(ppp, 0), (bgbb, 0)], wr=[(scb, 0)])
            E_("v", lambda v: v.reduce_max(sc_t[:, S_G1:S_G1 + 1],
                                           sc_t[:, S_GL:S_GL + 8], axis=AX, negate=True),
               rd=[(scb, 0)], wr=[(scb, 4)])
            E_("s", lambda s: s.activation(sc_t[:, S_E8:S_E8 + 8],
                                           sc_t[:, S_GL:S_GL + 8], AF.Exp,
                                           bias=sc_t[:, S_G1:S_G1 + 1]),
               rd=[(scb, 0), (scb, 4)], wr=[(scb, 1)])
            E_("v", lambda v: v.reduce_max(sc_t[:, S_G2:S_G2 + 1],
                                           sc_t[:, S_E8:S_E8 + 8], axis=AX),
               rd=[(scb, 1)], wr=[(scb, 5)])
            E_("v", lambda v: v.tensor_scalar(
                sc_t[:, S_MS:S_MS + 8], sc_t[:, S_E8:S_E8 + 8],
                sc_t[:, S_G2:S_G2 + 1], NEG_BIG, OP.is_ge, OP.mult),
               rd=[(scb, 1), (scb, 5)], wr=[(scb, 2)])
            E_("v", lambda v: v.tensor_add(sc_t[:, S_MS:S_MS + 8],
                                           sc_t[:, S_MS:S_MS + 8],
                                           sc_t[:, S_E8:S_E8 + 8]),
               rd=[(scb, 2), (scb, 1)], wr=[(scb, 2)])
            E_("v", lambda v: v.reduce_max(sc_t[:, S_G3:S_G3 + 1],
                                           sc_t[:, S_MS:S_MS + 8], axis=AX),
               rd=[(scb, 2)], wr=[(scb, 6)])
            E_("v", lambda v: v.tensor_scalar(
                sc_t[:, S_CW:S_CW + 8], sc_t[:, S_E8:S_E8 + 8],
                sc_t[:, S_G3:S_G3 + 1], None, OP.is_ge),
               rd=[(scb, 1), (scb, 6)], wr=[(scb, 3)])
            E_("v", lambda v: v.tensor_mul(sc_t[:, S_CW:S_CW + 8],
                                           sc_t[:, S_CW:S_CW + 8],
                                           sc_t[:, S_E8:S_E8 + 8]),
               rd=[(scb, 3), (scb, 1)], wr=[(scb, 3)])
            E_("v", lambda v: v.reduce_sum(sc_t[:, S_G4:S_G4 + 1],
                                           sc_t[:, S_CW:S_CW + 8], axis=AX),
               rd=[(scb, 3)], wr=[(scb, 7)])
            E_("v", lambda v: v.reciprocal(sc_t[:, S_G4:S_G4 + 1],
                                           sc_t[:, S_G4:S_G4 + 1]),
               rd=[(scb, 7)], wr=[(scb, 7)])
            E_("v", lambda v, tt=tt: v.tensor_scalar(
                cwf_t[:, tt * 8:(tt + 1) * 8], sc_t[:, S_CW:S_CW + 8],
                sc_t[:, S_G4:S_G4 + 1], None, OP.mult),
               rd=[(scb, 3), (scb, 7)], wr=[(cwfb, tt)])
        E_("v", lambda v: v.tensor_copy(cwbf_t[:, :], cwf_t[:, :]),
           rd=[(cwfb, i) for i in range(TT)], wr=[(cwbfb, 0)])
        for tt in range(TT):
            k = tt % 2
            E_("t", lambda t, tt=tt, k=k: t.transpose(
                PT_T[k][0:8, 0:128], cwbf_t[:, tt * 8:(tt + 1) * 8], idb_t[:, :]),
               rd=[(cwbfb, 0), (consts, C_ID)], wr=[(ptb[k], 0)])
            E_("v", lambda v, tt=tt, k=k: v.tensor_copy(
                a16(A_CWT + tt * 128, 128, 0, 8), PT_T[k][0:8, 0:128]),
               rd=[(ptb[k], 0)], wr=[(cwtb, tt)])
        return cwtb

    def phase_attn(L):
        slab_off = [A_QKW, A_QKW + 1024, A_QKW + 2048, A_QKW + 3072,
                    A_WOW, A_WOW + 1024, A_P, A_P + 1024]
        wvs = [ar_16.claim(slab_off[c], slab_off[c] + 1024, f"wv{L}_{c}")
               for c in range(KC)]
        vbufs = [ar_16.claim(A_V + tt * 1024, A_V + (tt + 1) * 1024, f"V{L}_{tt}")
                 for tt in range(TT)]
        wv_d = dr["l0v"] if L == 0 else dr["l1v"]
        if L == 1:
            bvrb = ar_32.claim(Z_BVR, Z_BVR + 1024, f"bvr{L}")
            bvbb = ar_32.claim(Z_CWB, Z_CWB + 1024, f"bvb{L}")
            E_("y", lambda y: y.dma_start(a32(Z_BVR, 1024, 0, 1), dr["l1bv"][:, :]),
               wr=[(bvrb, 0)], dma="bvr")
            for half in range(2):
                pst, ppp = PSBC[half]
                E_("t", lambda t, half=half, pst=pst: t.matmul(
                    pst[:, :], onerow_t[:, :], a32(Z_BVR + half * 512, 512, 0, 1),
                    start=True, stop=True),
                   rd=[(consts, C_ROW), (bvrb, 0)], wr=[(ppp, 0)])
                E_("v", lambda v, half=half, pst=pst: v.tensor_copy(
                    a32(Z_CWB + half * 512, 512), pst[:, :]),
                   rd=[(ppp, 0)], wr=[(bvbb, 0)])
        for c in range(KC):
            E_("q", lambda q, c=c: q.dma_start(
                a16(slab_off[c], 1024), wv_d[c * 128:(c + 1) * 128, :]),
                wr=[(wvs[c], 0)], dma="wv")
        em.bulk_fixup(wvs, (em.dma_chan("wv"), em.dma_chan("wv").n))
        for tt in range(TT):
            half = tt % 2
            mem = []
            for c in range(KC):
                for dv in range(2):
                    def f(t, tt=tt, c=c, half=half, dv=dv):
                        return t.matmul(
                            PS_A[:, half * 1024 + dv * 512:half * 1024 + dv * 512 + 512],
                            xn_t[c][:, tt * 128:(tt + 1) * 128],
                            a16(slab_off[c] + dv * 512, 512),
                            start=(c == 0), stop=(c == KC - 1))
                    mem.append((f, [(xn[c], tt // 4), (wvs[c], 0)]))
            G_("t", mem, wr=[(psA, 2 * half), (psA, 2 * half + 1)])
            if L == 0:
                E_("v", lambda v, tt=tt, half=half: v.tensor_copy(
                    a16(A_V + tt * 1024, 1024), PS_A[:, half * 1024:half * 1024 + 1024]),
                   rd=[(psA, 2 * half), (psA, 2 * half + 1)], wr=[(vbufs[tt], 0)])
            else:
                E_("v", lambda v, tt=tt, half=half: v.tensor_add(
                    a16(A_V + tt * 1024, 1024), a32(Z_CWB, 1024),
                    PS_A[:, half * 1024:half * 1024 + 1024]),
                   rd=[(psA, 2 * half), (psA, 2 * half + 1), (bvbb, 0)],
                   wr=[(vbufs[tt], 0)])

        qtp = ar_16.claim(A_QT, A_QT + 2048, f"qt{L}", 4)
        ktp = ar_16.claim(A_KT, A_KT + 2048, f"kt{L}", 4)
        otp = ar_16.claim(A_OT, A_OT + 2048, f"ot{L}", 16)
        pb = [ar_16.claim(A_P, A_P + 2048, f"P{L}a", 16),
              ar_16.claim(A_P2, A_P2 + 2048, f"P{L}b", 16),
              ar_16.claim(A_P3, A_P3 + 2048, f"P{L}c", 16)]
        PO = [A_P, A_P2, A_P3]
        pts = [ar_16.claim(A_PT + i * 512, A_PT + (i + 1) * 512, f"pts{L}{i}")
               for i in range(2)]
        wqs = [ar_16.claim(A_QKW + i * 2048, A_QKW + i * 2048 + 1024, f"wq{L}{i}")
               for i in range(2)]
        wks = [ar_16.claim(A_QKW + i * 2048 + 1024, A_QKW + (i + 1) * 2048, f"wk{L}{i}")
               for i in range(2)]
        wos = [ar_16.claim(A_WOW + i * 1024, A_WOW + (i + 1) * 1024, f"wo{L}{i}")
               for i in range(2)]
        mcolb = Buf("mcol", 1)
        mxb = Buf("mx", 1)
        nmb = Buf("nm", 1)
        lb = Buf("l", 1)
        rlb = Buf("rl", 1)

        wq_d = dr["l0q"] if L == 0 else dr["l1q"]
        wk_d = dr["l0k"] if L == 0 else dr["l1k"]
        wo_d = dr["l0o"] if L == 0 else dr["l1o"]
        bo_t = bias_t["l0bo"] if L == 0 else bias_t["l1bo"]

        def ld_qk(pair, sl):
            ch = "wq" + str(sl)
            E_("q", lambda q, pair=pair, sl=sl: q.dma_start(
                a16(A_QKW + sl * 2048, 1024).rearrange("p (k n) -> p k n", n=128),
                bass.AP(wq_d, pair * (8 * 128 * 128),
                        [[128, 128], [128 * 128, 8], [1, 128]])),
                wr=[(wqs[sl], 0)], dma=ch)
            tok = E_("q", lambda q, pair=pair, sl=sl: q.dma_start(
                a16(A_QKW + sl * 2048 + 1024, 1024).rearrange("p (k n) -> p k n", n=128),
                bass.AP(wk_d, pair * (8 * 128 * 128),
                        [[128, 128], [128 * 128, 8], [1, 128]])),
                wr=[(wks[sl], 0)], dma=ch)
            em.bulk_fixup([wqs[sl], wks[sl]], tok)

        def ld_wo(pair, sl):
            E_("q", lambda q, pair=pair, sl=sl: q.dma_start(
                a16(A_WOW + sl * 1024, 1024), wo_d[pair * 128:(pair + 1) * 128, :]),
                wr=[(wos[sl], 0)], dma="wo" + str(sl))

        ld_qk(0, 0)
        ld_wo(0, 0)

        for pair in range(NP):
            sl = pair % 2
            if pair + 1 < NP:
                ld_qk(pair + 1, 1 - sl)
                ld_wo(pair + 1, 1 - sl)
            for which, wbuf, dstoff, dstb, bcol in (
                    (0, wqs[sl], A_QT, qtp, "l1bq"), (1, wks[sl], A_KT, ktp, "l1bk")):
                for tg in range(NTG4):
                    pst, ppp = PSBC[tg % 2]
                    mem = []
                    for c in range(KC):
                        def f(t, which=which, sl=sl, c=c, tg=tg, pst=pst):
                            return t.matmul(
                                pst[:, :],
                                a16(A_QKW + sl * 2048 + which * 1024 + c * 128, 128),
                                xn_t[c][:, tg * 512:(tg + 1) * 512],
                                start=(c == 0), stop=(c == KC - 1))
                        mem.append((f, [(wbuf, 0), (xn[c], tg)]))
                    G_("t", mem, wr=[(ppp, 0)])
                    if L == 0:
                        E_("v", lambda v, dstoff=dstoff, tg=tg, pst=pst: v.tensor_copy(
                            a16(dstoff + tg * 512, 512), pst[:, :]),
                           rd=[(ppp, 0)], wr=[(dstb, tg)])
                    else:
                        E_("v", lambda v, dstoff=dstoff, tg=tg, pst=pst, bcol=bcol,
                           pair=pair: v.tensor_scalar(
                               a16(dstoff + tg * 512, 512), pst[:, :],
                               bias_t[bcol][:, pair:pair + 1], None, OP.add),
                           rd=[(ppp, 0), (biasb, B_IDX[bcol])], wr=[(dstb, tg)])

            def emit_S(qi, h, soff, ps):
                hp = h & 1
                nk = (qi + 1) * 128
                nkb = (nk + 511) // 512
                for kb in range(nkb):
                    n = min(512, nk - kb * 512)
                    E_("t", lambda t, qi=qi, hp=hp, kb=kb, n=n, soff=soff: t.matmul(
                        PS_A[:, soff + kb * 512:soff + kb * 512 + n],
                        a16(A_QT + qi * 128, 128, 64 * hp, 64 * hp + 64),
                        a16(A_KT + kb * 512, n, 64 * hp, 64 * hp + 64),
                        start=True, stop=True),
                       rd=[(qtp, qi // 4), (ktp, kb)], wr=[(psA, soff // 512 + kb)])
                dkb = (soff + qi * 128) // 512
                E_("v", lambda v, qi=qi, soff=soff: v.tensor_add(
                    PS_A[:, soff + qi * 128:soff + (qi + 1) * 128],
                    PS_A[:, soff + qi * 128:soff + (qi + 1) * 128],
                    mask_t[:, :]), rd=[(consts, C_MASK)], wr=[(psA, dkb)])
                E_("s", lambda s, nk=nk, ps=ps, soff=soff: s.activation(
                    a16(PO[ps], nk), PS_A[:, soff:soff + nk], AF.Exp,
                    scale=SCALE, accum_out=sc_t[:, S_L:S_L + 1]),
                   rd=[(psA, soff // 512 + kb) for kb in range(nkb)],
                   wr=[(pb[ps], k) for k in range(qi + 1)] + [(lb, 0)])
                E_("v", lambda v: v.reciprocal(sc_t[:, S_RL:S_RL + 1],
                                               sc_t[:, S_L:S_L + 1]),
                   rd=[(lb, 0)], wr=[(rlb, 0)])
                E_("v", lambda v, nk=nk, ps=ps: v.tensor_scalar(
                    a16(PO[ps], nk), a16(PO[ps], nk), sc_t[:, S_RL:S_RL + 1], None,
                    OP.mult),
                   rd=[(rlb, 0)] + [(pb[ps], k) for k in range(qi + 1)],
                   wr=[(pb[ps], k) for k in range(qi + 1)])
                if _DBG.get("dumpP") == (pair, qi, h):
                    E_("q", lambda q, nk=nk, hp=hp: q.dma_start(
                        outT[0:128, 0:nk], a16(PO[hp], nk)),
                       rd=[(pb[hp], k) for k in range(qi + 1)], dma="dbg")
                    E_("q", lambda q: q.dma_start(
                        outT[128:256, 0:1024], a16(A_V + 8 * 1024, 1024)),
                       rd=[(vbufs[8], 0)], dma="dbg")
                    E_("q", lambda q: q.dma_start(
                        outT[256:384, 0:1024], a16(A_KT, 1024, 0, 128)),
                       rd=[(ktp, 0), (ktp, 1)], dma="dbg")
                    E_("q", lambda q: q.dma_start(
                        outT[384:512, 0:1024], a16(A_KT + 1024, 1024, 0, 128)),
                       rd=[(ktp, 2), (ktp, 3)], dma="dbg")
                    E_("q", lambda q, qi=qi: q.dma_start(
                        outT[512:640, 0:128], a16(A_QT + qi * 128, 128)),
                       rd=[(qtp, qi // 4)], dma="dbg")

            def emit_TPV(qi, h, ps):
                hp = h & 1
                otps, otpb2 = PSBC[qi % 2]
                nkc = qi + 1
                batches = [(b4 * 4, min(nkc, b4 * 4 + 4)) for b4 in range((nkc + 3) // 4)]

                def tp_batch(j):
                    lo, hi = batches[j]
                    k = j % 2
                    for kc in range(lo, hi):
                        E_("t", lambda t, kc=kc, k=k, ps=ps, lo=lo: t.transpose(
                            PT_T[k][:, (kc - lo) * 128:(kc - lo) * 128 + 128],
                            a16(PO[ps] + kc * 128, 128), idb_t[:, :]),
                           rd=[(pb[ps], kc), (consts, C_ID)],
                           wr=[(ptb[k], kc - lo)], inc=(kc == hi - 1))
                    n = (hi - lo) * 128
                    if k == 0:
                        E_("v", lambda v, k=k, n=n: v.tensor_copy(
                            a16(A_PT + k * 512, n), PT_T[k][:, 0:n]),
                           rd=[(ptb[k], i) for i in range(hi - lo)], wr=[(pts[k], 0)])
                    else:
                        E_("s", lambda s, k=k, n=n: s.copy(
                            a16(A_PT + k * 512, n), PT_T[k][:, 0:n]),
                           rd=[(ptb[k], i) for i in range(hi - lo)], wr=[(pts[k], 0)])

                def pv_batch(j):
                    lo, hi = batches[j]
                    k = j % 2
                    for kc in range(lo, hi):
                        E_("t", lambda t, kc=kc, k=k, hp=hp, h=h, qi=qi, otps=otps,
                           lo=lo: t.matmul(
                               otps[64 * hp:64 * hp + 64, 0:128],
                               a16(A_V + kc * 1024 + h * 64, 64),
                               a16(A_PT + k * 512 + (kc - lo) * 128, 128),
                               start=(kc == 0), stop=(kc == qi),
                               tile_position=(0, 64 * hp)),
                           rd=[(pts[k], 0), (vbufs[kc], 0)],
                           wr=[(otpb2, 0)] if kc == qi else [], inc=(kc == qi))

                for j in range(len(batches)):
                    tp_batch(j)
                    if j > 0:
                        pv_batch(j - 1)
                pv_batch(len(batches) - 1)
                if hp == 1:
                    E_("v", lambda v, qi=qi, otps=otps: v.tensor_copy(
                        a16(A_OT + qi * 128, 128), otps[:, 0:128]),
                       rd=[(otpb2, 0)], wr=[(otp, qi)])

            pend = []
            idx = 0
            for qi in range(_DBG.get('maxqi', TT)):
                for h in (2 * pair, 2 * pair + 1):
                    soff = (idx % 2) * 1024 if qi < 8 else 0
                    ps = idx % 3
                    emit_S(qi, h, soff, ps)
                    pend.append((qi, h, ps))
                    idx += 1
                    want = 2 if qi < 8 else 1
                    while len(pend) > want:
                        emit_TPV(*pend.pop(0))
            while pend:
                emit_TPV(*pend.pop(0))

            for dc in range(KC):
                for tg in range(NTG4):
                    pst, ppp = PSBC[(dc * NTG4 + tg) % 2]
                    E_("t", lambda t, sl=sl, dc=dc, tg=tg, pst=pst: t.matmul(
                        pst[:, :], a16(A_WOW + sl * 1024 + dc * 128, 128),
                        a16(A_OT + tg * 512, 512), start=True, stop=True),
                       rd=[(wos[sl], 0)] + [(otp, q) for q in range(tg * 4, tg * 4 + 4)],
                       wr=[(ppp, 0)])
                    if pair == 0:
                        E_("v", lambda v, dc=dc, tg=tg, pst=pst, bo_t=bo_t:
                           v.scalar_tensor_tensor(
                               res_t[dc][:, tg * 512:(tg + 1) * 512], pst[:, :],
                               bo_t[:, dc:dc + 1],
                               res_t[dc][:, tg * 512:(tg + 1) * 512], OP.add, OP.add),
                           rd=[(ppp, 0), (biasb, B_IDX['l0bo'] if L == 0 else B_IDX['l1bo']), (res[dc], tg)], wr=[(res[dc], tg)])
                    else:
                        E_("v", lambda v, dc=dc, tg=tg, pst=pst: v.tensor_add(
                            res_t[dc][:, tg * 512:(tg + 1) * 512],
                            res_t[dc][:, tg * 512:(tg + 1) * 512], pst[:, :]),
                           rd=[(ppp, 0), (res[dc], tg)], wr=[(res[dc], tg)])

    def phase_moe(cwtb):
        hp = [ar_16.claim(A_V + f_ * 1024, A_V + (f_ + 1) * 1024, f"hp{f_}", 2)
              for f_ in range(FC)]
        w13 = [ar_16.claim(A_W13 + i * 2048, A_W13 + (i + 1) * 2048, f"w13_{i}")
               for i in range(2)]
        w2h = [ar_16.claim(A_W2R + i * 1024, A_W2R + (i + 1) * 1024, f"w2h{i}")
               for i in range(4)]
        cwbb = ar_32.claim(Z_CWB, Z_CWB + 1024, "cwb", 2)
        W13O = [A_W13, A_W13 + 2048]
        W2O = [A_W2R, A_W2R + 2048]

        def ld_w13(e, fc, sl):
            ch = "w13" + str(sl)
            E_("q", lambda q, e=e, fc=fc, sl=sl: q.dma_start(
                a16(W13O[sl], 1024).rearrange("p (k n) -> p k n", n=128),
                bass.AP(dr["w1r"], (e * 32 + fc) * 8 * 128 * 128,
                        [[128, 128], [128 * 128, 8], [1, 128]])),
                wr=[(w13[sl], 0)], dma=ch)
            tok = E_("q", lambda q, e=e, fc=fc, sl=sl: q.dma_start(
                a16(W13O[sl] + 1024, 1024).rearrange("p (k n) -> p k n", n=128),
                bass.AP(dr["w3r"], (e * 32 + fc) * 8 * 128 * 128,
                        [[128, 128], [128 * 128, 8], [1, 128]])),
                wr=[(w13[sl], 0)], dma=ch)
            em.bulk_fixup([w13[sl]], tok)

        def ld_w2q(e, dc, qq):
            sl = qq % 4
            E_("q", lambda q, e=e, dc=dc, qq=qq, sl=sl: q.dma_start(
                a16(A_W2R + sl * 1024, 1024).rearrange("p (k n) -> p k n", n=128),
                bass.AP(dr["w2r"], ((e * 8 + dc) * 32 + qq * 8) * 128 * 128,
                        [[128, 128], [128 * 128, 8], [1, 128]])),
                wr=[(w2h[sl], 0)], dma="w2" + str(sl))

        ld_w13(0, 0, 0)
        step = 0
        for e in range(E):
            for tgi in range(NTG):
                t0 = tgi * 1024
                for half in range(2):
                    pst, ppp = PSBC[half]
                    E_("t", lambda t, e=e, t0=t0, half=half, pst=pst: t.matmul(
                        pst[:, :], oh_t[:, e * 128:(e + 1) * 128],
                        a16(A_CWT + t0 + half * 512, 512, 0, 8),
                        start=True, stop=True),
                       rd=[(consts, C_OH)] + [(cwtb, (t0 + half * 512) // 128 + i)
                                           for i in range(4)],
                       wr=[(ppp, 0)])
                    E_("v", lambda v, half=half, pst=pst: v.tensor_copy(
                        a32(Z_CWB + half * 512, 512), pst[:, :]),
                       rd=[(ppp, 0)], wr=[(cwbb, half)])
                ROT = [(PS_B[:, :], psB, 0), (PS_C[:, :], psC, 0),
                       (PS_A[:, 0:512], psA, 0), (PS_A[:, 512:1024], psA, 1)]
                rot = 0
                for fc in range(FC):
                    sl = step % 2
                    step += 1
                    nxt = step
                    if nxt < E * NTG * FC:
                        ne, rem = divmod(nxt, NTG * FC)
                        _ntg, nfc = divmod(rem, FC)
                        ld_w13(ne, nfc, 1 - sl)
                    if fc >= FC - 4:
                        ld_w2q(e, 0, fc - (FC - 4))
                    for stg in range(2):
                        pstg, pppg, ppg = ROT[rot % 4]
                        pstu, pppu, ppu = ROT[(rot + 1) % 4]
                        rot += 2
                        memg, memu = [], []
                        for c in range(KC):
                            def fg(t, sl=sl, c=c, t0=t0, stg=stg, pstg=pstg):
                                return t.matmul(
                                    pstg, a16(W13O[sl] + c * 128, 128),
                                    xn_t[c][:, t0 + stg * 512:t0 + stg * 512 + 512],
                                    start=(c == 0), stop=(c == KC - 1))
                            memg.append((fg, [(w13[sl], 0),
                                              (xn[c], (t0 + stg * 512) // 512)]))
                            def fu(t, sl=sl, c=c, t0=t0, stg=stg, pstu=pstu):
                                return t.matmul(
                                    pstu, a16(W13O[sl] + 1024 + c * 128, 128),
                                    xn_t[c][:, t0 + stg * 512:t0 + stg * 512 + 512],
                                    start=(c == 0), stop=(c == KC - 1))
                            memu.append((fu, [(w13[sl], 0),
                                              (xn[c], (t0 + stg * 512) // 512)]))
                        G_("t", memg, wr=[(pppg, ppg)])
                        E_("s", lambda s, fc=fc, stg=stg, pstg=pstg: s.activation(
                            a16(A_V + fc * 1024 + stg * 512, 512), pstg, AF.Silu),
                           rd=[(pppg, ppg)], wr=[(hp[fc], stg)])
                        G_("t", memu, wr=[(pppu, ppu)])
                        E_("v", lambda v, fc=fc, stg=stg, pstu=pstu: v.tensor_mul(
                            a16(A_V + fc * 1024 + stg * 512, 512),
                            a16(A_V + fc * 1024 + stg * 512, 512), pstu),
                           rd=[(pppu, ppu), (hp[fc], stg)], wr=[(hp[fc], stg)])
                for dc in range(KC):
                    for stg in range(2):
                        pidx = (dc * 2 + stg) % 4
                        for qq in range(4):
                            sl2 = qq % 4
                            mem = []
                            for j in range(8):
                                fc = qq * 8 + j
                                def fd(t, sl2=sl2, j=j, fc=fc, stg=stg, pidx=pidx,
                                       first=(qq == 0 and j == 0),
                                       last=(qq == 3 and j == 7)):
                                    return t.matmul(
                                        PS_A[:, pidx * 512:pidx * 512 + 512],
                                        a16(A_W2R + sl2 * 1024 + j * 128, 128),
                                        a16(A_V + fc * 1024 + stg * 512, 512),
                                        start=first, stop=last)
                                mem.append((fd, [(w2h[sl2], 0), (hp[fc], stg)]))
                            G_("t", mem, wr=[(psA, pidx)])
                            if stg == 1 and dc + 1 < KC:
                                ld_w2q(e, dc + 1, qq)
                        g4 = (t0 + stg * 512) // 512
                        tsl = (dc * 2 + stg) % 2
                        E_("v", lambda v, pidx=pidx, stg=stg, tsl=tsl: v.tensor_mul(
                            a32(Z_TMP + tsl * 512, 512),
                            PS_A[:, pidx * 512:pidx * 512 + 512],
                            a32(Z_CWB + stg * 512, 512)),
                           rd=[(psA, pidx), (cwbb, stg)], wr=[(tmpf, tsl)])
                        E_("v", lambda v, dc=dc, g4=g4, tsl=tsl: v.tensor_add(
                            res_t[dc][:, g4 * 512:(g4 + 1) * 512],
                            res_t[dc][:, g4 * 512:(g4 + 1) * 512],
                            a32(Z_TMP + tsl * 512, 512)),
                           rd=[(tmpf, tsl), (res[dc], g4)], wr=[(res[dc], g4)])

    def phase_alpha():
        slA = ar_32.claim(Z_ALA, Z_ALA + 1024, "slA")
        slB = ar_32.claim(Z_ALB, Z_ALB + 2048, "slB")
        tcol = Buf("tcol", 1)
        for i in range(IC):
            E_("y", lambda y, i=i: y.dma_start(
                a32(Z_ALA, 1024), dr["w1n"][i * 128:(i + 1) * 128, :]),
                wr=[(slA, 0)], dma="alA")
            E_("v", lambda v, i=i: v.reduce_sum(
                al1_t[:, i:i + 1], a32(Z_ALA, 1024), axis=AX,
                apply_absolute_value=True), rd=[(slA, 0)], wr=[(al1b, 0)])
        E_("v", lambda v: v.tensor_scalar(al1_t[:, :], al1_t[:, :], 1.0 / D, None,
                                          OP.mult), rd=[(al1b, 0)], wr=[(al1b, 0)])
        for i in range(KC):
            for hh in range(2):
                E_("y", lambda y, i=i, hh=hh: y.dma_start(
                    a32(Z_ALB, 2048), dr["w2n"][i * 128:(i + 1) * 128,
                                                hh * 2048:(hh + 1) * 2048]),
                    wr=[(slB, 0)], dma="alB")
                E_("v", lambda v, hh=hh: v.reduce_sum(
                    sc_t[:, S_A2 + hh:S_A2 + hh + 1], a32(Z_ALB, 2048), axis=AX,
                    apply_absolute_value=True), rd=[(slB, 0)], wr=[(tcol, 0)])
            E_("v", lambda v, i=i: v.reduce_sum(
                al2_t[:, i:i + 1], sc_t[:, S_A2:S_A2 + 2], axis=AX),
               rd=[(tcol, 0)], wr=[(al2b, 0)])
        E_("v", lambda v: v.tensor_scalar(al2_t[:, :], al2_t[:, :], 1.0 / F, None,
                                          OP.mult), rd=[(al2b, 0)], wr=[(al2b, 0)])
        if _DBG.get("dump_al"):
            E_("q", lambda q: q.dma_start(dbgT[:, 0:32], al1_t[:, :]),
               rd=[(al1b, 0)], dma="dbg")
            E_("q", lambda q: q.dma_start(dbgT[:, 32:40], al2_t[:, :]),
               rd=[(al2b, 0)], dma="dbg")

    def phase_ffn1():
        hT = [ar_16.claim(A_V + i * 1024, A_V + (i + 1) * 1024, f"hT{i}", 2)
              for i in range(IC)]
        w1t = [ar_16.claim(A_W13 + i * 1024, A_W13 + (i + 1) * 1024, f"w1t{i}")
               for i in range(2)]
        s1t = [ar_16.claim(A_W13 + 2048 + i * 1024, A_W13 + 2048 + (i + 1) * 1024,
                           f"s1t{i}") for i in range(2)]
        w2t = [ar_16.claim(A_W2R + i * 2048, A_W2R + (i + 1) * 2048, f"w2t{i}")
               for i in range(2)]
        W1O = [A_W13, A_W13 + 1024]
        S1O = [A_W13 + 2048, A_W13 + 3072]
        W2TO = [A_W2R, A_W2R + 2048]

        def ld_s1(i, sl):
            E_("q", lambda q, i=i, sl=sl: q.dma_start(
                a16(W1O[sl], 1024).rearrange("p (k n) -> p k n", n=128),
                bass.AP(dr["w1tr"], i * 8 * 128 * 128,
                        [[128, 128], [128 * 128, 8], [1, 128]])),
                wr=[(w1t[sl], 0)], dma="s1" + str(sl))
            E_("s", lambda s, sl=sl: s.activation(
                a16(S1O[sl], 1024), a16(W1O[sl], 1024), AF.Sign),
               rd=[(w1t[sl], 0)], wr=[(s1t[sl], 0)])

        def ld_s2h(dc, hh, sl):
            E_("q", lambda q, dc=dc, hh=hh, sl=sl: q.dma_start(
                a16(W2TO[sl], 2048).rearrange("p (k n) -> p k n", n=128),
                bass.AP(dr["w2tr"], (dc * 32 + hh * 16) * 128 * 128,
                        [[128, 128], [128 * 128, 16], [1, 128]])),
                wr=[(w2t[sl], 0)], dma="s2" + str(sl))
            E_("s", lambda s, sl=sl: s.activation(
                a16(W2TO[sl], 2048), a16(W2TO[sl], 2048), AF.Sign),
               rd=[(w2t[sl], 0)], wr=[(w2t[sl], 0)])

        for tgi in range(NTG):
            t0 = tgi * 1024
            if tgi == 0:
                ld_s1(0, 0)
            for i in range(IC):
                sl = i % 2
                if i + 1 < IC:
                    ld_s1(i + 1, 1 - sl)
                if i == IC - 6:
                    ld_s2h(0, 0, 0)
                elif i == IC - 3:
                    ld_s2h(0, 1, 1)
                for stg in range(2):
                    pst, ppp = PSBC[stg]
                    mem = []
                    for c in range(KC):
                        def fu(t, sl=sl, c=c, t0=t0, stg=stg, pst=pst):
                            return t.matmul(
                                pst[:, :], a16(S1O[sl] + c * 128, 128),
                                xn_t[c][:, t0 + stg * 512:t0 + stg * 512 + 512],
                                start=(c == 0), stop=(c == KC - 1))
                        mem.append((fu, [(s1t[sl], 0),
                                         (xn[c], (t0 + stg * 512) // 512)]))
                    G_("t", mem, wr=[(ppp, 0)])
                    E_("s", lambda s, i=i, stg=stg, pst=pst: s.activation(
                        a16(A_V + i * 1024 + stg * 512, 512), pst[:, :], AF.Gelu,
                        bias=b1c_t[:, i:i + 1], scale=al1_t[:, i:i + 1]),
                       rd=[(ppp, 0), (biasb, B_IDX['b1c']), (al1b, 0)], wr=[(hT[i], stg)])
            for dc in range(KC):
                for stg in range(2):
                    pidx = (dc * 2 + stg) % 4
                    for hh in range(2):
                        sl2 = hh
                        mem = []
                        for j in range(16):
                            i = hh * 16 + j
                            def fd(t, sl2=sl2, j=j, i=i, stg=stg, pidx=pidx,
                                   first=(hh == 0 and j == 0),
                                   last=(hh == 1 and j == 15)):
                                return t.matmul(
                                    PS_A[:, pidx * 512:pidx * 512 + 512],
                                    a16(W2TO[sl2] + j * 128, 128),
                                    a16(A_V + i * 1024 + stg * 512, 512),
                                    start=first, stop=last)
                            mem.append((fd, [(w2t[sl2], 0), (hT[i], stg)]))
                        G_("t", mem, wr=[(psA, pidx)])
                        if stg == 1:
                            if hh == 0 and dc + 1 < KC:
                                ld_s2h(dc + 1, 0, 0)
                            elif hh == 1 and dc + 1 < KC:
                                ld_s2h(dc + 1, 1, 1)
                            elif hh == 1 and dc == KC - 1 and tgi == 0:
                                ld_s1(0, 0)
                    g4 = (t0 + stg * 512) // 512
                    tsl = (dc * 2 + stg) % 2
                    E_("s", lambda s, pidx=pidx, dc=dc, tsl=tsl: s.mul(
                        a32(Z_TMP + tsl * 512, 512),
                        PS_A[:, pidx * 512:pidx * 512 + 512], al2_t[:, dc:dc + 1]),
                       rd=[(psA, pidx), (al2b, 0)], wr=[(tmpf, tsl)])
                    E_("v", lambda v, dc=dc, g4=g4, tsl=tsl: v.scalar_tensor_tensor(
                        res_t[dc][:, g4 * 512:(g4 + 1) * 512],
                        a32(Z_TMP + tsl * 512, 512), bias_t["b2c"][:, dc:dc + 1],
                        res_t[dc][:, g4 * 512:(g4 + 1) * 512], OP.add, OP.add),
                       rd=[(tmpf, tsl), (biasb, B_IDX['b2c']), (res[dc], g4)],
                       wr=[(res[dc], g4)])

    phase_ln(0)
    cwtb = None
    if upto == "a1solo":
        phase_attn(1)
    elif upto != "ln0":
        phase_attn(0)
        cwtb = phase_gate()
    if upto in ("moe0", "ln1", "attn1", "full", "xnl1"):
        phase_moe(cwtb)
    if upto in ("ln1", "attn1", "full", "xnl1"):
        phase_ln(1)
    if upto == "full":
        phase_alpha()
    if upto in ("attn1", "full"):
        phase_attn(1)
    if upto == "full":
        phase_ffn1()

    if upto in ("ln0", "xnl1"):
        for c in range(KC):
            E_("q", lambda q, c=c: q.dma_start(
                outT[c * 128:(c + 1) * 128, :], xn_t[c][:, :]),
               rd=[(xn[c], p) for p in range(4)], dma="dbg")
        em.final_wait("q")
    else:
        for c in range(KC):
            E_("y", lambda y, c=c: y.dma_start(
                outT[c * 128:(c + 1) * 128, :], res_t[c][:, :]),
               rd=[(res[c], p) for p in range(4)], dma="out")
    em.final_wait("y")

    with ExitStack() as st:
        for ch in list(em.ch.values()) + list(em.dch.values()):
            ch.h = st.enter_context(nc.semaphore("ch_" + ch.name))
        with nc.Block() as block:
            @block.tensor
            def _(eng):
                _replay(em, "t", eng)

            @block.vector
            def _(eng):
                _replay(em, "v", eng)

            @block.scalar
            def _(eng):
                _replay(em, "s", eng)

            @block.gpsimd
            def _(eng):
                _replay(em, "q", eng)

            @block.sync
            def _(eng):
                _replay(em, "y", eng)
    return nc, em


_NC_CACHE = {}


def _get_nc(upto="full"):
    if upto not in _NC_CACHE:
        _NC_CACHE[upto] = _build(upto)
    return _NC_CACHE[upto]


def kernel(**inputs):
    x = np.asarray(inputs["L0_attn_x"], dtype=np.float32)
    host = _host_prep(inputs)
    nc, _em = _get_nc("full")
    in_maps = []
    for b in range(NCORE):
        m = {"xT": np.ascontiguousarray(x[b].T)}
        m.update(host)
        in_maps.append(m)
    res = run_bass_kernel_spmd(nc, in_maps, core_ids=list(range(NCORE)))
    out = np.stack([np.ascontiguousarray(r["outT"].T) for r in res.results], axis=0)
    return out


# version 08 (final)
# speedup vs baseline: 1.0696x; 1.0415x over previous
import numpy as np
import ml_dtypes
import concourse.bass as bass
import concourse.mybir as mybir
from concourse.bass_utils import run_bass_kernel_spmd
from contextlib import ExitStack

F32 = mybir.dt.float32
BF16 = mybir.dt.bfloat16
AX = mybir.AxisListType.X
OP = mybir.AluOpType
AF = mybir.ActivationFunctionType

D = 1024
T = 2048
KC = 8
TT = 16
NP = 8
E = 8
F = 4096
FC = 32
NTG = 2
IC = 32
NTG4 = 4
NCORE = 8
SCALE = 0.125
_DBG = {}
NEG_BIG = -1.0e9



class Chan:
    __slots__ = ("name", "n", "h")

    def __init__(self, name):
        self.name, self.n, self.h = name, 0, None


class Buf:
    __slots__ = ("name", "w", "r", "birth")

    def __init__(self, name, parts=1, birth=None):
        self.name = name
        self.w = [None] * parts
        self.r = [dict() for _ in range(parts)]
        self.birth = birth or []


class Arena:

    def __init__(self):
        self.live = []

    def claim(self, lo, hi, name, parts=1):
        toks = []
        for a, b, ob in self.live:
            if a < hi and lo < b:
                for p in range(len(ob.w)):
                    if ob.w[p] is not None:
                        toks.append(ob.w[p])
                    toks.extend(ob.r[p].values())
        nb = Buf(name, parts, birth=toks)
        self.live.append((lo, hi, nb))
        return nb


class Em:
    def __init__(self):
        self.ch = {e: Chan(e) for e in "tvsqy"}
        self.dch = {}
        self.ops = {e: [] for e in "tvsqy"}
        self.waited = {e: {} for e in "tvsqy"}
        self.pend = {}
        self.n_inst = 0

    def dma_chan(self, name):
        if name not in self.dch:
            self.dch[name] = Chan(name)
        return self.dch[name]

    def _wait(self, e, tok):
        ch, val = tok
        if self.waited[e].get(ch.name, 0) >= val:
            return
        assert val <= ch.n, f"wait on future token {ch.name}:{val} > {ch.n}"
        self.waited[e][ch.name] = val
        self.ops[e].append((0, ch, val))

    def _deps(self, rd, wr):
        deps = []
        for b, p in rd:
            t = b.w[p]
            if t is not None:
                deps.append(t)
        for b, p in wr:
            deps.extend(b.birth)
            if b.w[p] is not None:
                deps.append(b.w[p])
            deps.extend(b.r[p].values())
        return deps

    def _upd(self, rd, wr, tok):
        for b, p in rd:
            b.r[p][tok[0].name] = tok
        for b, p in wr:
            b.w[p] = tok
            b.r[p] = {}

    def emit(self, e, fn, rd=(), wr=(), dma=None, inc=True):
        for t in self._deps(rd, wr):
            self._wait(e, t)
        if not inc:
            self.ops[e].append((1, fn, None, 0))
            self.n_inst += 1
            self.pend.setdefault(e, []).append((list(rd), list(wr)))
            return None
        if dma is not None:
            ch = self.dma_chan(dma) if isinstance(dma, str) else dma
            amt = 16
        else:
            ch = self.ch[e]
            amt = 1
        ch.n += amt
        self.ops[e].append((1, fn, ch, amt))
        self.n_inst += 1
        tok = (ch, ch.n)
        for prd, pwr in self.pend.pop(e, []):
            self._upd(prd, pwr, tok)
        self._upd(rd, wr, tok)
        return tok

    def bulk_fixup(self, bufs, tok):
        for b in bufs:
            for p in range(len(b.w)):
                if b.w[p] is not None:
                    b.w[p] = tok

    def emit_group(self, e, members, wr=()):
        all_rd = []
        for fn, rd in members:
            for t in self._deps(rd, wr):
                self._wait(e, t)
            all_rd.extend(rd)
        ch = self.ch[e]
        for i, (fn, rd) in enumerate(members):
            last = i == len(members) - 1
            if last:
                ch.n += 1
            self.ops[e].append((1, fn, ch if last else None, 1))
            self.n_inst += 1
        tok = (ch, ch.n)
        self._upd(all_rd, wr, tok)
        return tok

    def final_wait(self, e):
        for ch in list(self.ch.values()) + list(self.dch.values()):
            if ch.n > 0:
                self.ops[e].append((0, ch, ch.n))


def _replay(em, e, eng):
    for entry in em.ops[e]:
        if entry[0] == 0:
            _, ch, val = entry
            eng.wait_ge(ch.h, val)
        else:
            _, fn, ch, amt = entry
            ins = fn(eng)
            if ch is not None:
                ins.then_inc(ch.h, amt)



def _pair_blocks(w):
    return np.ascontiguousarray(w.reshape(8, 128, 8, 128).transpose(2, 0, 1, 3))


def _cols(b, n):
    return np.ascontiguousarray(b.reshape(n, 128).T)


def _host_prep(inputs):
    f32 = lambda a: np.ascontiguousarray(np.asarray(a, dtype=np.float32))
    d = {}
    d["l0q"] = _pair_blocks(f32(inputs["L0_attn_Wq"]))
    d["l0k"] = _pair_blocks(f32(inputs["L0_attn_Wk"]))
    d["l0v"] = f32(inputs["L0_attn_Wv"])
    d["l0o"] = f32(inputs["L0_attn_Wo"])
    d["l0bo"] = _cols(f32(inputs["L0_attn_bo"]), 8)
    d["wg"] = np.ascontiguousarray(
        f32(inputs["L0_ffn_Wg"]).reshape(8, 128, 8).transpose(1, 0, 2).reshape(128, 64))
    d["bg"] = f32(inputs["L0_ffn_bg"]).reshape(1, 8)
    d["w1r"] = np.ascontiguousarray(
        f32(inputs["L0_ffn_W1"]).reshape(8, 8, 128, 32, 128).transpose(0, 3, 1, 2, 4))
    d["w3r"] = np.ascontiguousarray(
        f32(inputs["L0_ffn_W3"]).reshape(8, 8, 128, 32, 128).transpose(0, 3, 1, 2, 4))
    d["w2r"] = np.ascontiguousarray(
        f32(inputs["L0_ffn_W2"]).reshape(8, 32, 128, 8, 128).transpose(0, 3, 1, 2, 4))
    d["l1q"] = _pair_blocks(f32(inputs["L1_attn_Wq"]).T)
    d["l1k"] = _pair_blocks(f32(inputs["L1_attn_Wk"]).T)
    d["l1v"] = np.ascontiguousarray(f32(inputs["L1_attn_Wv"]).T)
    d["l1o"] = np.ascontiguousarray(f32(inputs["L1_attn_Wo"]).T)
    d["l1bq"] = _cols(f32(inputs["L1_attn_bq"]), 8)
    d["l1bk"] = _cols(f32(inputs["L1_attn_bk"]), 8)
    d["l1bv"] = f32(inputs["L1_attn_bv"]).reshape(1, 1024)
    d["l1bo"] = _cols(f32(inputs["L1_attn_bo"]), 8)
    d["w1n"] = f32(inputs["L1_ffn_w1"])
    d["w1tr"] = np.ascontiguousarray(
        f32(inputs["L1_ffn_w1"]).T.reshape(8, 128, 32, 128).transpose(2, 0, 1, 3))
    d["w2n"] = f32(inputs["L1_ffn_w2"])
    d["w2tr"] = np.ascontiguousarray(
        f32(inputs["L1_ffn_w2"]).T.reshape(32, 128, 8, 128).transpose(2, 0, 1, 3))
    d["b1c"] = _cols(f32(inputs["L1_ffn_b1"]), 32)
    d["b2c"] = _cols(f32(inputs["L1_ffn_b2"]), 8)
    return d


_DRAM_SPECS = [
    ("l0q", [8, 8, 128, 128]), ("l0k", [8, 8, 128, 128]),
    ("l0v", [1024, 1024]), ("l0o", [1024, 1024]), ("l0bo", [128, 8]),
    ("wg", [128, 64]), ("bg", [1, 8]),
    ("w1r", [8, 32, 8, 128, 128]), ("w3r", [8, 32, 8, 128, 128]),
    ("w2r", [8, 8, 32, 128, 128]),
    ("l1q", [8, 8, 128, 128]), ("l1k", [8, 8, 128, 128]),
    ("l1v", [1024, 1024]), ("l1o", [1024, 1024]),
    ("l1bq", [128, 8]), ("l1bk", [128, 8]), ("l1bv", [1, 1024]), ("l1bo", [128, 8]),
    ("w1n", [4096, 1024]), ("w1tr", [32, 8, 128, 128]),
    ("w2n", [1024, 4096]), ("w2tr", [8, 32, 128, 128]),
    ("b1c", [128, 32]), ("b2c", [128, 8]),
]

A_V = 0
A_QT = 16384
A_KT = 18432
A_OT = 20480
A_P = 22528
A_P2 = 24576
A_P3 = 33792
A_PT = 26624
A_QKW = 27648
A_WOW = 31744
A_W13 = 32768
A_W2R = 36864
A_CWT = 40960

Z_R1 = 0
Z_R2 = 2048
Z_TMP = 4096
Z_CWB = 0
Z_BVR = 1024
Z_ALA = 2048
Z_ALB = 2048


def _build(upto="full"):
    nc = bass.Bass("TRN2", target_bir_lowering=False)
    em = Em()
    ar_16 = Arena()
    ar_32 = Arena()

    xT = nc.dram_tensor("xT", [D, T], F32, kind="ExternalInput")
    outT = nc.dram_tensor("outT", [D, T], F32, kind="ExternalOutput")
    dbgT = nc.dram_tensor("dbgT", [128, 64], F32, kind="ExternalOutput") if _DBG else None
    dr = {n: nc.dram_tensor(n, s, F32, kind="ExternalInput") for n, s in _DRAM_SPECS}

    ident_bf = nc.inline_tensor(np.eye(128, dtype=ml_dtypes.bfloat16), name="ident_bf")
    maskc = np.zeros((128, 128), np.float32)
    maskc[np.triu_indices(128, 1)] = NEG_BIG
    mask_f = nc.inline_tensor(maskc, name="mask_f")
    onerow_f = nc.inline_tensor(np.ones((1, 128), np.float32), name="onerow_f")
    onecol_f = nc.inline_tensor(np.ones((128, 1), np.float32), name="onecol_f")
    ohm = np.zeros((8, 8 * 128), ml_dtypes.bfloat16)
    for e_ in range(8):
        ohm[e_, e_ * 128:(e_ + 1) * 128] = 1
    oh_bf = nc.inline_tensor(ohm, name="oh_bf")

    res_t = [nc.alloc_sbuf_tensor(f"res{c}", [128, T], F32) for c in range(KC)]
    xn_t = [nc.alloc_sbuf_tensor(f"xn{c}", [128, T], BF16) for c in range(KC)]
    AR16 = nc.alloc_sbuf_tensor("AR16", [128, 43008], BF16)
    AR32 = nc.alloc_sbuf_tensor("AR32", [128, 5120], F32)
    cwf_t = nc.alloc_sbuf_tensor("cwf", [128, 128], F32)
    cwbf_t = nc.alloc_sbuf_tensor("cwbf", [128, 128], BF16)
    idb_t = nc.alloc_sbuf_tensor("idb", [128, 128], BF16)
    mask_t = nc.alloc_sbuf_tensor("maskt", [128, 128], F32)
    onerow_t = nc.alloc_sbuf_tensor("onerow", [1, 128], F32)
    onecol_t = nc.alloc_sbuf_tensor("onecol", [128, 1], F32)
    oh_t = nc.alloc_sbuf_tensor("oht", [8, 8 * 128], BF16)
    wg_t = nc.alloc_sbuf_tensor("wgt", [128, 64], BF16)
    bgb_t = nc.alloc_sbuf_tensor("bgb", [128, 8], F32)
    bgrow_t = nc.alloc_sbuf_tensor("bgrow", [1, 8], F32)
    sc_t = nc.alloc_sbuf_tensor("sc", [128, 48], F32)
    bias_t = {n: nc.alloc_sbuf_tensor("b_" + n, [128, 8], F32)
              for n in ("l0bo", "l1bq", "l1bk", "l1bo", "b2c")}
    b1c_t = nc.alloc_sbuf_tensor("b_b1c", [128, 32], F32)
    al1_t = nc.alloc_sbuf_tensor("al1", [128, 32], F32)
    al2_t = nc.alloc_sbuf_tensor("al2", [128, 8], F32)

    PS_A = nc.alloc_psum_tensor("PS_A", [128, 2048], F32)
    PS_B = nc.alloc_psum_tensor("PS_B", [128, 512], F32)
    PS_C = nc.alloc_psum_tensor("PS_C", [128, 512], F32)
    PT_T = [nc.alloc_psum_tensor("PT_A", [128, 512], BF16),
            nc.alloc_psum_tensor("PT_B", [128, 512], BF16)]

    res = [Buf(f"res{c}", 4) for c in range(KC)]
    xn = [Buf(f"xn{c}", 4) for c in range(KC)]
    psA = Buf("psA", 4)
    psB = Buf("psB", 1)
    psC = Buf("psC", 1)
    ptb = [Buf("ptA", 4), Buf("ptB", 4)]
    consts = Buf("consts", 6)
    C_ID, C_MASK, C_ROW, C_COL, C_OH, C_WG = range(6)
    biasb = Buf("biasb", 6)
    B_IDX = {"l0bo": 0, "l1bq": 1, "l1bk": 2, "l1bo": 3, "b2c": 4, "b1c": 5}
    scb = Buf("scb", 16)
    tmpf = Buf("tmpf", 2)
    cwfb = Buf("cwf", 16)
    cwbfb = Buf("cwbf", 1)
    al1b = Buf("al1", 1)
    al2b = Buf("al2", 1)
    bgrb = Buf("bgrow", 1)
    bgbb = Buf("bgb", 1)
    wgb = Buf("wgb", 1)

    S_GL, S_E8, S_MS, S_CW = 0, 8, 16, 24
    S_MC = 32
    S_MX, S_NM, S_L, S_RL = 36, 37, 38, 39
    S_G1, S_G2, S_G3, S_G4 = 40, 41, 42, 43
    S_A2 = 44

    def a16(off, ln, p0=0, p1=128):
        return AR16[p0:p1, off:off + ln]

    def a32(off, ln, p0=0, p1=128):
        return AR32[p0:p1, off:off + ln]

    E_ = em.emit
    G_ = em.emit_group
    PSBC = [(PS_B, psB), (PS_C, psC)]

    E_("y", lambda y: y.dma_start(idb_t[:, :], ident_bf[:, :]), wr=[(consts, C_ID)], dma="cst")
    E_("y", lambda y: y.dma_start(mask_t[:, :], mask_f[:, :]), wr=[(consts, C_MASK)], dma="cst")
    E_("y", lambda y: y.dma_start(onerow_t[:, :], onerow_f[:, :]), wr=[(consts, C_ROW)], dma="cst")
    E_("y", lambda y: y.dma_start(onecol_t[:, :], onecol_f[:, :]), wr=[(consts, C_COL)], dma="cst")
    E_("y", lambda y: y.dma_start(oh_t[:, :], oh_bf[:, :]), wr=[(consts, C_OH)], dma="cst")
    E_("q", lambda q: q.dma_start(wg_t[:, :], dr["wg"][:, :]), wr=[(wgb, 0)], dma="cwg")
    for n in ("l0bo", "l1bq", "l1bk", "l1bo", "b2c"):
        E_("y", lambda y, n=n: y.dma_start(bias_t[n][:, :], dr[n][:, :]),
           wr=[(biasb, B_IDX[n])], dma="cst")
    E_("y", lambda y: y.dma_start(b1c_t[:, :], dr["b1c"][:, :]), wr=[(biasb, B_IDX["b1c"])], dma="cst")
    E_("y", lambda y: y.dma_start(bgrow_t[:, :], dr["bg"][:, :]), wr=[(bgrb, 0)], dma="cst")
    em.bulk_fixup([consts, biasb, bgrb], (em.dma_chan("cst"), em.dma_chan("cst").n))
    for c in range(KC):
        E_("y", lambda y, c=c: y.dma_start(res_t[c][:, :], xT[c * 128:(c + 1) * 128, :]),
           wr=[(res[c], p) for p in range(4)], dma="xin")
    em.bulk_fixup(res, (em.dma_chan("xin"), em.dma_chan("xin").n))
    E_("t", lambda t: t.matmul(PS_B[:, 0:8], onerow_t[:, :], bgrow_t[:, :],
                               start=True, stop=True),
       rd=[(consts, C_ROW), (bgrb, 0)], wr=[(psB, 0)])
    E_("v", lambda v: v.tensor_copy(bgb_t[:, :], PS_B[:, 0:8]),
       rd=[(psB, 0)], wr=[(bgbb, 0)])

    def phase_ln(L):
        r1 = ar_32.claim(Z_R1, Z_R1 + 2048, f"r1L{L}")
        r2 = ar_32.claim(Z_R2, Z_R2 + 2048, f"r2L{L}")
        for g in range(NTG4):
            for c in range(KC):
                E_("t", lambda t, c=c, g=g: t.matmul(
                    PS_A[0:1, g * 512:(g + 1) * 512], onecol_t[:, :],
                    res_t[c][:, g * 512:(g + 1) * 512],
                    start=(c == 0), stop=(c == KC - 1)),
                   rd=[(consts, C_COL), (res[c], g)],
                   wr=[(psA, g)] if c in (0, KC - 1) else [])
        for g in range(NTG4):
            E_("v", lambda v, g=g: v.tensor_scalar(
                a32(Z_R1 + g * 512, 512, 0, 1), PS_A[0:1, g * 512:(g + 1) * 512],
                1.0 / D, None, OP.mult), rd=[(psA, g)], wr=[(r1, 0)])
        for g in range(NTG4):
            for c in range(KC):
                sl = c % 2
                E_("s", lambda s, c=c, g=g, sl=sl: s.square(
                    a32(Z_TMP + sl * 512, 512), res_t[c][:, g * 512:(g + 1) * 512]),
                    rd=[(res[c], g)], wr=[(tmpf, sl)])
                E_("t", lambda t, c=c, g=g, sl=sl: t.matmul(
                    PS_A[32:33, g * 512:(g + 1) * 512], onecol_t[:, :],
                    a32(Z_TMP + sl * 512, 512),
                    start=(c == 0), stop=(c == KC - 1)),
                   rd=[(consts, C_COL), (tmpf, sl)],
                   wr=[(psA, g)] if c in (0, KC - 1) else [])
        for g in range(NTG4):
            E_("v", lambda v, g=g: v.tensor_scalar(
                a32(Z_R2 + g * 512, 512, 0, 1), PS_A[32:33, g * 512:(g + 1) * 512],
                1.0 / D, 1e-5, OP.mult, OP.add), rd=[(psA, g)], wr=[(r2, 0)])
        for hh in range(2):
            E_("v", lambda v, hh=hh: v.tensor_mul(
                a32(Z_TMP, 1024, 0, 1), a32(Z_R1 + hh * 1024, 1024, 0, 1),
                a32(Z_R1 + hh * 1024, 1024, 0, 1)),
               rd=[(r1, 0)], wr=[(tmpf, 0), (tmpf, 1)])
            E_("v", lambda v, hh=hh: v.tensor_sub(
                a32(Z_R2 + hh * 1024, 1024, 0, 1), a32(Z_R2 + hh * 1024, 1024, 0, 1),
                a32(Z_TMP, 1024, 0, 1)),
               rd=[(r2, 0), (tmpf, 0), (tmpf, 1)], wr=[(r2, 0)])
        E_("s", lambda s: s.activation(a32(Z_R2, T, 0, 1), a32(Z_R2, T, 0, 1),
                                       AF.Sqrt), rd=[(r2, 0)], wr=[(r2, 0)])
        E_("v", lambda v: v.reciprocal(a32(Z_R2, T, 0, 1), a32(Z_R2, T, 0, 1)),
           rd=[(r2, 0)], wr=[(r2, 0)])
        E_("v", lambda v: v.scalar_tensor_tensor(
            a32(Z_R1, T, 0, 1), a32(Z_R1, T, 0, 1), -1.0, a32(Z_R2, T, 0, 1),
            OP.mult, OP.mult), rd=[(r1, 0), (r2, 0)], wr=[(r1, 0)])
        for g in range(NTG4):
            nbt, nb = PSBC[g % 2]
            E_("t", lambda t, g=g: t.matmul(
                PS_A[:, g * 512:(g + 1) * 512], onerow_t[:, :],
                a32(Z_R2 + g * 512, 512, 0, 1), start=True, stop=True),
               rd=[(consts, C_ROW), (r2, 0)], wr=[(psA, g)])
            E_("t", lambda t, g=g, nbt=nbt: t.matmul(
                nbt[:, :], onerow_t[:, :], a32(Z_R1 + g * 512, 512, 0, 1),
                start=True, stop=True),
               rd=[(consts, C_ROW), (r1, 0)], wr=[(nb, 0)])
            for c in range(KC):
                sl = c % 2
                E_("v", lambda v, c=c, g=g, sl=sl: v.tensor_mul(
                    a32(Z_TMP + sl * 512, 512), res_t[c][:, g * 512:(g + 1) * 512],
                    PS_A[:, g * 512:(g + 1) * 512]),
                    rd=[(res[c], g), (psA, g)], wr=[(tmpf, sl)])
                E_("v", lambda v, c=c, g=g, sl=sl, nbt=nbt: v.tensor_add(
                    xn_t[c][:, g * 512:(g + 1) * 512], a32(Z_TMP + sl * 512, 512),
                    nbt[:, :]), rd=[(tmpf, sl), (nb, 0)], wr=[(xn[c], g)])

    def phase_gate():
        cwtb = ar_16.claim(A_CWT, A_CWT + 2048, "cwT", 16)
        for tt in range(TT):
            pst, ppp = PSBC[tt % 2]
            mem = []
            for c in range(KC):
                def f(t, c=c, tt=tt, pst=pst):
                    return t.matmul(pst[:, 0:8], xn_t[c][:, tt * 128:(tt + 1) * 128],
                                    wg_t[:, c * 8:(c + 1) * 8],
                                    start=(c == 0), stop=(c == KC - 1))
                mem.append((f, [(xn[c], tt // 4), (wgb, 0)]))
            G_("t", mem, wr=[(ppp, 0)])
            E_("v", lambda v, pst=pst: v.tensor_add(
                sc_t[:, S_GL:S_GL + 8], bgb_t[:, :], pst[:, 0:8]),
                rd=[(ppp, 0), (bgbb, 0)], wr=[(scb, 0)])
            E_("v", lambda v: v.reduce_max(sc_t[:, S_G1:S_G1 + 1],
                                           sc_t[:, S_GL:S_GL + 8], axis=AX, negate=True),
               rd=[(scb, 0)], wr=[(scb, 4)])
            E_("s", lambda s: s.activation(sc_t[:, S_E8:S_E8 + 8],
                                           sc_t[:, S_GL:S_GL + 8], AF.Exp,
                                           bias=sc_t[:, S_G1:S_G1 + 1]),
               rd=[(scb, 0), (scb, 4)], wr=[(scb, 1)])
            E_("v", lambda v: v.reduce_max(sc_t[:, S_G2:S_G2 + 1],
                                           sc_t[:, S_E8:S_E8 + 8], axis=AX),
               rd=[(scb, 1)], wr=[(scb, 5)])
            E_("v", lambda v: v.tensor_scalar(
                sc_t[:, S_MS:S_MS + 8], sc_t[:, S_E8:S_E8 + 8],
                sc_t[:, S_G2:S_G2 + 1], NEG_BIG, OP.is_ge, OP.mult),
               rd=[(scb, 1), (scb, 5)], wr=[(scb, 2)])
            E_("v", lambda v: v.tensor_add(sc_t[:, S_MS:S_MS + 8],
                                           sc_t[:, S_MS:S_MS + 8],
                                           sc_t[:, S_E8:S_E8 + 8]),
               rd=[(scb, 2), (scb, 1)], wr=[(scb, 2)])
            E_("v", lambda v: v.reduce_max(sc_t[:, S_G3:S_G3 + 1],
                                           sc_t[:, S_MS:S_MS + 8], axis=AX),
               rd=[(scb, 2)], wr=[(scb, 6)])
            E_("v", lambda v: v.tensor_scalar(
                sc_t[:, S_CW:S_CW + 8], sc_t[:, S_E8:S_E8 + 8],
                sc_t[:, S_G3:S_G3 + 1], None, OP.is_ge),
               rd=[(scb, 1), (scb, 6)], wr=[(scb, 3)])
            E_("v", lambda v: v.tensor_mul(sc_t[:, S_CW:S_CW + 8],
                                           sc_t[:, S_CW:S_CW + 8],
                                           sc_t[:, S_E8:S_E8 + 8]),
               rd=[(scb, 3), (scb, 1)], wr=[(scb, 3)])
            E_("v", lambda v: v.reduce_sum(sc_t[:, S_G4:S_G4 + 1],
                                           sc_t[:, S_CW:S_CW + 8], axis=AX),
               rd=[(scb, 3)], wr=[(scb, 7)])
            E_("v", lambda v: v.reciprocal(sc_t[:, S_G4:S_G4 + 1],
                                           sc_t[:, S_G4:S_G4 + 1]),
               rd=[(scb, 7)], wr=[(scb, 7)])
            E_("v", lambda v, tt=tt: v.tensor_scalar(
                cwf_t[:, tt * 8:(tt + 1) * 8], sc_t[:, S_CW:S_CW + 8],
                sc_t[:, S_G4:S_G4 + 1], None, OP.mult),
               rd=[(scb, 3), (scb, 7)], wr=[(cwfb, tt)])
        E_("v", lambda v: v.tensor_copy(cwbf_t[:, :], cwf_t[:, :]),
           rd=[(cwfb, i) for i in range(TT)], wr=[(cwbfb, 0)])
        for tt in range(TT):
            k = tt % 2
            E_("t", lambda t, tt=tt, k=k: t.transpose(
                PT_T[k][0:8, 0:128], cwbf_t[:, tt * 8:(tt + 1) * 8], idb_t[:, :]),
               rd=[(cwbfb, 0), (consts, C_ID)], wr=[(ptb[k], 0)])
            E_("v", lambda v, tt=tt, k=k: v.tensor_copy(
                a16(A_CWT + tt * 128, 128, 0, 8), PT_T[k][0:8, 0:128]),
               rd=[(ptb[k], 0)], wr=[(cwtb, tt)])
        return cwtb

    def phase_attn(L):
        slab_off = [A_QKW, A_QKW + 1024, A_QKW + 2048, A_QKW + 3072,
                    A_WOW, A_WOW + 1024, A_P, A_P + 1024]
        wvs = [ar_16.claim(slab_off[c], slab_off[c] + 1024, f"wv{L}_{c}")
               for c in range(KC)]
        vbufs = [ar_16.claim(A_V + tt * 1024, A_V + (tt + 1) * 1024, f"V{L}_{tt}")
                 for tt in range(TT)]
        wv_d = dr["l0v"] if L == 0 else dr["l1v"]
        if L == 1:
            bvrb = ar_32.claim(Z_BVR, Z_BVR + 1024, f"bvr{L}")
            bvbb = ar_32.claim(Z_CWB, Z_CWB + 1024, f"bvb{L}")
            E_("y", lambda y: y.dma_start(a32(Z_BVR, 1024, 0, 1), dr["l1bv"][:, :]),
               wr=[(bvrb, 0)], dma="bvr")
            for half in range(2):
                pst, ppp = PSBC[half]
                E_("t", lambda t, half=half, pst=pst: t.matmul(
                    pst[:, :], onerow_t[:, :], a32(Z_BVR + half * 512, 512, 0, 1),
                    start=True, stop=True),
                   rd=[(consts, C_ROW), (bvrb, 0)], wr=[(ppp, 0)])
                E_("v", lambda v, half=half, pst=pst: v.tensor_copy(
                    a32(Z_CWB + half * 512, 512), pst[:, :]),
                   rd=[(ppp, 0)], wr=[(bvbb, 0)])
        for c in range(KC):
            E_("q", lambda q, c=c: q.dma_start(
                a16(slab_off[c], 1024), wv_d[c * 128:(c + 1) * 128, :]),
                wr=[(wvs[c], 0)], dma="wv")
        em.bulk_fixup(wvs, (em.dma_chan("wv"), em.dma_chan("wv").n))
        for tt in range(TT):
            half = tt % 2
            mem = []
            for c in range(KC):
                for dv in range(2):
                    def f(t, tt=tt, c=c, half=half, dv=dv):
                        return t.matmul(
                            PS_A[:, half * 1024 + dv * 512:half * 1024 + dv * 512 + 512],
                            xn_t[c][:, tt * 128:(tt + 1) * 128],
                            a16(slab_off[c] + dv * 512, 512),
                            start=(c == 0), stop=(c == KC - 1))
                    mem.append((f, [(xn[c], tt // 4), (wvs[c], 0)]))
            G_("t", mem, wr=[(psA, 2 * half), (psA, 2 * half + 1)])
            if L == 0:
                E_("v", lambda v, tt=tt, half=half: v.tensor_copy(
                    a16(A_V + tt * 1024, 1024), PS_A[:, half * 1024:half * 1024 + 1024]),
                   rd=[(psA, 2 * half), (psA, 2 * half + 1)], wr=[(vbufs[tt], 0)])
            else:
                E_("v", lambda v, tt=tt, half=half: v.tensor_add(
                    a16(A_V + tt * 1024, 1024), a32(Z_CWB, 1024),
                    PS_A[:, half * 1024:half * 1024 + 1024]),
                   rd=[(psA, 2 * half), (psA, 2 * half + 1), (bvbb, 0)],
                   wr=[(vbufs[tt], 0)])

        qtp = ar_16.claim(A_QT, A_QT + 2048, f"qt{L}", 4)
        ktp = ar_16.claim(A_KT, A_KT + 2048, f"kt{L}", 4)
        otp = ar_16.claim(A_OT, A_OT + 2048, f"ot{L}", 16)
        pb = [ar_16.claim(A_P, A_P + 2048, f"P{L}a", 16),
              ar_16.claim(A_P2, A_P2 + 2048, f"P{L}b", 16),
              ar_16.claim(A_P3, A_P3 + 2048, f"P{L}c", 16)]
        PO = [A_P, A_P2, A_P3]
        pts = [ar_16.claim(A_PT + i * 512, A_PT + (i + 1) * 512, f"pts{L}{i}")
               for i in range(2)]
        wqs = [ar_16.claim(A_QKW + i * 2048, A_QKW + i * 2048 + 1024, f"wq{L}{i}")
               for i in range(2)]
        wks = [ar_16.claim(A_QKW + i * 2048 + 1024, A_QKW + (i + 1) * 2048, f"wk{L}{i}")
               for i in range(2)]
        wos = [ar_16.claim(A_WOW + i * 1024, A_WOW + (i + 1) * 1024, f"wo{L}{i}")
               for i in range(2)]
        mcolb = Buf("mcol", 1)
        mxb = Buf("mx", 1)
        nmb = Buf("nm", 1)
        lb = Buf("l", 1)
        rlb = Buf("rl", 1)

        wq_d = dr["l0q"] if L == 0 else dr["l1q"]
        wk_d = dr["l0k"] if L == 0 else dr["l1k"]
        wo_d = dr["l0o"] if L == 0 else dr["l1o"]
        bo_t = bias_t["l0bo"] if L == 0 else bias_t["l1bo"]

        def ld_qk(pair, sl):
            ch = "wq" + str(sl)
            E_("q", lambda q, pair=pair, sl=sl: q.dma_start(
                a16(A_QKW + sl * 2048, 1024).rearrange("p (k n) -> p k n", n=128),
                bass.AP(wq_d, pair * (8 * 128 * 128),
                        [[128, 128], [128 * 128, 8], [1, 128]])),
                wr=[(wqs[sl], 0)], dma=ch)
            tok = E_("q", lambda q, pair=pair, sl=sl: q.dma_start(
                a16(A_QKW + sl * 2048 + 1024, 1024).rearrange("p (k n) -> p k n", n=128),
                bass.AP(wk_d, pair * (8 * 128 * 128),
                        [[128, 128], [128 * 128, 8], [1, 128]])),
                wr=[(wks[sl], 0)], dma=ch)
            em.bulk_fixup([wqs[sl], wks[sl]], tok)

        def ld_wo(pair, sl):
            E_("q", lambda q, pair=pair, sl=sl: q.dma_start(
                a16(A_WOW + sl * 1024, 1024), wo_d[pair * 128:(pair + 1) * 128, :]),
                wr=[(wos[sl], 0)], dma="wo" + str(sl))

        ld_qk(0, 0)
        ld_wo(0, 0)

        for pair in range(NP):
            sl = pair % 2
            if pair + 1 < NP:
                ld_qk(pair + 1, 1 - sl)
                ld_wo(pair + 1, 1 - sl)
            for which, wbuf, dstoff, dstb, bcol in (
                    (0, wqs[sl], A_QT, qtp, "l1bq"), (1, wks[sl], A_KT, ktp, "l1bk")):
                for tg in range(NTG4):
                    pst, ppp = PSBC[tg % 2]
                    mem = []
                    for c in range(KC):
                        def f(t, which=which, sl=sl, c=c, tg=tg, pst=pst):
                            return t.matmul(
                                pst[:, :],
                                a16(A_QKW + sl * 2048 + which * 1024 + c * 128, 128),
                                xn_t[c][:, tg * 512:(tg + 1) * 512],
                                start=(c == 0), stop=(c == KC - 1))
                        mem.append((f, [(wbuf, 0), (xn[c], tg)]))
                    G_("t", mem, wr=[(ppp, 0)])
                    if L == 0:
                        E_("v", lambda v, dstoff=dstoff, tg=tg, pst=pst: v.tensor_copy(
                            a16(dstoff + tg * 512, 512), pst[:, :]),
                           rd=[(ppp, 0)], wr=[(dstb, tg)])
                    else:
                        E_("v", lambda v, dstoff=dstoff, tg=tg, pst=pst, bcol=bcol,
                           pair=pair: v.tensor_scalar(
                               a16(dstoff + tg * 512, 512), pst[:, :],
                               bias_t[bcol][:, pair:pair + 1], None, OP.add),
                           rd=[(ppp, 0), (biasb, B_IDX[bcol])], wr=[(dstb, tg)])

            def emit_S(qi, h, soff, ps):
                hp = h & 1
                nk = (qi + 1) * 128
                nkb = (nk + 511) // 512
                for kb in range(nkb):
                    n = min(512, nk - kb * 512)
                    E_("t", lambda t, qi=qi, hp=hp, kb=kb, n=n, soff=soff: t.matmul(
                        PS_A[:, soff + kb * 512:soff + kb * 512 + n],
                        a16(A_QT + qi * 128, 128, 64 * hp, 64 * hp + 64),
                        a16(A_KT + kb * 512, n, 64 * hp, 64 * hp + 64),
                        start=True, stop=True),
                       rd=[(qtp, qi // 4), (ktp, kb)], wr=[(psA, soff // 512 + kb)])
                dkb = (soff + qi * 128) // 512
                E_("v", lambda v, qi=qi, soff=soff: v.tensor_add(
                    PS_A[:, soff + qi * 128:soff + (qi + 1) * 128],
                    PS_A[:, soff + qi * 128:soff + (qi + 1) * 128],
                    mask_t[:, :]), rd=[(consts, C_MASK)], wr=[(psA, dkb)])
                E_("s", lambda s, nk=nk, ps=ps, soff=soff: s.activation(
                    a16(PO[ps], nk), PS_A[:, soff:soff + nk], AF.Exp,
                    scale=SCALE, accum_out=sc_t[:, S_L:S_L + 1]),
                   rd=[(psA, soff // 512 + kb) for kb in range(nkb)],
                   wr=[(pb[ps], k) for k in range(qi + 1)] + [(lb, 0)])
                E_("v", lambda v: v.reciprocal(sc_t[:, S_RL:S_RL + 1],
                                               sc_t[:, S_L:S_L + 1]),
                   rd=[(lb, 0)], wr=[(rlb, 0)])
                E_("v", lambda v, nk=nk, ps=ps: v.tensor_scalar(
                    a16(PO[ps], nk), a16(PO[ps], nk), sc_t[:, S_RL:S_RL + 1], None,
                    OP.mult),
                   rd=[(rlb, 0)] + [(pb[ps], k) for k in range(qi + 1)],
                   wr=[(pb[ps], k) for k in range(qi + 1)])
                if _DBG.get("dumpP") == (pair, qi, h):
                    E_("q", lambda q, nk=nk, hp=hp: q.dma_start(
                        outT[0:128, 0:nk], a16(PO[hp], nk)),
                       rd=[(pb[hp], k) for k in range(qi + 1)], dma="dbg")
                    E_("q", lambda q: q.dma_start(
                        outT[128:256, 0:1024], a16(A_V + 8 * 1024, 1024)),
                       rd=[(vbufs[8], 0)], dma="dbg")
                    E_("q", lambda q: q.dma_start(
                        outT[256:384, 0:1024], a16(A_KT, 1024, 0, 128)),
                       rd=[(ktp, 0), (ktp, 1)], dma="dbg")
                    E_("q", lambda q: q.dma_start(
                        outT[384:512, 0:1024], a16(A_KT + 1024, 1024, 0, 128)),
                       rd=[(ktp, 2), (ktp, 3)], dma="dbg")
                    E_("q", lambda q, qi=qi: q.dma_start(
                        outT[512:640, 0:128], a16(A_QT + qi * 128, 128)),
                       rd=[(qtp, qi // 4)], dma="dbg")

            def emit_TPV(qi, h, ps):
                hp = h & 1
                otps, otpb2 = PSBC[qi % 2]
                nkc = qi + 1
                batches = [(b4 * 4, min(nkc, b4 * 4 + 4)) for b4 in range((nkc + 3) // 4)]

                def tp_batch(j):
                    lo, hi = batches[j]
                    k = j % 2
                    for kc in range(lo, hi):
                        E_("t", lambda t, kc=kc, k=k, ps=ps, lo=lo: t.transpose(
                            PT_T[k][:, (kc - lo) * 128:(kc - lo) * 128 + 128],
                            a16(PO[ps] + kc * 128, 128), idb_t[:, :]),
                           rd=[(pb[ps], kc), (consts, C_ID)],
                           wr=[(ptb[k], kc - lo)], inc=(kc == hi - 1))
                    n = (hi - lo) * 128
                    if k == 0:
                        E_("v", lambda v, k=k, n=n: v.tensor_copy(
                            a16(A_PT + k * 512, n), PT_T[k][:, 0:n]),
                           rd=[(ptb[k], i) for i in range(hi - lo)], wr=[(pts[k], 0)])
                    else:
                        E_("s", lambda s, k=k, n=n: s.copy(
                            a16(A_PT + k * 512, n), PT_T[k][:, 0:n]),
                           rd=[(ptb[k], i) for i in range(hi - lo)], wr=[(pts[k], 0)])

                def pv_batch(j):
                    lo, hi = batches[j]
                    k = j % 2
                    for kc in range(lo, hi):
                        E_("t", lambda t, kc=kc, k=k, hp=hp, h=h, qi=qi, otps=otps,
                           lo=lo: t.matmul(
                               otps[64 * hp:64 * hp + 64, 0:128],
                               a16(A_V + kc * 1024 + h * 64, 64),
                               a16(A_PT + k * 512 + (kc - lo) * 128, 128),
                               start=(kc == 0), stop=(kc == qi),
                               tile_position=(0, 64 * hp)),
                           rd=[(pts[k], 0), (vbufs[kc], 0)],
                           wr=[(otpb2, 0)] if kc == qi else [], inc=(kc == qi))

                for j in range(len(batches)):
                    tp_batch(j)
                    if j > 0:
                        pv_batch(j - 1)
                pv_batch(len(batches) - 1)
                if hp == 1:
                    E_("v", lambda v, qi=qi, otps=otps: v.tensor_copy(
                        a16(A_OT + qi * 128, 128), otps[:, 0:128]),
                       rd=[(otpb2, 0)], wr=[(otp, qi)])

            pend = []
            idx = 0
            for qi in range(_DBG.get('maxqi', TT)):
                for h in (2 * pair, 2 * pair + 1):
                    soff = (idx % 2) * 1024 if qi < 8 else 0
                    ps = idx % 3
                    emit_S(qi, h, soff, ps)
                    pend.append((qi, h, ps))
                    idx += 1
                    want = 2 if qi < 8 else 1
                    while len(pend) > want:
                        emit_TPV(*pend.pop(0))
            while pend:
                emit_TPV(*pend.pop(0))

            for dc in range(KC):
                for tg in range(NTG4):
                    pst, ppp = PSBC[(dc * NTG4 + tg) % 2]
                    E_("t", lambda t, sl=sl, dc=dc, tg=tg, pst=pst: t.matmul(
                        pst[:, :], a16(A_WOW + sl * 1024 + dc * 128, 128),
                        a16(A_OT + tg * 512, 512), start=True, stop=True),
                       rd=[(wos[sl], 0)] + [(otp, q) for q in range(tg * 4, tg * 4 + 4)],
                       wr=[(ppp, 0)])
                    if pair == 0:
                        E_("v", lambda v, dc=dc, tg=tg, pst=pst, bo_t=bo_t:
                           v.scalar_tensor_tensor(
                               res_t[dc][:, tg * 512:(tg + 1) * 512], pst[:, :],
                               bo_t[:, dc:dc + 1],
                               res_t[dc][:, tg * 512:(tg + 1) * 512], OP.add, OP.add),
                           rd=[(ppp, 0), (biasb, B_IDX['l0bo'] if L == 0 else B_IDX['l1bo']), (res[dc], tg)], wr=[(res[dc], tg)])
                    else:
                        E_("v", lambda v, dc=dc, tg=tg, pst=pst: v.tensor_add(
                            res_t[dc][:, tg * 512:(tg + 1) * 512],
                            res_t[dc][:, tg * 512:(tg + 1) * 512], pst[:, :]),
                           rd=[(ppp, 0), (res[dc], tg)], wr=[(res[dc], tg)])

    def phase_moe(cwtb):
        hp = [ar_16.claim(A_V + f_ * 1024, A_V + (f_ + 1) * 1024, f"hp{f_}", 2)
              for f_ in range(FC)]
        w13 = [ar_16.claim(A_W13 + i * 2048, A_W13 + (i + 1) * 2048, f"w13_{i}")
               for i in range(2)]
        w2h = [ar_16.claim(A_W2R + i * 1024, A_W2R + (i + 1) * 1024, f"w2h{i}")
               for i in range(4)]
        cwbb = ar_32.claim(Z_CWB, Z_CWB + 1024, "cwb", 2)
        W13O = [A_W13, A_W13 + 2048]
        W2O = [A_W2R, A_W2R + 2048]

        def ld_w13(e, fc, sl):
            ch = "w13" + str(sl)
            E_("q", lambda q, e=e, fc=fc, sl=sl: q.dma_start(
                a16(W13O[sl], 1024).rearrange("p (k n) -> p k n", n=128),
                bass.AP(dr["w1r"], (e * 32 + fc) * 8 * 128 * 128,
                        [[128, 128], [128 * 128, 8], [1, 128]])),
                wr=[(w13[sl], 0)], dma=ch)
            tok = E_("q", lambda q, e=e, fc=fc, sl=sl: q.dma_start(
                a16(W13O[sl] + 1024, 1024).rearrange("p (k n) -> p k n", n=128),
                bass.AP(dr["w3r"], (e * 32 + fc) * 8 * 128 * 128,
                        [[128, 128], [128 * 128, 8], [1, 128]])),
                wr=[(w13[sl], 0)], dma=ch)
            em.bulk_fixup([w13[sl]], tok)

        def ld_w2q(e, dc, qq):
            sl = qq % 4
            E_("q", lambda q, e=e, dc=dc, qq=qq, sl=sl: q.dma_start(
                a16(A_W2R + sl * 1024, 1024).rearrange("p (k n) -> p k n", n=128),
                bass.AP(dr["w2r"], ((e * 8 + dc) * 32 + qq * 8) * 128 * 128,
                        [[128, 128], [128 * 128, 8], [1, 128]])),
                wr=[(w2h[sl], 0)], dma="w2" + str(sl))

        ld_w13(0, 0, 0)
        step = 0
        for e in range(E):
            for tgi in range(NTG):
                t0 = tgi * 1024
                for half in range(2):
                    pst, ppp = PSBC[half]
                    E_("t", lambda t, e=e, t0=t0, half=half, pst=pst: t.matmul(
                        pst[:, :], oh_t[:, e * 128:(e + 1) * 128],
                        a16(A_CWT + t0 + half * 512, 512, 0, 8),
                        start=True, stop=True),
                       rd=[(consts, C_OH)] + [(cwtb, (t0 + half * 512) // 128 + i)
                                           for i in range(4)],
                       wr=[(ppp, 0)])
                    E_("v", lambda v, half=half, pst=pst: v.tensor_copy(
                        a32(Z_CWB + half * 512, 512), pst[:, :]),
                       rd=[(ppp, 0)], wr=[(cwbb, half)])
                ROT = [(PS_B[:, :], psB, 0), (PS_C[:, :], psC, 0),
                       (PS_A[:, 0:512], psA, 0), (PS_A[:, 512:1024], psA, 1)]
                rot = 0
                for fc in range(FC):
                    sl = step % 2
                    step += 1
                    nxt = step
                    if nxt < E * NTG * FC:
                        ne, rem = divmod(nxt, NTG * FC)
                        _ntg, nfc = divmod(rem, FC)
                        ld_w13(ne, nfc, 1 - sl)
                    if fc >= FC - 4:
                        ld_w2q(e, 0, fc - (FC - 4))
                    for stg in range(2):
                        pstg, pppg, ppg = ROT[rot % 4]
                        pstu, pppu, ppu = ROT[(rot + 1) % 4]
                        rot += 2
                        memg, memu = [], []
                        for c in range(KC):
                            def fg(t, sl=sl, c=c, t0=t0, stg=stg, pstg=pstg):
                                return t.matmul(
                                    pstg, a16(W13O[sl] + c * 128, 128),
                                    xn_t[c][:, t0 + stg * 512:t0 + stg * 512 + 512],
                                    start=(c == 0), stop=(c == KC - 1))
                            memg.append((fg, [(w13[sl], 0),
                                              (xn[c], (t0 + stg * 512) // 512)]))
                            def fu(t, sl=sl, c=c, t0=t0, stg=stg, pstu=pstu):
                                return t.matmul(
                                    pstu, a16(W13O[sl] + 1024 + c * 128, 128),
                                    xn_t[c][:, t0 + stg * 512:t0 + stg * 512 + 512],
                                    start=(c == 0), stop=(c == KC - 1))
                            memu.append((fu, [(w13[sl], 0),
                                              (xn[c], (t0 + stg * 512) // 512)]))
                        G_("t", memg, wr=[(pppg, ppg)])
                        E_("s", lambda s, fc=fc, stg=stg, pstg=pstg: s.activation(
                            a16(A_V + fc * 1024 + stg * 512, 512), pstg, AF.Silu),
                           rd=[(pppg, ppg)], wr=[(hp[fc], stg)])
                        G_("t", memu, wr=[(pppu, ppu)])
                        E_("v", lambda v, fc=fc, stg=stg, pstu=pstu: v.tensor_mul(
                            a16(A_V + fc * 1024 + stg * 512, 512),
                            a16(A_V + fc * 1024 + stg * 512, 512), pstu),
                           rd=[(pppu, ppu), (hp[fc], stg)], wr=[(hp[fc], stg)])
                for dc in range(KC):
                    for stg in range(2):
                        pidx = (dc * 2 + stg) % 4
                        for qq in range(4):
                            sl2 = qq % 4
                            mem = []
                            for j in range(8):
                                fc = qq * 8 + j
                                def fd(t, sl2=sl2, j=j, fc=fc, stg=stg, pidx=pidx,
                                       first=(qq == 0 and j == 0),
                                       last=(qq == 3 and j == 7)):
                                    return t.matmul(
                                        PS_A[:, pidx * 512:pidx * 512 + 512],
                                        a16(A_W2R + sl2 * 1024 + j * 128, 128),
                                        a16(A_V + fc * 1024 + stg * 512, 512),
                                        start=first, stop=last)
                                mem.append((fd, [(w2h[sl2], 0), (hp[fc], stg)]))
                            G_("t", mem, wr=[(psA, pidx)])
                            if stg == 1 and dc + 1 < KC:
                                ld_w2q(e, dc + 1, qq)
                        g4 = (t0 + stg * 512) // 512
                        tsl = (dc * 2 + stg) % 2
                        E_("v", lambda v, pidx=pidx, stg=stg, tsl=tsl: v.tensor_mul(
                            a32(Z_TMP + tsl * 512, 512),
                            PS_A[:, pidx * 512:pidx * 512 + 512],
                            a32(Z_CWB + stg * 512, 512)),
                           rd=[(psA, pidx), (cwbb, stg)], wr=[(tmpf, tsl)])
                        E_("v", lambda v, dc=dc, g4=g4, tsl=tsl: v.tensor_add(
                            res_t[dc][:, g4 * 512:(g4 + 1) * 512],
                            res_t[dc][:, g4 * 512:(g4 + 1) * 512],
                            a32(Z_TMP + tsl * 512, 512)),
                           rd=[(tmpf, tsl), (res[dc], g4)], wr=[(res[dc], g4)])

    def phase_alpha():
        slA = ar_32.claim(Z_ALA, Z_ALA + 1024, "slA")
        tcol = Buf("tcol", 1)
        for i in range(IC):
            E_("y", lambda y, i=i: y.dma_start(
                a32(Z_ALA, 1024), dr["w1n"][i * 128:(i + 1) * 128, :]),
                wr=[(slA, 0)], dma="alA")
            E_("v", lambda v, i=i: v.reduce_sum(
                al1_t[:, i:i + 1], a32(Z_ALA, 1024), axis=AX,
                apply_absolute_value=True), rd=[(slA, 0)], wr=[(al1b, 0)])
        E_("v", lambda v: v.tensor_scalar(al1_t[:, :], al1_t[:, :], 1.0 / D, None,
                                          OP.mult), rd=[(al1b, 0)], wr=[(al1b, 0)])
        slB = ar_32.claim(Z_ALB, Z_ALB + 2048, "slB")
        for i in range(KC):
            for hh in range(2):
                E_("y", lambda y, i=i, hh=hh: y.dma_start(
                    a32(Z_ALB, 2048), dr["w2n"][i * 128:(i + 1) * 128,
                                                hh * 2048:(hh + 1) * 2048]),
                    wr=[(slB, 0)], dma="alB")
                E_("v", lambda v, hh=hh: v.reduce_sum(
                    sc_t[:, S_A2 + hh:S_A2 + hh + 1], a32(Z_ALB, 2048), axis=AX,
                    apply_absolute_value=True), rd=[(slB, 0)], wr=[(tcol, 0)])
            E_("v", lambda v, i=i: v.reduce_sum(
                al2_t[:, i:i + 1], sc_t[:, S_A2:S_A2 + 2], axis=AX),
               rd=[(tcol, 0)], wr=[(al2b, 0)])
        E_("v", lambda v: v.tensor_scalar(al2_t[:, :], al2_t[:, :], 1.0 / F, None,
                                          OP.mult), rd=[(al2b, 0)], wr=[(al2b, 0)])
        if _DBG.get("dump_al"):
            E_("q", lambda q: q.dma_start(dbgT[:, 0:32], al1_t[:, :]),
               rd=[(al1b, 0)], dma="dbg")
            E_("q", lambda q: q.dma_start(dbgT[:, 32:40], al2_t[:, :]),
               rd=[(al2b, 0)], dma="dbg")

    def phase_ffn1():
        hT = [ar_16.claim(A_V + i * 1024, A_V + (i + 1) * 1024, f"hT{i}", 2)
              for i in range(IC)]
        w1t = [ar_16.claim(A_W13 + i * 1024, A_W13 + (i + 1) * 1024, f"w1t{i}")
               for i in range(2)]
        s1t = [ar_16.claim(A_W13 + 2048 + i * 1024, A_W13 + 2048 + (i + 1) * 1024,
                           f"s1t{i}") for i in range(2)]
        w2t = [ar_16.claim(A_W2R + i * 2048, A_W2R + (i + 1) * 2048, f"w2t{i}")
               for i in range(2)]
        W1O = [A_W13, A_W13 + 1024]
        S1O = [A_W13 + 2048, A_W13 + 3072]
        W2TO = [A_W2R, A_W2R + 2048]

        def ld_s1(i, sl):
            E_("q", lambda q, i=i, sl=sl: q.dma_start(
                a16(W1O[sl], 1024).rearrange("p (k n) -> p k n", n=128),
                bass.AP(dr["w1tr"], i * 8 * 128 * 128,
                        [[128, 128], [128 * 128, 8], [1, 128]])),
                wr=[(w1t[sl], 0)], dma="s1" + str(sl))
            E_("s", lambda s, sl=sl: s.activation(
                a16(S1O[sl], 1024), a16(W1O[sl], 1024), AF.Sign),
               rd=[(w1t[sl], 0)], wr=[(s1t[sl], 0)])

        def ld_s2h(dc, hh, sl):
            E_("q", lambda q, dc=dc, hh=hh, sl=sl: q.dma_start(
                a16(W2TO[sl], 2048).rearrange("p (k n) -> p k n", n=128),
                bass.AP(dr["w2tr"], (dc * 32 + hh * 16) * 128 * 128,
                        [[128, 128], [128 * 128, 16], [1, 128]])),
                wr=[(w2t[sl], 0)], dma="s2" + str(sl))
            E_("s", lambda s, sl=sl: s.activation(
                a16(W2TO[sl], 2048), a16(W2TO[sl], 2048), AF.Sign),
               rd=[(w2t[sl], 0)], wr=[(w2t[sl], 0)])

        for tgi in range(NTG):
            t0 = tgi * 1024
            if tgi == 0:
                ld_s1(0, 0)
            for i in range(IC):
                sl = i % 2
                if i + 1 < IC:
                    ld_s1(i + 1, 1 - sl)
                if i == IC - 6:
                    ld_s2h(0, 0, 0)
                elif i == IC - 3:
                    ld_s2h(0, 1, 1)
                for stg in range(2):
                    pst, ppp = PSBC[stg]
                    mem = []
                    for c in range(KC):
                        def fu(t, sl=sl, c=c, t0=t0, stg=stg, pst=pst):
                            return t.matmul(
                                pst[:, :], a16(S1O[sl] + c * 128, 128),
                                xn_t[c][:, t0 + stg * 512:t0 + stg * 512 + 512],
                                start=(c == 0), stop=(c == KC - 1))
                        mem.append((fu, [(s1t[sl], 0),
                                         (xn[c], (t0 + stg * 512) // 512)]))
                    G_("t", mem, wr=[(ppp, 0)])
                    E_("s", lambda s, i=i, stg=stg, pst=pst: s.activation(
                        a16(A_V + i * 1024 + stg * 512, 512), pst[:, :], AF.Gelu,
                        bias=b1c_t[:, i:i + 1], scale=al1_t[:, i:i + 1]),
                       rd=[(ppp, 0), (biasb, B_IDX['b1c']), (al1b, 0)], wr=[(hT[i], stg)])
            for dc in range(KC):
                for stg in range(2):
                    pidx = (dc * 2 + stg) % 4
                    for hh in range(2):
                        sl2 = hh
                        mem = []
                        for j in range(16):
                            i = hh * 16 + j
                            def fd(t, sl2=sl2, j=j, i=i, stg=stg, pidx=pidx,
                                   first=(hh == 0 and j == 0),
                                   last=(hh == 1 and j == 15)):
                                return t.matmul(
                                    PS_A[:, pidx * 512:pidx * 512 + 512],
                                    a16(W2TO[sl2] + j * 128, 128),
                                    a16(A_V + i * 1024 + stg * 512, 512),
                                    start=first, stop=last)
                            mem.append((fd, [(w2t[sl2], 0), (hT[i], stg)]))
                        G_("t", mem, wr=[(psA, pidx)])
                        if stg == 1:
                            if hh == 0 and dc + 1 < KC:
                                ld_s2h(dc + 1, 0, 0)
                            elif hh == 1 and dc + 1 < KC:
                                ld_s2h(dc + 1, 1, 1)
                            elif hh == 1 and dc == KC - 1 and tgi == 0:
                                ld_s1(0, 0)
                    g4 = (t0 + stg * 512) // 512
                    tsl = (dc * 2 + stg) % 2
                    E_("s", lambda s, pidx=pidx, dc=dc, tsl=tsl: s.mul(
                        a32(Z_TMP + tsl * 512, 512),
                        PS_A[:, pidx * 512:pidx * 512 + 512], al2_t[:, dc:dc + 1]),
                       rd=[(psA, pidx), (al2b, 0)], wr=[(tmpf, tsl)])
                    E_("v", lambda v, dc=dc, g4=g4, tsl=tsl: v.scalar_tensor_tensor(
                        res_t[dc][:, g4 * 512:(g4 + 1) * 512],
                        a32(Z_TMP + tsl * 512, 512), bias_t["b2c"][:, dc:dc + 1],
                        res_t[dc][:, g4 * 512:(g4 + 1) * 512], OP.add, OP.add),
                       rd=[(tmpf, tsl), (biasb, B_IDX['b2c']), (res[dc], g4)],
                       wr=[(res[dc], g4)])

    phase_ln(0)
    cwtb = None
    if upto == "a1solo":
        phase_attn(1)
    elif upto != "ln0":
        phase_attn(0)
        cwtb = phase_gate()
    if upto in ("moe0", "ln1", "attn1", "full", "xnl1"):
        phase_moe(cwtb)
    if upto in ("ln1", "attn1", "full", "xnl1"):
        phase_ln(1)
    if upto == "full":
        phase_alpha()
    if upto in ("attn1", "full"):
        phase_attn(1)
    if upto == "full":
        phase_ffn1()

    if upto in ("ln0", "xnl1"):
        for c in range(KC):
            E_("q", lambda q, c=c: q.dma_start(
                outT[c * 128:(c + 1) * 128, :], xn_t[c][:, :]),
               rd=[(xn[c], p) for p in range(4)], dma="dbg")
        em.final_wait("q")
    else:
        for c in range(KC):
            E_("y", lambda y, c=c: y.dma_start(
                outT[c * 128:(c + 1) * 128, :], res_t[c][:, :]),
               rd=[(res[c], p) for p in range(4)], dma="out")
    em.final_wait("y")

    with ExitStack() as st:
        for ch in list(em.ch.values()) + list(em.dch.values()):
            ch.h = st.enter_context(nc.semaphore("ch_" + ch.name))
        with nc.Block() as block:
            @block.tensor
            def _(eng):
                _replay(em, "t", eng)

            @block.vector
            def _(eng):
                _replay(em, "v", eng)

            @block.scalar
            def _(eng):
                _replay(em, "s", eng)

            @block.gpsimd
            def _(eng):
                _replay(em, "q", eng)

            @block.sync
            def _(eng):
                _replay(em, "y", eng)
    return nc, em


_NC_CACHE = {}


def _get_nc(upto="full"):
    if upto not in _NC_CACHE:
        _NC_CACHE[upto] = _build(upto)
    return _NC_CACHE[upto]


def kernel(**inputs):
    x = np.asarray(inputs["L0_attn_x"], dtype=np.float32)
    host = _host_prep(inputs)
    nc, _em = _get_nc("full")
    in_maps = []
    for b in range(NCORE):
        m = {"xT": np.ascontiguousarray(x[b].T)}
        m.update(host)
        in_maps.append(m)
    res = run_bass_kernel_spmd(nc, in_maps, core_ids=list(range(NCORE)))
    out = np.stack([np.ascontiguousarray(r["outT"].T) for r in res.results], axis=0)
    return out


# version 09
# speedup vs baseline: 1.0747x; 1.0047x over previous
import numpy as np
import ml_dtypes
import concourse.bass as bass
import concourse.mybir as mybir
from concourse.bass_utils import run_bass_kernel_spmd
from contextlib import ExitStack

F32 = mybir.dt.float32
BF16 = mybir.dt.bfloat16
AX = mybir.AxisListType.X
OP = mybir.AluOpType
AF = mybir.ActivationFunctionType

D = 1024
T = 2048
KC = 8
TT = 16
NP = 8
E = 8
F = 4096
FC = 32
NTG = 2
IC = 32
NTG4 = 4
NCORE = 8
SCALE = 0.125
_DBG = {}
NEG_BIG = -1.0e9



class Chan:
    __slots__ = ("name", "n", "h")

    def __init__(self, name):
        self.name, self.n, self.h = name, 0, None


class Buf:
    __slots__ = ("name", "w", "r", "birth")

    def __init__(self, name, parts=1, birth=None):
        self.name = name
        self.w = [None] * parts
        self.r = [dict() for _ in range(parts)]
        self.birth = birth or []


class Arena:

    def __init__(self):
        self.live = []

    def claim(self, lo, hi, name, parts=1):
        toks = []
        for a, b, ob in self.live:
            if a < hi and lo < b:
                for p in range(len(ob.w)):
                    if ob.w[p] is not None:
                        toks.append(ob.w[p])
                    toks.extend(ob.r[p].values())
        nb = Buf(name, parts, birth=toks)
        self.live.append((lo, hi, nb))
        return nb


class Em:
    def __init__(self):
        self.ch = {e: Chan(e) for e in "tvsqy"}
        self.dch = {}
        self.ops = {e: [] for e in "tvsqy"}
        self.waited = {e: {} for e in "tvsqy"}
        self.pend = {}
        self.n_inst = 0

    def dma_chan(self, name):
        if name not in self.dch:
            self.dch[name] = Chan(name)
        return self.dch[name]

    def _wait(self, e, tok):
        ch, val = tok
        if self.waited[e].get(ch.name, 0) >= val:
            return
        assert val <= ch.n, f"wait on future token {ch.name}:{val} > {ch.n}"
        self.waited[e][ch.name] = val
        self.ops[e].append((0, ch, val))

    def _deps(self, rd, wr):
        deps = []
        for b, p in rd:
            t = b.w[p]
            if t is not None:
                deps.append(t)
        for b, p in wr:
            deps.extend(b.birth)
            if b.w[p] is not None:
                deps.append(b.w[p])
            deps.extend(b.r[p].values())
        return deps

    def _upd(self, rd, wr, tok):
        for b, p in rd:
            b.r[p][tok[0].name] = tok
        for b, p in wr:
            b.w[p] = tok
            b.r[p] = {}

    def emit(self, e, fn, rd=(), wr=(), dma=None, inc=True):
        for t in self._deps(rd, wr):
            self._wait(e, t)
        if not inc:
            self.ops[e].append((1, fn, None, 0))
            self.n_inst += 1
            self.pend.setdefault(e, []).append((list(rd), list(wr)))
            return None
        if dma is not None:
            ch = self.dma_chan(dma) if isinstance(dma, str) else dma
            amt = 16
        else:
            ch = self.ch[e]
            amt = 1
        ch.n += amt
        self.ops[e].append((1, fn, ch, amt))
        self.n_inst += 1
        tok = (ch, ch.n)
        for prd, pwr in self.pend.pop(e, []):
            self._upd(prd, pwr, tok)
        self._upd(rd, wr, tok)
        return tok

    def bulk_fixup(self, bufs, tok):
        for b in bufs:
            for p in range(len(b.w)):
                if b.w[p] is not None:
                    b.w[p] = tok

    def emit_group(self, e, members, wr=()):
        all_rd = []
        for fn, rd in members:
            for t in self._deps(rd, wr):
                self._wait(e, t)
            all_rd.extend(rd)
        ch = self.ch[e]
        for i, (fn, rd) in enumerate(members):
            last = i == len(members) - 1
            if last:
                ch.n += 1
            self.ops[e].append((1, fn, ch if last else None, 1))
            self.n_inst += 1
        tok = (ch, ch.n)
        self._upd(all_rd, wr, tok)
        return tok

    def final_wait(self, e):
        for ch in list(self.ch.values()) + list(self.dch.values()):
            if ch.n > 0:
                self.ops[e].append((0, ch, ch.n))


def _replay(em, e, eng):
    for entry in em.ops[e]:
        if entry[0] == 0:
            _, ch, val = entry
            eng.wait_ge(ch.h, val)
        else:
            _, fn, ch, amt = entry
            ins = fn(eng)
            if ch is not None:
                ins.then_inc(ch.h, amt)



def _pair_blocks(w):
    return np.ascontiguousarray(w.reshape(8, 128, 8, 128).transpose(2, 0, 1, 3))


def _cols(b, n):
    return np.ascontiguousarray(b.reshape(n, 128).T)


def _host_prep(inputs):
    f32 = lambda a: np.ascontiguousarray(np.asarray(a, dtype=np.float32))
    d = {}
    d["l0q"] = _pair_blocks(f32(inputs["L0_attn_Wq"]))
    d["l0k"] = _pair_blocks(f32(inputs["L0_attn_Wk"]))
    d["l0v"] = f32(inputs["L0_attn_Wv"])
    d["l0o"] = f32(inputs["L0_attn_Wo"])
    d["l0bo"] = _cols(f32(inputs["L0_attn_bo"]), 8)
    d["wg"] = np.ascontiguousarray(
        f32(inputs["L0_ffn_Wg"]).reshape(8, 128, 8).transpose(1, 0, 2).reshape(128, 64))
    d["bg"] = f32(inputs["L0_ffn_bg"]).reshape(1, 8)
    d["w1r"] = np.ascontiguousarray(
        f32(inputs["L0_ffn_W1"]).reshape(8, 8, 128, 32, 128).transpose(0, 3, 1, 2, 4))
    d["w3r"] = np.ascontiguousarray(
        f32(inputs["L0_ffn_W3"]).reshape(8, 8, 128, 32, 128).transpose(0, 3, 1, 2, 4))
    d["w2r"] = np.ascontiguousarray(
        f32(inputs["L0_ffn_W2"]).reshape(8, 32, 128, 8, 128).transpose(0, 3, 1, 2, 4))
    d["l1q"] = _pair_blocks(f32(inputs["L1_attn_Wq"]).T)
    d["l1k"] = _pair_blocks(f32(inputs["L1_attn_Wk"]).T)
    d["l1v"] = np.ascontiguousarray(f32(inputs["L1_attn_Wv"]).T)
    d["l1o"] = np.ascontiguousarray(f32(inputs["L1_attn_Wo"]).T)
    d["l1bq"] = _cols(f32(inputs["L1_attn_bq"]), 8)
    d["l1bk"] = _cols(f32(inputs["L1_attn_bk"]), 8)
    d["l1bv"] = f32(inputs["L1_attn_bv"]).reshape(1, 1024)
    d["l1bo"] = _cols(f32(inputs["L1_attn_bo"]), 8)
    d["w1n"] = f32(inputs["L1_ffn_w1"])
    d["w1tr"] = np.ascontiguousarray(
        f32(inputs["L1_ffn_w1"]).T.reshape(8, 128, 32, 128).transpose(2, 0, 1, 3))
    d["w2n"] = f32(inputs["L1_ffn_w2"])
    d["w2tr"] = np.ascontiguousarray(
        f32(inputs["L1_ffn_w2"]).T.reshape(32, 128, 8, 128).transpose(2, 0, 1, 3))
    d["b1c"] = _cols(f32(inputs["L1_ffn_b1"]), 32)
    d["b2c"] = _cols(f32(inputs["L1_ffn_b2"]), 8)
    return d


_DRAM_SPECS = [
    ("l0q", [8, 8, 128, 128]), ("l0k", [8, 8, 128, 128]),
    ("l0v", [1024, 1024]), ("l0o", [1024, 1024]), ("l0bo", [128, 8]),
    ("wg", [128, 64]), ("bg", [1, 8]),
    ("w1r", [8, 32, 8, 128, 128]), ("w3r", [8, 32, 8, 128, 128]),
    ("w2r", [8, 8, 32, 128, 128]),
    ("l1q", [8, 8, 128, 128]), ("l1k", [8, 8, 128, 128]),
    ("l1v", [1024, 1024]), ("l1o", [1024, 1024]),
    ("l1bq", [128, 8]), ("l1bk", [128, 8]), ("l1bv", [1, 1024]), ("l1bo", [128, 8]),
    ("w1n", [4096, 1024]), ("w1tr", [32, 8, 128, 128]),
    ("w2n", [1024, 4096]), ("w2tr", [8, 32, 128, 128]),
    ("b1c", [128, 32]), ("b2c", [128, 8]),
]

A_V = 0
A_QT = 16384
A_KT = 18432
A_OT = 20480
A_P = 22528
A_P2 = 24576
A_P3 = 33792
A_PT = 26624
A_QKW = 27648
A_WOW = 31744
A_W13 = 32768
A_W2R = 36864
A_CWT = 40960

Z_R1 = 0
Z_R2 = 2048
Z_TMP = 4096
Z_CWB = 0
Z_BVR = 1024
Z_ALA = 2048
Z_ALB = 2048


def _build(upto="full"):
    nc = bass.Bass("TRN2", target_bir_lowering=False)
    em = Em()
    ar_16 = Arena()
    ar_32 = Arena()

    xT = nc.dram_tensor("xT", [D, T], F32, kind="ExternalInput")
    outT = nc.dram_tensor("outT", [D, T], F32, kind="ExternalOutput")
    dbgT = nc.dram_tensor("dbgT", [128, 64], F32, kind="ExternalOutput") if _DBG else None
    dr = {n: nc.dram_tensor(n, s, F32, kind="ExternalInput") for n, s in _DRAM_SPECS}

    ident_bf = nc.inline_tensor(np.eye(128, dtype=ml_dtypes.bfloat16), name="ident_bf")
    maskc = np.zeros((128, 128), np.float32)
    maskc[np.triu_indices(128, 1)] = NEG_BIG
    mask_f = nc.inline_tensor(maskc, name="mask_f")
    onerow_f = nc.inline_tensor(np.ones((1, 128), np.float32), name="onerow_f")
    onecol_f = nc.inline_tensor(np.ones((128, 1), np.float32), name="onecol_f")
    ohm = np.zeros((8, 8 * 128), ml_dtypes.bfloat16)
    for e_ in range(8):
        ohm[e_, e_ * 128:(e_ + 1) * 128] = 1
    oh_bf = nc.inline_tensor(ohm, name="oh_bf")

    res_t = [nc.alloc_sbuf_tensor(f"res{c}", [128, T], F32) for c in range(KC)]
    xn_t = [nc.alloc_sbuf_tensor(f"xn{c}", [128, T], BF16) for c in range(KC)]
    AR16 = nc.alloc_sbuf_tensor("AR16", [128, 43008], BF16)
    AR32 = nc.alloc_sbuf_tensor("AR32", [128, 5120], F32)
    cwf_t = nc.alloc_sbuf_tensor("cwf", [128, 128], F32)
    cwbf_t = nc.alloc_sbuf_tensor("cwbf", [128, 128], BF16)
    idb_t = nc.alloc_sbuf_tensor("idb", [128, 128], BF16)
    mask_t = nc.alloc_sbuf_tensor("maskt", [128, 128], F32)
    onerow_t = nc.alloc_sbuf_tensor("onerow", [1, 128], F32)
    onecol_t = nc.alloc_sbuf_tensor("onecol", [128, 1], F32)
    oh_t = nc.alloc_sbuf_tensor("oht", [8, 8 * 128], BF16)
    wg_t = nc.alloc_sbuf_tensor("wgt", [128, 64], BF16)
    bgb_t = nc.alloc_sbuf_tensor("bgb", [128, 8], F32)
    bgrow_t = nc.alloc_sbuf_tensor("bgrow", [1, 8], F32)
    sc_t = nc.alloc_sbuf_tensor("sc", [128, 48], F32)
    bias_t = {n: nc.alloc_sbuf_tensor("b_" + n, [128, 8], F32)
              for n in ("l0bo", "l1bq", "l1bk", "l1bo", "b2c")}
    b1c_t = nc.alloc_sbuf_tensor("b_b1c", [128, 32], F32)
    al1_t = nc.alloc_sbuf_tensor("al1", [128, 32], F32)
    al2_t = nc.alloc_sbuf_tensor("al2", [128, 8], F32)

    PS_A = nc.alloc_psum_tensor("PS_A", [128, 2048], F32)
    PS_B = nc.alloc_psum_tensor("PS_B", [128, 512], F32)
    PS_C = nc.alloc_psum_tensor("PS_C", [128, 512], F32)
    PT_T = [nc.alloc_psum_tensor("PT_A", [128, 512], BF16),
            nc.alloc_psum_tensor("PT_B", [128, 512], BF16)]

    res = [Buf(f"res{c}", 4) for c in range(KC)]
    xn = [Buf(f"xn{c}", 4) for c in range(KC)]
    psA = Buf("psA", 4)
    psB = Buf("psB", 1)
    psC = Buf("psC", 1)
    ptb = [Buf("ptA", 4), Buf("ptB", 4)]
    consts = Buf("consts", 6)
    C_ID, C_MASK, C_ROW, C_COL, C_OH, C_WG = range(6)
    biasb = Buf("biasb", 6)
    B_IDX = {"l0bo": 0, "l1bq": 1, "l1bk": 2, "l1bo": 3, "b2c": 4, "b1c": 5}
    scb = Buf("scb", 16)
    tmpf = Buf("tmpf", 2)
    cwfb = Buf("cwf", 16)
    cwbfb = Buf("cwbf", 1)
    al1b = Buf("al1", 1)
    al2b = Buf("al2", 1)
    bgrb = Buf("bgrow", 1)
    bgbb = Buf("bgb", 1)
    wgb = Buf("wgb", 1)

    S_GL, S_E8, S_MS, S_CW = 0, 8, 16, 24
    S_MC = 32
    S_MX, S_NM, S_L, S_RL = 36, 37, 38, 39
    S_G1, S_G2, S_G3, S_G4 = 40, 41, 42, 43
    S_A2 = 44

    def a16(off, ln, p0=0, p1=128):
        return AR16[p0:p1, off:off + ln]

    def a32(off, ln, p0=0, p1=128):
        return AR32[p0:p1, off:off + ln]

    E_ = em.emit
    G_ = em.emit_group
    PSBC = [(PS_B, psB), (PS_C, psC)]

    E_("y", lambda y: y.dma_start(idb_t[:, :], ident_bf[:, :]), wr=[(consts, C_ID)], dma="cst")
    E_("y", lambda y: y.dma_start(mask_t[:, :], mask_f[:, :]), wr=[(consts, C_MASK)], dma="cst")
    E_("y", lambda y: y.dma_start(onerow_t[:, :], onerow_f[:, :]), wr=[(consts, C_ROW)], dma="cst")
    E_("y", lambda y: y.dma_start(onecol_t[:, :], onecol_f[:, :]), wr=[(consts, C_COL)], dma="cst")
    E_("y", lambda y: y.dma_start(oh_t[:, :], oh_bf[:, :]), wr=[(consts, C_OH)], dma="cst")
    E_("q", lambda q: q.dma_start(wg_t[:, :], dr["wg"][:, :]), wr=[(wgb, 0)], dma="cwg")
    for n in ("l0bo", "l1bq", "l1bk", "l1bo", "b2c"):
        E_("y", lambda y, n=n: y.dma_start(bias_t[n][:, :], dr[n][:, :]),
           wr=[(biasb, B_IDX[n])], dma="cst")
    E_("y", lambda y: y.dma_start(b1c_t[:, :], dr["b1c"][:, :]), wr=[(biasb, B_IDX["b1c"])], dma="cst")
    E_("y", lambda y: y.dma_start(bgrow_t[:, :], dr["bg"][:, :]), wr=[(bgrb, 0)], dma="cst")
    em.bulk_fixup([consts, biasb, bgrb], (em.dma_chan("cst"), em.dma_chan("cst").n))
    for c in range(KC):
        E_("y", lambda y, c=c: y.dma_start(res_t[c][:, :], xT[c * 128:(c + 1) * 128, :]),
           wr=[(res[c], p) for p in range(4)], dma="xin")
    em.bulk_fixup(res, (em.dma_chan("xin"), em.dma_chan("xin").n))
    E_("t", lambda t: t.matmul(PS_B[:, 0:8], onerow_t[:, :], bgrow_t[:, :],
                               start=True, stop=True),
       rd=[(consts, C_ROW), (bgrb, 0)], wr=[(psB, 0)])
    E_("v", lambda v: v.tensor_copy(bgb_t[:, :], PS_B[:, 0:8]),
       rd=[(psB, 0)], wr=[(bgbb, 0)])

    def phase_ln(L):
        r1 = ar_32.claim(Z_R1, Z_R1 + 2048, f"r1L{L}")
        r2 = ar_32.claim(Z_R2, Z_R2 + 2048, f"r2L{L}")
        for g in range(NTG4):
            for c in range(KC):
                E_("t", lambda t, c=c, g=g: t.matmul(
                    PS_A[0:1, g * 512:(g + 1) * 512], onecol_t[:, :],
                    res_t[c][:, g * 512:(g + 1) * 512],
                    start=(c == 0), stop=(c == KC - 1)),
                   rd=[(consts, C_COL), (res[c], g)],
                   wr=[(psA, g)] if c in (0, KC - 1) else [])
        for g in range(NTG4):
            E_("v", lambda v, g=g: v.tensor_scalar(
                a32(Z_R1 + g * 512, 512, 0, 1), PS_A[0:1, g * 512:(g + 1) * 512],
                1.0 / D, None, OP.mult), rd=[(psA, g)], wr=[(r1, 0)])
        for g in range(NTG4):
            for c in range(KC):
                sl = c % 2
                E_("s", lambda s, c=c, g=g, sl=sl: s.square(
                    a32(Z_TMP + sl * 512, 512), res_t[c][:, g * 512:(g + 1) * 512]),
                    rd=[(res[c], g)], wr=[(tmpf, sl)])
                E_("t", lambda t, c=c, g=g, sl=sl: t.matmul(
                    PS_A[32:33, g * 512:(g + 1) * 512], onecol_t[:, :],
                    a32(Z_TMP + sl * 512, 512),
                    start=(c == 0), stop=(c == KC - 1)),
                   rd=[(consts, C_COL), (tmpf, sl)],
                   wr=[(psA, g)] if c in (0, KC - 1) else [])
        for g in range(NTG4):
            E_("v", lambda v, g=g: v.tensor_scalar(
                a32(Z_R2 + g * 512, 512, 0, 1), PS_A[32:33, g * 512:(g + 1) * 512],
                1.0 / D, 1e-5, OP.mult, OP.add), rd=[(psA, g)], wr=[(r2, 0)])
        for hh in range(2):
            E_("v", lambda v, hh=hh: v.tensor_mul(
                a32(Z_TMP, 1024, 0, 1), a32(Z_R1 + hh * 1024, 1024, 0, 1),
                a32(Z_R1 + hh * 1024, 1024, 0, 1)),
               rd=[(r1, 0)], wr=[(tmpf, 0), (tmpf, 1)])
            E_("v", lambda v, hh=hh: v.tensor_sub(
                a32(Z_R2 + hh * 1024, 1024, 0, 1), a32(Z_R2 + hh * 1024, 1024, 0, 1),
                a32(Z_TMP, 1024, 0, 1)),
               rd=[(r2, 0), (tmpf, 0), (tmpf, 1)], wr=[(r2, 0)])
        E_("s", lambda s: s.activation(a32(Z_R2, T, 0, 1), a32(Z_R2, T, 0, 1),
                                       AF.Sqrt), rd=[(r2, 0)], wr=[(r2, 0)])
        E_("v", lambda v: v.reciprocal(a32(Z_R2, T, 0, 1), a32(Z_R2, T, 0, 1)),
           rd=[(r2, 0)], wr=[(r2, 0)])
        E_("v", lambda v: v.scalar_tensor_tensor(
            a32(Z_R1, T, 0, 1), a32(Z_R1, T, 0, 1), -1.0, a32(Z_R2, T, 0, 1),
            OP.mult, OP.mult), rd=[(r1, 0), (r2, 0)], wr=[(r1, 0)])
        for g in range(NTG4):
            nbt, nb = PSBC[g % 2]
            E_("t", lambda t, g=g: t.matmul(
                PS_A[:, g * 512:(g + 1) * 512], onerow_t[:, :],
                a32(Z_R2 + g * 512, 512, 0, 1), start=True, stop=True),
               rd=[(consts, C_ROW), (r2, 0)], wr=[(psA, g)])
            E_("t", lambda t, g=g, nbt=nbt: t.matmul(
                nbt[:, :], onerow_t[:, :], a32(Z_R1 + g * 512, 512, 0, 1),
                start=True, stop=True),
               rd=[(consts, C_ROW), (r1, 0)], wr=[(nb, 0)])
            for c in range(KC):
                sl = c % 2
                E_("v", lambda v, c=c, g=g, sl=sl: v.tensor_mul(
                    a32(Z_TMP + sl * 512, 512), res_t[c][:, g * 512:(g + 1) * 512],
                    PS_A[:, g * 512:(g + 1) * 512]),
                    rd=[(res[c], g), (psA, g)], wr=[(tmpf, sl)])
                E_("v", lambda v, c=c, g=g, sl=sl, nbt=nbt: v.tensor_add(
                    xn_t[c][:, g * 512:(g + 1) * 512], a32(Z_TMP + sl * 512, 512),
                    nbt[:, :]), rd=[(tmpf, sl), (nb, 0)], wr=[(xn[c], g)])

    def phase_gate():
        cwtb = ar_16.claim(A_CWT, A_CWT + 2048, "cwT", 16)
        for tt in range(TT):
            pst, ppp = PSBC[tt % 2]
            mem = []
            for c in range(KC):
                def f(t, c=c, tt=tt, pst=pst):
                    return t.matmul(pst[:, 0:8], xn_t[c][:, tt * 128:(tt + 1) * 128],
                                    wg_t[:, c * 8:(c + 1) * 8],
                                    start=(c == 0), stop=(c == KC - 1))
                mem.append((f, [(xn[c], tt // 4), (wgb, 0)]))
            G_("t", mem, wr=[(ppp, 0)])
            E_("v", lambda v, pst=pst: v.tensor_add(
                sc_t[:, S_GL:S_GL + 8], bgb_t[:, :], pst[:, 0:8]),
                rd=[(ppp, 0), (bgbb, 0)], wr=[(scb, 0)])
            E_("v", lambda v: v.reduce_max(sc_t[:, S_G1:S_G1 + 1],
                                           sc_t[:, S_GL:S_GL + 8], axis=AX, negate=True),
               rd=[(scb, 0)], wr=[(scb, 4)])
            E_("s", lambda s: s.activation(sc_t[:, S_E8:S_E8 + 8],
                                           sc_t[:, S_GL:S_GL + 8], AF.Exp,
                                           bias=sc_t[:, S_G1:S_G1 + 1]),
               rd=[(scb, 0), (scb, 4)], wr=[(scb, 1)])
            E_("v", lambda v: v.reduce_max(sc_t[:, S_G2:S_G2 + 1],
                                           sc_t[:, S_E8:S_E8 + 8], axis=AX),
               rd=[(scb, 1)], wr=[(scb, 5)])
            E_("v", lambda v: v.tensor_scalar(
                sc_t[:, S_MS:S_MS + 8], sc_t[:, S_E8:S_E8 + 8],
                sc_t[:, S_G2:S_G2 + 1], NEG_BIG, OP.is_ge, OP.mult),
               rd=[(scb, 1), (scb, 5)], wr=[(scb, 2)])
            E_("v", lambda v: v.tensor_add(sc_t[:, S_MS:S_MS + 8],
                                           sc_t[:, S_MS:S_MS + 8],
                                           sc_t[:, S_E8:S_E8 + 8]),
               rd=[(scb, 2), (scb, 1)], wr=[(scb, 2)])
            E_("v", lambda v: v.reduce_max(sc_t[:, S_G3:S_G3 + 1],
                                           sc_t[:, S_MS:S_MS + 8], axis=AX),
               rd=[(scb, 2)], wr=[(scb, 6)])
            E_("v", lambda v: v.tensor_scalar(
                sc_t[:, S_CW:S_CW + 8], sc_t[:, S_E8:S_E8 + 8],
                sc_t[:, S_G3:S_G3 + 1], None, OP.is_ge),
               rd=[(scb, 1), (scb, 6)], wr=[(scb, 3)])
            E_("v", lambda v: v.tensor_mul(sc_t[:, S_CW:S_CW + 8],
                                           sc_t[:, S_CW:S_CW + 8],
                                           sc_t[:, S_E8:S_E8 + 8]),
               rd=[(scb, 3), (scb, 1)], wr=[(scb, 3)])
            E_("v", lambda v: v.reduce_sum(sc_t[:, S_G4:S_G4 + 1],
                                           sc_t[:, S_CW:S_CW + 8], axis=AX),
               rd=[(scb, 3)], wr=[(scb, 7)])
            E_("v", lambda v: v.reciprocal(sc_t[:, S_G4:S_G4 + 1],
                                           sc_t[:, S_G4:S_G4 + 1]),
               rd=[(scb, 7)], wr=[(scb, 7)])
            E_("v", lambda v, tt=tt: v.tensor_scalar(
                cwf_t[:, tt * 8:(tt + 1) * 8], sc_t[:, S_CW:S_CW + 8],
                sc_t[:, S_G4:S_G4 + 1], None, OP.mult),
               rd=[(scb, 3), (scb, 7)], wr=[(cwfb, tt)])
        E_("v", lambda v: v.tensor_copy(cwbf_t[:, :], cwf_t[:, :]),
           rd=[(cwfb, i) for i in range(TT)], wr=[(cwbfb, 0)])
        for tt in range(TT):
            k = tt % 2
            E_("t", lambda t, tt=tt, k=k: t.transpose(
                PT_T[k][0:8, 0:128], cwbf_t[:, tt * 8:(tt + 1) * 8], idb_t[:, :]),
               rd=[(cwbfb, 0), (consts, C_ID)], wr=[(ptb[k], 0)])
            E_("v", lambda v, tt=tt, k=k: v.tensor_copy(
                a16(A_CWT + tt * 128, 128, 0, 8), PT_T[k][0:8, 0:128]),
               rd=[(ptb[k], 0)], wr=[(cwtb, tt)])
        return cwtb

    def phase_attn(L):
        slab_off = [A_QKW, A_QKW + 1024, A_QKW + 2048, A_QKW + 3072,
                    A_WOW, A_WOW + 1024, A_P, A_P + 1024]
        wvs = [ar_16.claim(slab_off[c], slab_off[c] + 1024, f"wv{L}_{c}")
               for c in range(KC)]
        vbufs = [ar_16.claim(A_V + tt * 1024, A_V + (tt + 1) * 1024, f"V{L}_{tt}")
                 for tt in range(TT)]
        wv_d = dr["l0v"] if L == 0 else dr["l1v"]
        if L == 1:
            bvrb = ar_32.claim(Z_BVR, Z_BVR + 1024, f"bvr{L}")
            bvbb = ar_32.claim(Z_CWB, Z_CWB + 1024, f"bvb{L}")
            E_("y", lambda y: y.dma_start(a32(Z_BVR, 1024, 0, 1), dr["l1bv"][:, :]),
               wr=[(bvrb, 0)], dma="bvr")
            for half in range(2):
                pst, ppp = PSBC[half]
                E_("t", lambda t, half=half, pst=pst: t.matmul(
                    pst[:, :], onerow_t[:, :], a32(Z_BVR + half * 512, 512, 0, 1),
                    start=True, stop=True),
                   rd=[(consts, C_ROW), (bvrb, 0)], wr=[(ppp, 0)])
                E_("v", lambda v, half=half, pst=pst: v.tensor_copy(
                    a32(Z_CWB + half * 512, 512), pst[:, :]),
                   rd=[(ppp, 0)], wr=[(bvbb, 0)])
        for c in range(KC):
            E_("q", lambda q, c=c: q.dma_start(
                a16(slab_off[c], 1024), wv_d[c * 128:(c + 1) * 128, :]),
                wr=[(wvs[c], 0)], dma="wv")
        em.bulk_fixup(wvs, (em.dma_chan("wv"), em.dma_chan("wv").n))
        for tt in range(TT):
            half = tt % 2
            mem = []
            for c in range(KC):
                for dv in range(2):
                    def f(t, tt=tt, c=c, half=half, dv=dv):
                        return t.matmul(
                            PS_A[:, half * 1024 + dv * 512:half * 1024 + dv * 512 + 512],
                            xn_t[c][:, tt * 128:(tt + 1) * 128],
                            a16(slab_off[c] + dv * 512, 512),
                            start=(c == 0), stop=(c == KC - 1))
                    mem.append((f, [(xn[c], tt // 4), (wvs[c], 0)]))
            G_("t", mem, wr=[(psA, 2 * half), (psA, 2 * half + 1)])
            if L == 0:
                E_("v", lambda v, tt=tt, half=half: v.tensor_copy(
                    a16(A_V + tt * 1024, 1024), PS_A[:, half * 1024:half * 1024 + 1024]),
                   rd=[(psA, 2 * half), (psA, 2 * half + 1)], wr=[(vbufs[tt], 0)])
            else:
                E_("v", lambda v, tt=tt, half=half: v.tensor_add(
                    a16(A_V + tt * 1024, 1024), a32(Z_CWB, 1024),
                    PS_A[:, half * 1024:half * 1024 + 1024]),
                   rd=[(psA, 2 * half), (psA, 2 * half + 1), (bvbb, 0)],
                   wr=[(vbufs[tt], 0)])

        qtp = ar_16.claim(A_QT, A_QT + 2048, f"qt{L}", 4)
        ktp = ar_16.claim(A_KT, A_KT + 2048, f"kt{L}", 4)
        otp = ar_16.claim(A_OT, A_OT + 2048, f"ot{L}", 16)
        pb = [ar_16.claim(A_P, A_P + 2048, f"P{L}a", 16),
              ar_16.claim(A_P2, A_P2 + 2048, f"P{L}b", 16),
              ar_16.claim(A_P3, A_P3 + 2048, f"P{L}c", 16)]
        PO = [A_P, A_P2, A_P3]
        pts = [ar_16.claim(A_PT + i * 512, A_PT + (i + 1) * 512, f"pts{L}{i}")
               for i in range(2)]
        wqs = [ar_16.claim(A_QKW + i * 2048, A_QKW + i * 2048 + 1024, f"wq{L}{i}")
               for i in range(2)]
        wks = [ar_16.claim(A_QKW + i * 2048 + 1024, A_QKW + (i + 1) * 2048, f"wk{L}{i}")
               for i in range(2)]
        wos = [ar_16.claim(A_WOW + i * 1024, A_WOW + (i + 1) * 1024, f"wo{L}{i}")
               for i in range(2)]
        mcolb = Buf("mcol", 1)
        mxb = Buf("mx", 1)
        nmb = Buf("nm", 1)
        lb = Buf("l", 1)
        rlb = Buf("rl", 1)

        wq_d = dr["l0q"] if L == 0 else dr["l1q"]
        wk_d = dr["l0k"] if L == 0 else dr["l1k"]
        wo_d = dr["l0o"] if L == 0 else dr["l1o"]
        bo_t = bias_t["l0bo"] if L == 0 else bias_t["l1bo"]

        def ld_qk(pair, sl):
            ch = "wq" + str(sl)
            E_("q", lambda q, pair=pair, sl=sl: q.dma_start(
                a16(A_QKW + sl * 2048, 1024).rearrange("p (k n) -> p k n", n=128),
                bass.AP(wq_d, pair * (8 * 128 * 128),
                        [[128, 128], [128 * 128, 8], [1, 128]])),
                wr=[(wqs[sl], 0)], dma=ch)
            tok = E_("q", lambda q, pair=pair, sl=sl: q.dma_start(
                a16(A_QKW + sl * 2048 + 1024, 1024).rearrange("p (k n) -> p k n", n=128),
                bass.AP(wk_d, pair * (8 * 128 * 128),
                        [[128, 128], [128 * 128, 8], [1, 128]])),
                wr=[(wks[sl], 0)], dma=ch)
            em.bulk_fixup([wqs[sl], wks[sl]], tok)

        def ld_wo(pair, sl):
            E_("q", lambda q, pair=pair, sl=sl: q.dma_start(
                a16(A_WOW + sl * 1024, 1024), wo_d[pair * 128:(pair + 1) * 128, :]),
                wr=[(wos[sl], 0)], dma="wo" + str(sl))

        ld_qk(0, 0)
        ld_wo(0, 0)

        for pair in range(NP):
            sl = pair % 2
            if pair + 1 < NP:
                ld_qk(pair + 1, 1 - sl)
                ld_wo(pair + 1, 1 - sl)
            for which, wbuf, dstoff, dstb, bcol in (
                    (0, wqs[sl], A_QT, qtp, "l1bq"), (1, wks[sl], A_KT, ktp, "l1bk")):
                for tg in range(NTG4):
                    pst, ppp = PSBC[tg % 2]
                    mem = []
                    for c in range(KC):
                        def f(t, which=which, sl=sl, c=c, tg=tg, pst=pst):
                            return t.matmul(
                                pst[:, :],
                                a16(A_QKW + sl * 2048 + which * 1024 + c * 128, 128),
                                xn_t[c][:, tg * 512:(tg + 1) * 512],
                                start=(c == 0), stop=(c == KC - 1))
                        mem.append((f, [(wbuf, 0), (xn[c], tg)]))
                    G_("t", mem, wr=[(ppp, 0)])
                    if L == 0:
                        E_("v", lambda v, dstoff=dstoff, tg=tg, pst=pst: v.tensor_copy(
                            a16(dstoff + tg * 512, 512), pst[:, :]),
                           rd=[(ppp, 0)], wr=[(dstb, tg)])
                    else:
                        E_("v", lambda v, dstoff=dstoff, tg=tg, pst=pst, bcol=bcol,
                           pair=pair: v.tensor_scalar(
                               a16(dstoff + tg * 512, 512), pst[:, :],
                               bias_t[bcol][:, pair:pair + 1], None, OP.add),
                           rd=[(ppp, 0), (biasb, B_IDX[bcol])], wr=[(dstb, tg)])

            def emit_S(qi, h, soff, ps):
                hp = h & 1
                nk = (qi + 1) * 128
                nkb = (nk + 511) // 512
                for kb in range(nkb):
                    n = min(512, nk - kb * 512)
                    E_("t", lambda t, qi=qi, hp=hp, kb=kb, n=n, soff=soff: t.matmul(
                        PS_A[:, soff + kb * 512:soff + kb * 512 + n],
                        a16(A_QT + qi * 128, 128, 64 * hp, 64 * hp + 64),
                        a16(A_KT + kb * 512, n, 64 * hp, 64 * hp + 64),
                        start=True, stop=True),
                       rd=[(qtp, qi // 4), (ktp, kb)], wr=[(psA, soff // 512 + kb)])
                dkb = (soff + qi * 128) // 512
                E_("v", lambda v, qi=qi, soff=soff: v.tensor_add(
                    PS_A[:, soff + qi * 128:soff + (qi + 1) * 128],
                    PS_A[:, soff + qi * 128:soff + (qi + 1) * 128],
                    mask_t[:, :]), rd=[(consts, C_MASK)], wr=[(psA, dkb)])
                E_("s", lambda s, nk=nk, ps=ps, soff=soff: s.activation(
                    a16(PO[ps], nk), PS_A[:, soff:soff + nk], AF.Exp,
                    scale=SCALE, accum_out=sc_t[:, S_L:S_L + 1]),
                   rd=[(psA, soff // 512 + kb) for kb in range(nkb)],
                   wr=[(pb[ps], k) for k in range(qi + 1)] + [(lb, 0)])
                E_("v", lambda v: v.reciprocal(sc_t[:, S_RL:S_RL + 1],
                                               sc_t[:, S_L:S_L + 1]),
                   rd=[(lb, 0)], wr=[(rlb, 0)])
                E_("v", lambda v, nk=nk, ps=ps: v.tensor_scalar(
                    a16(PO[ps], nk), a16(PO[ps], nk), sc_t[:, S_RL:S_RL + 1], None,
                    OP.mult),
                   rd=[(rlb, 0)] + [(pb[ps], k) for k in range(qi + 1)],
                   wr=[(pb[ps], k) for k in range(qi + 1)])
                if _DBG.get("dumpP") == (pair, qi, h):
                    E_("q", lambda q, nk=nk, hp=hp: q.dma_start(
                        outT[0:128, 0:nk], a16(PO[hp], nk)),
                       rd=[(pb[hp], k) for k in range(qi + 1)], dma="dbg")
                    E_("q", lambda q: q.dma_start(
                        outT[128:256, 0:1024], a16(A_V + 8 * 1024, 1024)),
                       rd=[(vbufs[8], 0)], dma="dbg")
                    E_("q", lambda q: q.dma_start(
                        outT[256:384, 0:1024], a16(A_KT, 1024, 0, 128)),
                       rd=[(ktp, 0), (ktp, 1)], dma="dbg")
                    E_("q", lambda q: q.dma_start(
                        outT[384:512, 0:1024], a16(A_KT + 1024, 1024, 0, 128)),
                       rd=[(ktp, 2), (ktp, 3)], dma="dbg")
                    E_("q", lambda q, qi=qi: q.dma_start(
                        outT[512:640, 0:128], a16(A_QT + qi * 128, 128)),
                       rd=[(qtp, qi // 4)], dma="dbg")

            def emit_TPV(qi, h, ps):
                hp = h & 1
                otps, otpb2 = PSBC[qi % 2]
                nkc = qi + 1
                batches = [(b4 * 4, min(nkc, b4 * 4 + 4)) for b4 in range((nkc + 3) // 4)]

                def tp_batch(j):
                    lo, hi = batches[j]
                    k = j % 2
                    for kc in range(lo, hi):
                        E_("t", lambda t, kc=kc, k=k, ps=ps, lo=lo: t.transpose(
                            PT_T[k][:, (kc - lo) * 128:(kc - lo) * 128 + 128],
                            a16(PO[ps] + kc * 128, 128), idb_t[:, :]),
                           rd=[(pb[ps], kc), (consts, C_ID)],
                           wr=[(ptb[k], kc - lo)], inc=(kc == hi - 1))
                    n = (hi - lo) * 128
                    if k == 0:
                        E_("v", lambda v, k=k, n=n: v.tensor_copy(
                            a16(A_PT + k * 512, n), PT_T[k][:, 0:n]),
                           rd=[(ptb[k], i) for i in range(hi - lo)], wr=[(pts[k], 0)])
                    else:
                        E_("s", lambda s, k=k, n=n: s.copy(
                            a16(A_PT + k * 512, n), PT_T[k][:, 0:n]),
                           rd=[(ptb[k], i) for i in range(hi - lo)], wr=[(pts[k], 0)])

                def pv_batch(j):
                    lo, hi = batches[j]
                    k = j % 2
                    for kc in range(lo, hi):
                        E_("t", lambda t, kc=kc, k=k, hp=hp, h=h, qi=qi, otps=otps,
                           lo=lo: t.matmul(
                               otps[64 * hp:64 * hp + 64, 0:128],
                               a16(A_V + kc * 1024 + h * 64, 64),
                               a16(A_PT + k * 512 + (kc - lo) * 128, 128),
                               start=(kc == 0), stop=(kc == qi),
                               tile_position=(0, 64 * hp)),
                           rd=[(pts[k], 0), (vbufs[kc], 0)],
                           wr=[(otpb2, 0)] if kc == qi else [], inc=(kc == qi))

                for j in range(len(batches)):
                    tp_batch(j)
                    if j > 0:
                        pv_batch(j - 1)
                pv_batch(len(batches) - 1)
                if hp == 1:
                    E_("v", lambda v, qi=qi, otps=otps: v.tensor_copy(
                        a16(A_OT + qi * 128, 128), otps[:, 0:128]),
                       rd=[(otpb2, 0)], wr=[(otp, qi)])

            pend = []
            idx = 0
            for qi in range(_DBG.get('maxqi', TT)):
                for h in (2 * pair, 2 * pair + 1):
                    soff = (idx % 2) * 1024 if qi < 8 else 0
                    ps = idx % 3
                    emit_S(qi, h, soff, ps)
                    pend.append((qi, h, ps))
                    idx += 1
                    want = 2 if qi < 8 else 1
                    while len(pend) > want:
                        emit_TPV(*pend.pop(0))
            while pend:
                emit_TPV(*pend.pop(0))

            WROT = [(PS_B[:, :], psB, 0), (PS_C[:, :], psC, 0),
                    (PS_A[:, 0:512], psA, 0), (PS_A[:, 512:1024], psA, 1)]
            for dc in range(KC):
                for tg in range(NTG4):
                    pst, ppp, ppi = WROT[(dc * NTG4 + tg) % 4]
                    E_("t", lambda t, sl=sl, dc=dc, tg=tg, pst=pst: t.matmul(
                        pst, a16(A_WOW + sl * 1024 + dc * 128, 128),
                        a16(A_OT + tg * 512, 512), start=True, stop=True),
                       rd=[(wos[sl], 0)] + [(otp, q) for q in range(tg * 4, tg * 4 + 4)],
                       wr=[(ppp, ppi)])
                    if pair == 0:
                        E_("v", lambda v, dc=dc, tg=tg, pst=pst, bo_t=bo_t:
                           v.scalar_tensor_tensor(
                               res_t[dc][:, tg * 512:(tg + 1) * 512], pst,
                               bo_t[:, dc:dc + 1],
                               res_t[dc][:, tg * 512:(tg + 1) * 512], OP.add, OP.add),
                           rd=[(ppp, ppi), (biasb, B_IDX['l0bo'] if L == 0 else B_IDX['l1bo']), (res[dc], tg)], wr=[(res[dc], tg)])
                    else:
                        E_("v", lambda v, dc=dc, tg=tg, pst=pst: v.tensor_add(
                            res_t[dc][:, tg * 512:(tg + 1) * 512],
                            res_t[dc][:, tg * 512:(tg + 1) * 512], pst),
                           rd=[(ppp, ppi), (res[dc], tg)], wr=[(res[dc], tg)])

    def phase_moe(cwtb):
        hp = [ar_16.claim(A_V + f_ * 1024, A_V + (f_ + 1) * 1024, f"hp{f_}", 2)
              for f_ in range(FC)]
        w13 = [ar_16.claim(A_W13 + i * 2048, A_W13 + (i + 1) * 2048, f"w13_{i}")
               for i in range(2)]
        w2h = [ar_16.claim(A_W2R + i * 1024, A_W2R + (i + 1) * 1024, f"w2h{i}")
               for i in range(4)]
        cwbb = ar_32.claim(Z_CWB, Z_CWB + 1024, "cwb", 2)
        W13O = [A_W13, A_W13 + 2048]
        W2O = [A_W2R, A_W2R + 2048]

        def ld_w13(e, fc, sl):
            ch = "w13" + str(sl)
            E_("q", lambda q, e=e, fc=fc, sl=sl: q.dma_start(
                a16(W13O[sl], 1024).rearrange("p (k n) -> p k n", n=128),
                bass.AP(dr["w1r"], (e * 32 + fc) * 8 * 128 * 128,
                        [[128, 128], [128 * 128, 8], [1, 128]])),
                wr=[(w13[sl], 0)], dma=ch)
            tok = E_("q", lambda q, e=e, fc=fc, sl=sl: q.dma_start(
                a16(W13O[sl] + 1024, 1024).rearrange("p (k n) -> p k n", n=128),
                bass.AP(dr["w3r"], (e * 32 + fc) * 8 * 128 * 128,
                        [[128, 128], [128 * 128, 8], [1, 128]])),
                wr=[(w13[sl], 0)], dma=ch)
            em.bulk_fixup([w13[sl]], tok)

        def ld_w2q(e, dc, qq):
            sl = qq % 4
            E_("q", lambda q, e=e, dc=dc, qq=qq, sl=sl: q.dma_start(
                a16(A_W2R + sl * 1024, 1024).rearrange("p (k n) -> p k n", n=128),
                bass.AP(dr["w2r"], ((e * 8 + dc) * 32 + qq * 8) * 128 * 128,
                        [[128, 128], [128 * 128, 8], [1, 128]])),
                wr=[(w2h[sl], 0)], dma="w2" + str(sl))

        ld_w13(0, 0, 0)
        step = 0
        for e in range(E):
            for tgi in range(NTG):
                t0 = tgi * 1024
                for half in range(2):
                    pst, ppp = PSBC[half]
                    E_("t", lambda t, e=e, t0=t0, half=half, pst=pst: t.matmul(
                        pst[:, :], oh_t[:, e * 128:(e + 1) * 128],
                        a16(A_CWT + t0 + half * 512, 512, 0, 8),
                        start=True, stop=True),
                       rd=[(consts, C_OH)] + [(cwtb, (t0 + half * 512) // 128 + i)
                                           for i in range(4)],
                       wr=[(ppp, 0)])
                    E_("v", lambda v, half=half, pst=pst: v.tensor_copy(
                        a32(Z_CWB + half * 512, 512), pst[:, :]),
                       rd=[(ppp, 0)], wr=[(cwbb, half)])
                ROT = [(PS_B[:, :], psB, 0), (PS_C[:, :], psC, 0),
                       (PS_A[:, 0:512], psA, 0), (PS_A[:, 512:1024], psA, 1)]
                rot = 0
                for fc in range(FC):
                    sl = step % 2
                    step += 1
                    nxt = step
                    if nxt < E * NTG * FC:
                        ne, rem = divmod(nxt, NTG * FC)
                        _ntg, nfc = divmod(rem, FC)
                        ld_w13(ne, nfc, 1 - sl)
                    if fc >= FC - 4:
                        ld_w2q(e, 0, fc - (FC - 4))
                    for stg in range(2):
                        pstg, pppg, ppg = ROT[rot % 4]
                        pstu, pppu, ppu = ROT[(rot + 1) % 4]
                        rot += 2
                        memg, memu = [], []
                        for c in range(KC):
                            def fg(t, sl=sl, c=c, t0=t0, stg=stg, pstg=pstg):
                                return t.matmul(
                                    pstg, a16(W13O[sl] + c * 128, 128),
                                    xn_t[c][:, t0 + stg * 512:t0 + stg * 512 + 512],
                                    start=(c == 0), stop=(c == KC - 1))
                            memg.append((fg, [(w13[sl], 0),
                                              (xn[c], (t0 + stg * 512) // 512)]))
                            def fu(t, sl=sl, c=c, t0=t0, stg=stg, pstu=pstu):
                                return t.matmul(
                                    pstu, a16(W13O[sl] + 1024 + c * 128, 128),
                                    xn_t[c][:, t0 + stg * 512:t0 + stg * 512 + 512],
                                    start=(c == 0), stop=(c == KC - 1))
                            memu.append((fu, [(w13[sl], 0),
                                              (xn[c], (t0 + stg * 512) // 512)]))
                        G_("t", memg, wr=[(pppg, ppg)])
                        E_("s", lambda s, fc=fc, stg=stg, pstg=pstg: s.activation(
                            a16(A_V + fc * 1024 + stg * 512, 512), pstg, AF.Silu),
                           rd=[(pppg, ppg)], wr=[(hp[fc], stg)])
                        G_("t", memu, wr=[(pppu, ppu)])
                        E_("v", lambda v, fc=fc, stg=stg, pstu=pstu: v.tensor_mul(
                            a16(A_V + fc * 1024 + stg * 512, 512),
                            a16(A_V + fc * 1024 + stg * 512, 512), pstu),
                           rd=[(pppu, ppu), (hp[fc], stg)], wr=[(hp[fc], stg)])
                for dc in range(KC):
                    for stg in range(2):
                        pidx = (dc * 2 + stg) % 4
                        for qq in range(4):
                            sl2 = qq % 4
                            mem = []
                            for j in range(8):
                                fc = qq * 8 + j
                                def fd(t, sl2=sl2, j=j, fc=fc, stg=stg, pidx=pidx,
                                       first=(qq == 0 and j == 0),
                                       last=(qq == 3 and j == 7)):
                                    return t.matmul(
                                        PS_A[:, pidx * 512:pidx * 512 + 512],
                                        a16(A_W2R + sl2 * 1024 + j * 128, 128),
                                        a16(A_V + fc * 1024 + stg * 512, 512),
                                        start=first, stop=last)
                                mem.append((fd, [(w2h[sl2], 0), (hp[fc], stg)]))
                            G_("t", mem, wr=[(psA, pidx)])
                            if stg == 1 and dc + 1 < KC:
                                ld_w2q(e, dc + 1, qq)
                        g4 = (t0 + stg * 512) // 512
                        tsl = (dc * 2 + stg) % 2
                        E_("v", lambda v, pidx=pidx, stg=stg, tsl=tsl: v.tensor_mul(
                            a32(Z_TMP + tsl * 512, 512),
                            PS_A[:, pidx * 512:pidx * 512 + 512],
                            a32(Z_CWB + stg * 512, 512)),
                           rd=[(psA, pidx), (cwbb, stg)], wr=[(tmpf, tsl)])
                        E_("v", lambda v, dc=dc, g4=g4, tsl=tsl: v.tensor_add(
                            res_t[dc][:, g4 * 512:(g4 + 1) * 512],
                            res_t[dc][:, g4 * 512:(g4 + 1) * 512],
                            a32(Z_TMP + tsl * 512, 512)),
                           rd=[(tmpf, tsl), (res[dc], g4)], wr=[(res[dc], g4)])

    def phase_alpha():
        slA = ar_32.claim(Z_ALA, Z_ALA + 1024, "slA")
        tcol = Buf("tcol", 1)
        for i in range(IC):
            E_("y", lambda y, i=i: y.dma_start(
                a32(Z_ALA, 1024), dr["w1n"][i * 128:(i + 1) * 128, :]),
                wr=[(slA, 0)], dma="alA")
            E_("v", lambda v, i=i: v.reduce_sum(
                al1_t[:, i:i + 1], a32(Z_ALA, 1024), axis=AX,
                apply_absolute_value=True), rd=[(slA, 0)], wr=[(al1b, 0)])
        E_("v", lambda v: v.tensor_scalar(al1_t[:, :], al1_t[:, :], 1.0 / D, None,
                                          OP.mult), rd=[(al1b, 0)], wr=[(al1b, 0)])
        slB = ar_32.claim(Z_ALB, Z_ALB + 2048, "slB")
        for i in range(KC):
            for hh in range(2):
                E_("y", lambda y, i=i, hh=hh: y.dma_start(
                    a32(Z_ALB, 2048), dr["w2n"][i * 128:(i + 1) * 128,
                                                hh * 2048:(hh + 1) * 2048]),
                    wr=[(slB, 0)], dma="alB")
                E_("v", lambda v, hh=hh: v.reduce_sum(
                    sc_t[:, S_A2 + hh:S_A2 + hh + 1], a32(Z_ALB, 2048), axis=AX,
                    apply_absolute_value=True), rd=[(slB, 0)], wr=[(tcol, 0)])
            E_("v", lambda v, i=i: v.reduce_sum(
                al2_t[:, i:i + 1], sc_t[:, S_A2:S_A2 + 2], axis=AX),
               rd=[(tcol, 0)], wr=[(al2b, 0)])
        E_("v", lambda v: v.tensor_scalar(al2_t[:, :], al2_t[:, :], 1.0 / F, None,
                                          OP.mult), rd=[(al2b, 0)], wr=[(al2b, 0)])
        if _DBG.get("dump_al"):
            E_("q", lambda q: q.dma_start(dbgT[:, 0:32], al1_t[:, :]),
               rd=[(al1b, 0)], dma="dbg")
            E_("q", lambda q: q.dma_start(dbgT[:, 32:40], al2_t[:, :]),
               rd=[(al2b, 0)], dma="dbg")

    def phase_ffn1():
        hT = [ar_16.claim(A_V + i * 1024, A_V + (i + 1) * 1024, f"hT{i}", 2)
              for i in range(IC)]
        w1t = [ar_16.claim(A_W13 + i * 1024, A_W13 + (i + 1) * 1024, f"w1t{i}")
               for i in range(2)]
        s1t = [ar_16.claim(A_W13 + 2048 + i * 1024, A_W13 + 2048 + (i + 1) * 1024,
                           f"s1t{i}") for i in range(2)]
        w2t = [ar_16.claim(A_W2R + i * 2048, A_W2R + (i + 1) * 2048, f"w2t{i}")
               for i in range(2)]
        W1O = [A_W13, A_W13 + 1024]
        S1O = [A_W13 + 2048, A_W13 + 3072]
        W2TO = [A_W2R, A_W2R + 2048]

        def ld_s1(i, sl):
            E_("q", lambda q, i=i, sl=sl: q.dma_start(
                a16(W1O[sl], 1024).rearrange("p (k n) -> p k n", n=128),
                bass.AP(dr["w1tr"], i * 8 * 128 * 128,
                        [[128, 128], [128 * 128, 8], [1, 128]])),
                wr=[(w1t[sl], 0)], dma="s1" + str(sl))
            E_("s", lambda s, sl=sl: s.activation(
                a16(S1O[sl], 1024), a16(W1O[sl], 1024), AF.Sign),
               rd=[(w1t[sl], 0)], wr=[(s1t[sl], 0)])

        def ld_s2h(dc, hh, sl):
            E_("q", lambda q, dc=dc, hh=hh, sl=sl: q.dma_start(
                a16(W2TO[sl], 2048).rearrange("p (k n) -> p k n", n=128),
                bass.AP(dr["w2tr"], (dc * 32 + hh * 16) * 128 * 128,
                        [[128, 128], [128 * 128, 16], [1, 128]])),
                wr=[(w2t[sl], 0)], dma="s2" + str(sl))
            E_("s", lambda s, sl=sl: s.activation(
                a16(W2TO[sl], 2048), a16(W2TO[sl], 2048), AF.Sign),
               rd=[(w2t[sl], 0)], wr=[(w2t[sl], 0)])

        for tgi in range(NTG):
            t0 = tgi * 1024
            if tgi == 0:
                ld_s1(0, 0)
            for i in range(IC):
                sl = i % 2
                if i + 1 < IC:
                    ld_s1(i + 1, 1 - sl)
                if i == IC - 6:
                    ld_s2h(0, 0, 0)
                elif i == IC - 3:
                    ld_s2h(0, 1, 1)
                for stg in range(2):
                    pst, ppp = PSBC[stg]
                    mem = []
                    for c in range(KC):
                        def fu(t, sl=sl, c=c, t0=t0, stg=stg, pst=pst):
                            return t.matmul(
                                pst[:, :], a16(S1O[sl] + c * 128, 128),
                                xn_t[c][:, t0 + stg * 512:t0 + stg * 512 + 512],
                                start=(c == 0), stop=(c == KC - 1))
                        mem.append((fu, [(s1t[sl], 0),
                                         (xn[c], (t0 + stg * 512) // 512)]))
                    G_("t", mem, wr=[(ppp, 0)])
                    E_("s", lambda s, i=i, stg=stg, pst=pst: s.activation(
                        a16(A_V + i * 1024 + stg * 512, 512), pst[:, :], AF.Gelu,
                        bias=b1c_t[:, i:i + 1], scale=al1_t[:, i:i + 1]),
                       rd=[(ppp, 0), (biasb, B_IDX['b1c']), (al1b, 0)], wr=[(hT[i], stg)])
            for dc in range(KC):
                for stg in range(2):
                    pidx = (dc * 2 + stg) % 4
                    for hh in range(2):
                        sl2 = hh
                        mem = []
                        for j in range(16):
                            i = hh * 16 + j
                            def fd(t, sl2=sl2, j=j, i=i, stg=stg, pidx=pidx,
                                   first=(hh == 0 and j == 0),
                                   last=(hh == 1 and j == 15)):
                                return t.matmul(
                                    PS_A[:, pidx * 512:pidx * 512 + 512],
                                    a16(W2TO[sl2] + j * 128, 128),
                                    a16(A_V + i * 1024 + stg * 512, 512),
                                    start=first, stop=last)
                            mem.append((fd, [(w2t[sl2], 0), (hT[i], stg)]))
                        G_("t", mem, wr=[(psA, pidx)])
                        if stg == 1:
                            if hh == 0 and dc + 1 < KC:
                                ld_s2h(dc + 1, 0, 0)
                            elif hh == 1 and dc + 1 < KC:
                                ld_s2h(dc + 1, 1, 1)
                            elif hh == 1 and dc == KC - 1 and tgi == 0:
                                ld_s1(0, 0)
                    g4 = (t0 + stg * 512) // 512
                    tsl = (dc * 2 + stg) % 2
                    E_("s", lambda s, pidx=pidx, dc=dc, tsl=tsl: s.mul(
                        a32(Z_TMP + tsl * 512, 512),
                        PS_A[:, pidx * 512:pidx * 512 + 512], al2_t[:, dc:dc + 1]),
                       rd=[(psA, pidx), (al2b, 0)], wr=[(tmpf, tsl)])
                    E_("v", lambda v, dc=dc, g4=g4, tsl=tsl: v.scalar_tensor_tensor(
                        res_t[dc][:, g4 * 512:(g4 + 1) * 512],
                        a32(Z_TMP + tsl * 512, 512), bias_t["b2c"][:, dc:dc + 1],
                        res_t[dc][:, g4 * 512:(g4 + 1) * 512], OP.add, OP.add),
                       rd=[(tmpf, tsl), (biasb, B_IDX['b2c']), (res[dc], g4)],
                       wr=[(res[dc], g4)])

    phase_ln(0)
    cwtb = None
    if upto == "a1solo":
        phase_attn(1)
    elif upto != "ln0":
        phase_attn(0)
        cwtb = phase_gate()
    if upto in ("moe0", "ln1", "attn1", "full", "xnl1"):
        phase_moe(cwtb)
    if upto in ("ln1", "attn1", "full", "xnl1"):
        phase_ln(1)
    if upto == "full":
        phase_alpha()
    if upto in ("attn1", "full"):
        phase_attn(1)
    if upto == "full":
        phase_ffn1()

    if upto in ("ln0", "xnl1"):
        for c in range(KC):
            E_("q", lambda q, c=c: q.dma_start(
                outT[c * 128:(c + 1) * 128, :], xn_t[c][:, :]),
               rd=[(xn[c], p) for p in range(4)], dma="dbg")
        em.final_wait("q")
    else:
        for c in range(KC):
            E_("y", lambda y, c=c: y.dma_start(
                outT[c * 128:(c + 1) * 128, :], res_t[c][:, :]),
               rd=[(res[c], p) for p in range(4)], dma="out")
    em.final_wait("y")

    with ExitStack() as st:
        for ch in list(em.ch.values()) + list(em.dch.values()):
            ch.h = st.enter_context(nc.semaphore("ch_" + ch.name))
        with nc.Block() as block:
            @block.tensor
            def _(eng):
                _replay(em, "t", eng)

            @block.vector
            def _(eng):
                _replay(em, "v", eng)

            @block.scalar
            def _(eng):
                _replay(em, "s", eng)

            @block.gpsimd
            def _(eng):
                _replay(em, "q", eng)

            @block.sync
            def _(eng):
                _replay(em, "y", eng)
    return nc, em


_NC_CACHE = {}


def _get_nc(upto="full"):
    if upto not in _NC_CACHE:
        _NC_CACHE[upto] = _build(upto)
    return _NC_CACHE[upto]


def kernel(**inputs):
    x = np.asarray(inputs["L0_attn_x"], dtype=np.float32)
    host = _host_prep(inputs)
    nc, _em = _get_nc("full")
    in_maps = []
    for b in range(NCORE):
        m = {"xT": np.ascontiguousarray(x[b].T)}
        m.update(host)
        in_maps.append(m)
    res = run_bass_kernel_spmd(nc, in_maps, core_ids=list(range(NCORE)))
    out = np.stack([np.ascontiguousarray(r["outT"].T) for r in res.results], axis=0)
    return out
